# version 07
# baseline (speedup 1.0000x reference)
import contextlib
import numpy as np
import ml_dtypes
import concourse.bass as bass
import concourse.mybir as mybir
from concourse.bass_utils import run_bass_kernel_spmd

F32 = mybir.dt.float32
BF16 = mybir.dt.bfloat16
U8 = mybir.dt.uint8
U16 = mybir.dt.uint16
I32 = mybir.dt.int32
F16 = mybir.dt.float16
AF = mybir.ActivationFunctionType
ALU = mybir.AluOpType
AX = mybir.AxisListType
BF = ml_dtypes.bfloat16

C = 2048
T = 2048
NCT = C // 128
NTB = T // 512
NCORES = 8



class Ev:
    __slots__ = ("sem", "val", "eng")

    def __init__(self, sem, val, eng):
        self.sem, self.val, self.eng = sem, val, eng


class Core:

    ENGINES = ("sync", "scalar", "vector", "tensor", "gpsimd")

    def __init__(self, nc):
        self.nc = nc
        self.es = contextlib.ExitStack()
        self.ops = []
        self.dram = {}
        self.esem, self.ecnt = {}, {}
        for e in self.ENGINES:
            self.esem[e] = self.es.enter_context(nc.semaphore(f"E_{e}"))
            self.ecnt[e] = 0
        self.waited = {e: {} for e in self.ENGINES}
        self.streams = {}

    def dram_in(self, name, shape, dtype):
        t = self.nc.dram_tensor(name, list(shape), dtype, kind="ExternalInput").ap()
        self.dram[name] = t
        return t

    def dram_out(self, name, shape, dtype):
        t = self.nc.dram_tensor(name, list(shape), dtype, kind="ExternalOutput").ap()
        self.dram[name] = t
        return t

    def dram_tmp(self, name, shape, dtype):
        t = self.nc.dram_tensor(name, list(shape), dtype, kind="Internal").ap()
        self.dram[name] = t
        return t

    def sb(self, name, shape, dtype):
        return self.es.enter_context(self.nc.sbuf_tensor(name, list(shape), dtype))

    def psb(self, name):
        return self.es.enter_context(self.nc.psum_tensor(name, [128, 512], F32))

    def _emit_waits(self, eng, after):
        need = {}
        for ev in after:
            if ev is None:
                continue
            key = id(ev.sem)
            if key not in need or need[key][1] < ev.val:
                need[key] = (ev.sem, ev.val)
        out = []
        wm = self.waited[eng]
        for key, (sem, val) in need.items():
            if wm.get(key, -1) >= val:
                continue
            wm[key] = val
            out.append((sem, val))
        return out

    def run(self, eng, emit, after=(), sig=True):
        waits = self._emit_waits(eng, after)
        ev = None
        if sig:
            self.ecnt[eng] += 1
            ev = Ev(self.esem[eng], self.ecnt[eng], eng)

        def fn(e, waits=waits, emit=emit, ev=ev):
            for (sem, val) in waits:
                e.wait_ge(sem, val)
            inst = emit(e)
            if ev is not None:
                assert inst is not None
                inst.then_inc(ev.sem, 1)
        self.ops.append((eng, fn))
        return ev

    def dma(self, q, out, in_, stream=None, after=(), **kw):
        assert stream is not None
        st = self.streams.get(stream)
        if st is None:
            st = [self.es.enter_context(self.nc.semaphore(f"S_{stream}")), 0, None]
            self.streams[stream] = st
        waits = self._emit_waits(q, list(after) + [st[2]])
        st[1] += 16
        ev = Ev(st[0], st[1], None)
        st[2] = ev

        def fn(e, waits=waits, out=out, in_=in_, ev=ev, kw=kw):
            for (sem, val) in waits:
                e.wait_ge(sem, val)
            e.dma_start(out=out, in_=in_, **kw).then_inc(ev.sem, 16)
        self.ops.append((q, fn))
        return ev

    def stream_tails(self):
        return [st[2] for st in self.streams.values() if st[2] is not None]

    def wait(self, eng, after):
        waits = self._emit_waits(eng, after)
        if not waits:
            return

        def fn(e, waits=waits):
            for (sem, val) in waits:
                e.wait_ge(sem, val)
        self.ops.append((eng, fn))

    def build_block(self):
        block = self.es.enter_context(self.nc.Block())
        ops = self.ops

        def mk(name):
            def closure(e):
                for eng, fn in ops:
                    if eng == name:
                        fn(e)
            return closure
        block.sync(mk("sync"))
        block.scalar(mk("scalar"))
        block.vector(mk("vector"))
        block.tensor(mk("tensor"))
        block.gpsimd(mk("gpsimd"))

    def finish(self):
        self.es.close()


class Carver:

    def __init__(self, core, nbytes=140 * 1024, name="work"):
        self.buf = core.sb(name, (128, nbytes), U8)
        self.nbytes = nbytes
        self.off = 0

    def reset(self, off=0):
        self.off = off
        return off

    def take(self, cols, dtype):
        size = {F32: 4, BF16: 2, U16: 2, F16: 2, U8: 1, I32: 4}[dtype]
        self.off = (self.off + 511) // 512 * 512
        o = self.off
        self.off += cols * size
        assert self.off <= self.nbytes, f"work arena overflow: {self.off}"
        if dtype == U8:
            return self.buf[:, o:o + cols]
        return self.buf[:, o:o + cols * size].bitcast(dtype)

    def take2(self, cols, dt1, dt2):
        size = {F32: 4, BF16: 2, U16: 2, F16: 2, I32: 4}[dt1]
        self.off = (self.off + 511) // 512 * 512
        o = self.off
        self.off += cols * size
        assert self.off <= self.nbytes, f"work arena overflow: {self.off}"
        raw = self.buf[:, o:o + cols * size]
        return raw.bitcast(dt1), raw.bitcast(dt2)



def _bf(x):
    return np.ascontiguousarray(np.asarray(x, BF))


def _f32(x):
    return np.ascontiguousarray(np.asarray(x, np.float32))


def pack_lhsT(W, dtype=np.float32):
    F, K = W.shape
    assert F % 128 == 0 and K % 128 == 0
    Mct, Kct = F // 128, K // 128
    a = np.asarray(W, dtype).reshape(Mct, 128, Kct, 128)
    a = np.ascontiguousarray(a.transpose(0, 3, 2, 1))
    return a.reshape(Mct, 128, Kct * 128)


def pack_bytes_fp4(packed_i32, group_scales_f16):
    Fo, Kp = packed_i32.shape
    K = Kp * 8
    by = np.ascontiguousarray(packed_i32).view(np.uint8).reshape(Fo, K // 2)
    dup = np.repeat(by.T, 2, axis=0)
    Kct, Mct = K // 128, Fo // 128
    a = dup.reshape(Kct, 128, Mct, 128)
    bslab = np.ascontiguousarray(a.transpose(2, 1, 0, 3)).reshape(Mct, 128, Kct * 128)
    sc = np.asarray(group_scales_f16, np.float16)
    s = sc.T.reshape(Mct, 128, Kct)
    s = np.broadcast_to(s.transpose(0, 2, 1)[:, None, :, :], (Mct, 128, Kct, 128))
    sslab = np.ascontiguousarray(s).reshape(Mct, 128, Kct * 128)
    return bslab, sslab


def host_constants():
    c = {}
    c["ones1b"] = _bf(np.ones((128, 1)))
    c["ones1r"] = _bf(np.ones((1, 128)))
    c["identb"] = _bf(np.eye(128))
    c["identf"] = _f32(np.eye(128))
    c["shift01"] = np.ascontiguousarray(
        np.tile(np.array([0, 4], np.uint16), 64).reshape(128, 1))
    c["six16"] = np.full((128, 1), 6, np.uint16)
    return c


def load_constants(core, cst_host, names):
    out, evs = {}, []
    for n in names:
        h = cst_host[n]
        dt = {np.dtype(np.uint16): U16, np.dtype(np.float32): F32,
              np.dtype(ml_dtypes.bfloat16): BF16, np.dtype(np.float16): F16}[h.dtype]
        d = core.dram_in(f"cst_{n}", h.shape, dt)
        s = core.sb(f"cs_{n}", h.shape, dt)
        evs.append(core.dma("sync", s[:], d[:], stream="cst"))
        out[n] = s[:]
    return out, evs



def emit_ln(core, tag, x_dram, xn_arena, cst, psum, cv, after=()):
    xt = cv.take(2 * T, F32)
    xb = cv.take(2 * T, BF16)
    sq = cv.take(2 * T, BF16)
    stats = cv.take(4 * T, F32)
    sums = stats[0:1, 0:2 * T]
    rstd = stats[0:1, 2 * T:4 * T]
    rsb = cv.take(2 * T, BF16)[0:1, :]
    bcast = cv.take(2 * T, F32)
    ones1b, ones1r = cst["ones1b"], cst["ones1r"]

    lastmm, rdev, mmev = {}, {}, {}
    for ct in range(NCT):
        b = ct % 2
        xv = xt[:, b * T:(b + 1) * T]
        xbv = xb[:, b * T:(b + 1) * T]
        sqv = sq[:, b * T:(b + 1) * T]
        dep = [lastmm.get(ct - 2)] + (list(after) if ct < 2 else [])
        lev = core.dma("sync", xv, x_dram[128 * ct:128 * (ct + 1), :], stream=f"ln{tag}x{b}", after=dep)
        cev = core.run("vector", lambda e, a=xbv, c=xv: e.tensor_copy(a, c), after=[lev])
        qev = core.run("scalar", lambda e, a=sqv, c=xv: e.activation(a, c, AF.Square), after=[lev])
        rdev[ct] = [cev, qev]
        for tb in range(NTB):
            f, l = ct == 0, ct == NCT - 1
            sl = slice(512 * tb, 512 * (tb + 1))
            mmev[(0, tb)] = core.run("tensor",
                                     lambda e, p=psum[tb], r=xbv[:, sl], f=f, l=l:
                                     e.matmul(p[0:1, 0:512], ones1b, r, start=f, stop=l),
                                     after=[cev])
            mmev[(1, tb)] = core.run("tensor",
                                     lambda e, p=psum[4 + tb], r=sqv[:, sl], f=f, l=l:
                                     e.matmul(p[0:1, 0:512], ones1b, r, start=f, stop=l),
                                     after=[qev])
        lastmm[ct] = mmev[(1, NTB - 1)]
    sev = []
    for tb in range(NTB):
        sl = slice(512 * tb, 512 * (tb + 1))
        s2 = slice(T + 512 * tb, T + 512 * (tb + 1))
        sev.append(core.run("vector",
                            lambda e, a=sums[0:1, sl], p=psum[tb]:
                            e.tensor_scalar(a, p[0:1, 0:512], 1.0 / C, None, ALU.mult),
                            after=[mmev[(0, tb)]]))
        sev.append(core.run("vector",
                            lambda e, a=sums[0:1, s2], p=psum[4 + tb]:
                            e.tensor_scalar(a, p[0:1, 0:512], 1.0 / C, None, ALU.mult),
                            after=[mmev[(1, tb)]]))
    e1 = core.run("scalar", lambda e: e.activation(rstd[0:1, 0:T], sums[0:1, 0:T], AF.Square),
                  after=sev)
    e2 = core.run("vector", lambda e: e.tensor_sub(rstd[0:1, 0:T], sums[0:1, T:2 * T], rstd[0:1, 0:T]),
                  after=[e1] + sev)
    e2b = core.run("vector", lambda e: e.tensor_scalar(rstd[0:1, 0:T], rstd[0:1, 0:T], 1e-5, None, ALU.add),
                   after=[e2])
    e3 = core.run("scalar", lambda e: e.activation(rstd[0:1, 0:T], rstd[0:1, 0:T], AF.Sqrt),
                  after=[e2b])
    e4 = core.run("vector", lambda e: e.reciprocal(rstd[0:1, 0:T], rstd[0:1, 0:T]), after=[e3])
    e5 = core.run("vector", lambda e: e.tensor_mul(rstd[0:1, T:2 * T], sums[0:1, 0:T], rstd[0:1, 0:T]),
                  after=[e4])
    e6 = core.run("vector", lambda e: e.tensor_copy(rsb[0:1, :], rstd[0:1, :]), after=[e5])
    bev, pprev = [], {}
    for blk in range(2 * NTB):
        sl = slice(512 * blk, 512 * (blk + 1))
        bank = psum[6 + (blk % 2)]
        pe = core.run("tensor",
                      lambda e, p=bank, r=rsb[0:1, sl]:
                      e.matmul(p[:, 0:512], ones1r, r, start=True, stop=True),
                      after=[e6, pprev.get(blk - 2)])
        ev = core.run("vector",
                      lambda e, a=bcast[:, sl], p=bank: e.tensor_copy(a, p[:, 0:512]),
                      after=[pe])
        pprev[blk] = ev
        bev.append(ev)
    done = []
    for ct in range(NCT):
        b = ct % 2
        xv = xt[:, b * T:(b + 1) * T]
        dep = [done[ct - 2]] if ct >= 2 else rdev[NCT - 2 + ct]
        lev = core.dma("sync", xv, x_dram[128 * ct:128 * (ct + 1), :], stream=f"ln{tag}x{b}", after=dep)
        xnv = xn_arena[:, T * ct:T * (ct + 1)]
        m1 = core.run("vector", lambda e, o=xnv, x=xv: e.tensor_mul(o, x, bcast[:, 0:T]),
                      after=[lev] + bev)
        done.append(core.run("vector", lambda e, o=xnv: e.tensor_sub(o, o, bcast[:, T:2 * T]),
                             after=[m1]))
    return done



def emit_dequant2(core, bbuf, scbuf, deq, nib, sgn, cst, ev_in):
    W = deq.shape[1]
    e1 = core.run("vector", lambda e: e.tensor_tensor(nib, bbuf, cst["shift01"].broadcast_to([128, W]),
                                                      ALU.logical_shift_right),
                  after=ev_in)
    e2 = core.run("vector", lambda e: e.tensor_scalar(sgn, nib, 8, 12,
                                                      ALU.bitwise_and, ALU.logical_shift_left),
                  after=[e1])
    e3 = core.run("vector", lambda e: e.tensor_scalar(nib, nib, 7, None, ALU.bitwise_and),
                  after=[e2])
    e4 = core.run("vector", lambda e: e.tensor_scalar(bbuf, nib, 252, None, ALU.add),
                  after=[e3])
    e5 = core.run("vector", lambda e: e.scalar_tensor_tensor(bbuf, nib, 252.0, bbuf,
                                                             ALU.mult, ALU.min),
                  after=[e4])
    e6 = core.run("vector", lambda e: e.scalar_tensor_tensor(nib, bbuf, 64.0, sgn,
                                                             ALU.mult, ALU.add),
                  after=[e5])
    e7 = core.run("vector", lambda e: e.tensor_mul(deq, deq, scbuf),
                  after=[e6])
    return e6, e7



def emit_ffn_fp4(core, tag, xn_arena, cst, psum, dram, cv, after=(), resid_q="gpsimd"):
    IT = 44
    ones1b, ones1r = cst["ones1b"], cst["ones1r"]
    base = cv.off

    wb = cv.take(2 * 2048, U16)
    wsc = cv.take(2 * 2048, F16)
    wdq, wnib = cv.take2(2 * 2048, BF16, U16)
    wsg = cv.take(2 * 2048, U16)
    hsil = cv.take(T, BF16)
    hbuf = cv.take(T, BF16)

    def rhs(ct, tb):
        return xn_arena[:, T * ct + 512 * tb: T * ct + 512 * tb + 512]

    hev = {}
    mm_last = {}
    silu_ev = {}
    dq6, dq7 = {}, {}

    def emit_mult(i):
        for tb in range(NTB):
            silu_ev[(1, i, tb)] = core.run(
                "vector", lambda e, p=psum[4 + tb], o=hbuf[:, 512 * tb:512 * (tb + 1)],
                s=hsil[:, 512 * tb:512 * (tb + 1)]:
                e.tensor_mul(o, s, p[:, 0:512]),
                after=[mm_last[(2 * i + 1, tb)], silu_ev[(0, i, tb)],
                       hev.get(i - 1)])
        hev[i] = core.dma("sync", dram["h_scr"][:, 2048 * i:2048 * (i + 1)], hbuf[:],
                          stream="hst", after=[silu_ev[(1, i, tb)] for tb in range(NTB)])

    def emit_mms(idx, b, d7, bank0):
        for tb in range(NTB):
            bank = psum[bank0 + tb]
            mmdep = [d7, silu_ev.get((bank0 // 4, (idx - 2) // 2, tb))]
            for ct in range(NCT):
                ev = core.run("tensor",
                              lambda e, p=bank, l=wdq[:, b * 2048 + ct * 128:b * 2048 + (ct + 1) * 128],
                              r=rhs(ct, tb), f=(ct == 0), s=(ct == NCT - 1):
                              e.matmul(p[:, 0:512], l, r, start=f, stop=s),
                              after=mmdep, sig=(ct == NCT - 1))
                mmdep = []
            mm_last[(idx, tb)] = ev
        mm_last[idx] = ev

    def emit_dq(idx, mt, b):
        sl = slice(b * 2048, (b + 1) * 2048)
        dep0 = list(after) if idx < 2 else []
        eb = core.dma("gpsimd", wb[:, sl], dram["gu_bytes"][mt], stream=f"gub{b}",
                      after=[dq6.get(idx - 2)] + dep0)
        es = core.dma("sync", wsc[:, sl], dram["gu_sc"][mt], stream=f"gus{b}",
                      after=[dq7.get(idx - 2)] + dep0)
        d6, d7 = emit_dequant2(core, wb[:, sl], wsc[:, sl], wdq[:, sl],
                               wnib[:, sl], wsg[:, sl], cst,
                               [eb, es, mm_last.get(idx - 2)])
        dq6[idx], dq7[idx] = d6, d7
        return d7

    for i in range(IT):
        d7 = emit_dq(2 * i, i, 0)
        emit_mms(2 * i, 0, d7, 0)
        if i:
            emit_mult(i - 1)
        for tb in range(NTB):
            silu_ev[(0, i, tb)] = core.run(
                "scalar", lambda e, p=psum[tb], o=hsil[:, 512 * tb:512 * (tb + 1)]:
                e.activation(o, p[:, 0:512], AF.Silu),
                after=[mm_last[(2 * i, tb)], silu_ev.get((1, i - 1, tb))])
        d7 = emit_dq(2 * i + 1, IT + i, 1)
        emit_mms(2 * i + 1, 1, d7, 4)
    emit_mult(IT - 1)
    hev = [hev[i] for i in range(IT)]

    cv.reset(base)
    dnb = cv.take(2 * 5632, U16)
    dnsc = cv.take(2 * 5632, F16)
    dndq, dnnib = cv.take2(2 * 5632, BF16, U16)
    dnsg = cv.take(5632, U16)
    y2 = cv.take(16 * 512, F32)
    y2q = cv.take(512, BF16)
    rmsrow = cv.take(512, F32)[0:1, :]
    rmsb = cv.take(512, BF16)[0:1, :]
    rbc = cv.take(512, F32)
    hslab = xn_arena[:, 0:IT * 512]
    LAST = 2 * IT - 1
    ph1_done = [dq7[LAST], mm_last[LAST], hev[IT - 1],
                silu_ev[(0, IT - 1, NTB - 1)], silu_ev[(1, IT - 1, NTB - 1)]]

    out_ev = []
    prev_r6 = [None]
    prev_r5 = [None]
    dmm = {}
    de6, de7 = {}, {}
    e1ev = {}
    sqev = {}
    hld = {}

    def dn_epi(s):
        tb, ot = divmod(s, 16)
        e1 = core.run("vector", lambda e, p=psum[ot % 2], o=y2[:, 512 * ot:512 * (ot + 1)]:
                      e.tensor_copy(o, p[:, 0:512]),
                      after=[dmm[s], out_ev[-16 + ot] if tb else None])
        e1ev[s] = e1
        e2 = core.run("scalar", lambda e, o=y2[:, 512 * ot:512 * (ot + 1)]:
                      e.activation(y2q[:], o, AF.Square),
                      after=[e1, sqev.get(s - 1)])
        sqev[s] = core.run("tensor", lambda e, f=(ot == 0), l=(ot == 15):
                           e.matmul(psum[7][0:1, 0:512], ones1b, y2q[:], start=f, stop=l),
                           after=[e2, prev_r6[0] if ot == 0 else None,
                                  silu_ev[(1, IT - 1, NTB - 1)] if s == 0 else None])

    def emit_rms(tbx):
        s0x = 16 * tbx
        r1 = core.run("vector", lambda e: e.tensor_scalar(rmsrow, psum[7][0:1, 0:512],
                                                          1.0 / 2048, 1e-6, ALU.mult, ALU.add),
                      after=[sqev[s0x + 15]])
        r2 = core.run("scalar", lambda e: e.activation(rmsrow, rmsrow, AF.Sqrt), after=[r1])
        r3 = core.run("vector", lambda e: e.reciprocal(rmsrow, rmsrow), after=[r2])
        r4 = core.run("vector", lambda e: e.tensor_copy(rmsb, rmsrow), after=[r3, prev_r5[0]])
        r5 = core.run("tensor", lambda e: e.matmul(psum[7][:, 0:512], ones1r, rmsb,
                                                   start=True, stop=True), after=[r4, r1])
        r6 = core.run("vector", lambda e: e.tensor_copy(rbc[:], psum[7][:, 0:512]), after=[r5])
        prev_r6[0] = r6
        prev_r5[0] = r5
        for ot in range(16):
            sc = core.run("vector", lambda e, o=y2[:, 512 * ot:512 * (ot + 1)]:
                          e.tensor_mul(o, o, rbc[:]), after=[r6, e1ev[s0x + ot]])
            out_ev.append(core.dma(resid_q,
                                   dram["x_resid"][128 * ot:128 * (ot + 1), 512 * tbx:512 * (tbx + 1)],
                                   y2[:, 512 * ot:512 * (ot + 1)], stream=f"res{ot % 4}",
                                   after=[sc], accum_op=ALU.add))

    for s in range(16 * NTB):
        tb, ot = divmod(s, 16)
        b = ot % 2
        if ot == 0:
            hsrc = dram["h_scr"].rearrange("p (i t) -> p i t", t=2048)[:, :, 512 * tb:512 * (tb + 1)]
            hld[tb] = core.dma("sync", hslab.rearrange("p (i t) -> p i t", t=512), hsrc,
                               stream="hsl",
                               after=(ph1_done if tb == 0 else []) + [dmm.get(s - 1)] + hev[-1:])
        sl = slice(b * 5632, (b + 1) * 5632)
        eb = core.dma("gpsimd", dnb[:, sl], dram["dn_bytes"][ot], stream=f"dnb{b}",
                      after=[de6.get(s - 2)] + (ph1_done if s < 2 else []))
        es = core.dma("sync", dnsc[:, sl], dram["dn_sc"][ot], stream=f"dns{b}",
                      after=[de7.get(s - 2)] + (ph1_done if s < 2 else []))
        d6, d7 = emit_dequant2(core, dnb[:, sl], dnsc[:, sl], dndq[:, sl],
                               dnnib[:, sl], dnsg[:], cst,
                               [eb, es, dmm.get(s - 2)])
        de6[s], de7[s] = d6, d7
        bank = psum[b]
        dep = [d7, hld[tb], e1ev.get(s - 2)]
        if s < 2:
            dep += [silu_ev[(0, IT - 1, b)]]
        for it in range(IT):
            ev = core.run("tensor",
                          lambda e, p=bank, l=dndq[:, b * 5632 + it * 128:b * 5632 + (it + 1) * 128],
                          r=hslab[:, it * 512:(it + 1) * 512], f=(it == 0), st=(it == IT - 1):
                          e.matmul(p[:, 0:512], l, r, start=f, stop=st),
                          after=dep, sig=(it == IT - 1))
            dep = []
        dmm[s] = ev
        if s:
            dn_epi(s - 1)
        if ot == 0 and tb:
            emit_rms(tb - 1)
    dn_epi(16 * NTB - 1)
    emit_rms(NTB - 1)
    return out_ev



def emit_gemm(core, tag, wt_dram, n_mt, n_kt, rhs_fn, epi, psum, banks, cv=None,
              wbuf=None, after=(), wq="gpsimd", ntb=NTB, mm_extra=None):
    kw = n_kt * 128
    if wbuf is None:
        wbuf = cv.take(3 * kw, BF16)
    mm_last = {}
    freeev = {}
    out = []
    wev = {}
    for mt in range(n_mt):
        b = mt % 3
        sl = slice(b * kw, (b + 1) * kw)
        wev[mt] = core.dma(wq, wbuf[:, sl], wt_dram[mt], stream=f"{tag}w{b}",
                           after=[mm_last.get(mt - 3)] + (list(after) if mt < 3 else []))
        for tb in range(ntb):
            slot = (mt * ntb + tb) % len(banks)
            bank = psum[banks[slot]]
            dep = [wev[mt], freeev.get(slot)]
            for ct in range(n_kt):
                ev = core.run("tensor",
                              lambda e, p=bank, l=wbuf[:, b * kw + ct * 128:b * kw + (ct + 1) * 128],
                              r=rhs_fn(ct, tb), f=(ct == 0), s=(ct == n_kt - 1):
                              e.matmul(p[:, 0:512], l, r, start=f, stop=s),
                              after=dep, sig=(ct == n_kt - 1))
                dep = []
            if mm_extra is not None:
                ev = mm_extra(mt, tb, bank, ev)
            mm_last[mt] = ev
            fe = epi(mt, tb, bank, ev)
            freeev[slot] = fe
            out.append(fe)
    return out



def emit_l1_attn(core, tag, xn_arena, cst, psum, dram, cv, after=(), ln_guard=()):
    base = cv.off
    av_arena = cv.take(16 * T, BF16)
    sub = cv.off
    ones1b, ones1r = cst["ones1b"], cst["ones1r"]
    cosb, sinb = cst["cosb"], cst["sinb"]

    qsb = cv.take(T, BF16)
    rsb = cv.take(T, BF16)
    t1 = cv.take(4 * 512, F32)
    bq_sb = cv.take(32, F32)
    bl = core.dma("sync", bq_sb, dram["bqk"], stream="cst", after=list(after))
    st_ev = {}
    rd_ev = {}

    def rhs(ct, tb):
        return xn_arena[:, T * ct + 512 * tb: T * ct + 512 * tb + 512]

    rope_mm = {}

    def epi_qk(mt, tb, bank, ev):
        sl = slice(512 * tb, 512 * (tb + 1))
        e1 = core.run("vector",
                      lambda e, o=qsb[:, sl], p=bank:
                      e.tensor_scalar(o, p[:, 0:512], bq_sb[:, mt:mt + 1], None, ALU.add),
                      after=[ev, bl, rope_mm.get((mt - 1, tb)), rd_ev.get((mt - 1, tb))])
        b2 = psum[4 + tb % 2]
        prev_b2 = (mt, tb - 2) if tb >= 2 else (mt - 1, tb + 2)
        e2 = core.run("tensor",
                      lambda e, p=b2, r=qsb[:, sl]:
                      e.matmul(p[:, 0:512], cst["rt"], r, start=True, stop=True),
                      after=[e1, rd_ev.get(prev_b2)])
        rope_mm[(mt, tb)] = e2
        e3 = core.run("vector",
                      lambda e, p=b2, o=t1[:, sl]: e.tensor_mul(o, p[:, 0:512], sinb[:, sl]),
                      after=[e2])
        e4 = core.run("vector",
                      lambda e, o=rsb[:, sl]: e.tensor_mul(o, qsb[:, sl], cosb[:, sl]),
                      after=[e1, st_ev.get((mt - 1, tb))])
        e5 = core.run("vector",
                      lambda e, o=rsb[:, sl]: e.tensor_add(o, o, t1[:, sl]),
                      after=[e3, e4])
        rd_ev[(mt, tb)] = e5
        dst = dram["qr"] if mt < 16 else dram["kr"]
        st_ev[(mt, tb)] = core.dma("sync", dst[mt % 16][:, sl], rsb[:, sl],
                                   stream=f"qk{tb % 2}", after=[e5])
        return e1

    ev_qk = emit_gemm(core, "qkv", dram["wqkv_slab"], 32, NCT, rhs, epi_qk,
                      psum, (0, 1, 2, 3), cv=cv, after=after)
    phA = [st_ev[(31, tb)] for tb in range(NTB)] + [rd_ev[(31, tb)] for tb in range(NTB)]

    cv.reset(sub)
    wvb = cv.take(2 * 8192, BF16)
    vts = cv.take(2 * 512, BF16)
    vstore = {}
    mmv = {}
    for ib in range(4):
        b = ib % 2
        sl = slice(b * 8192, (b + 1) * 8192)
        wvld = core.dma("gpsimd", wvb[:, sl], dram["wv_rhs"][ib],
                        stream=f"wv{b}", after=[mmv.get((ib - 2, 15))] + (phA if ib < 2 else []))
        for tc in range(16):
            bank = psum[6 + tc % 2]
            dep = [wvld, vstore.get((ib, tc - 2)), phA[0] if ib == 0 and tc < 2 else None,
                   vstore.get((ib - 1, 14 + tc)) if tc < 2 else None]
            for ct in range(NCT):
                ev = core.run("tensor",
                              lambda e, p=bank,
                              l=xn_arena[:, T * ct + 128 * tc:T * ct + 128 * tc + 128],
                              r=wvb[:, b * 8192 + 512 * ct:b * 8192 + 512 * (ct + 1)],
                              f=(ct == 0), s=(ct == NCT - 1):
                              e.matmul(p[:, 0:512], l, r, start=f, stop=s),
                              after=dep, sig=(ct == NCT - 1))
                dep = []
            mmv[(ib, tc)] = ev
            vb = vts[:, 512 * (tc % 2):512 * (tc % 2 + 1)]
            ve = core.run("vector", lambda e, p=bank, o=vb: e.tensor_copy(o, p[:, 0:512]),
                          after=[ev, vstore.get((ib, tc - 2)),
                                 vstore.get((ib - 1, 14 + tc)) if tc < 2 else None])
            vstore[(ib, tc)] = core.dma(
                "sync",
                dram["vtm"].rearrange("p (tc i) -> p tc i", i=2048)[:, tc, 512 * ib:512 * (ib + 1)],
                vb, stream=f"vt{tc % 2}", after=[ve])
    phB = [vstore[(3, tc)] for tc in range(16)] + [mmv[(3, 15)], mmv[(3, 14)]]

    cv.reset(sub)
    qh = cv.take(2 * T, BF16)
    kh = cv.take(2 * T, BF16)
    vh = cv.take(2 * T, BF16)
    expb = cv.take(2 * 512, BF16)
    zrow = cv.take(512, F32)
    zb16 = cv.take(512, BF16)
    zbc = cv.take(512, F32)
    hld = {}
    mm_h = {}
    avn_ev = {}
    exp_ev = {}
    zmm = {}
    avmm = {}
    zevac = {}
    zbc_ev = {}
    zmm4 = {}
    for h in range(16):
        b = h % 2
        sl = slice(b * T, (b + 1) * T)
        pdep = [mm_h.get(h - 2)] + (phA + phB if h < 2 else [])
        lq = core.dma("sync", qh[:, sl], dram["qr"][h], stream=f"hq{b}", after=pdep)
        lk = core.dma("sync", kh[:, sl], dram["kr"][h], stream=f"hk{b}", after=pdep)
        lv = core.dma("gpsimd",
                      vh[:, sl].rearrange("p (tc j) -> p tc j", j=128),
                      dram["vtm"].rearrange("p (tc i) -> p tc i", i=2048)[:, :, 128 * h:128 * (h + 1)],
                      stream=f"hv{b}", after=pdep)
        last_mm = None
        for qc in range(NTB):
            qsl = slice(512 * qc, 512 * (qc + 1))
            for kb in range(4 * qc + 4):
                sbank = psum[kb % 2]
                smm = core.run("tensor",
                               lambda e, p=sbank, l=kh[:, sl][:, 128 * kb:128 * (kb + 1)],
                               r=qh[:, sl][:, qsl]:
                               e.matmul(p[:, 0:512], l, r, start=True, stop=True),
                               after=[lq, lk, exp_ev.get((h, qc, kb - 2), exp_ev.get((h, qc - 1, 4 * qc - 2 + kb)) if qc else None)])
                eb = expb[:, 512 * (kb % 2):512 * (kb % 2 + 1)]
                ee = core.run("scalar",
                              lambda e, p=sbank, o=eb:
                              e.activation(o, p[:, 0:512], AF.Exp, bias=cst["neg20"],
                                           scale=0.08838834764831845),
                              after=[smm, zmm.get((h, qc, kb - 2)), avmm.get((h, qc, kb - 2)),
                                     zmm.get((h, qc - 1, 4 * qc - 2 + kb)) if qc else None,
                                     avmm.get((h, qc - 1, 4 * qc - 2 + kb)) if qc else None,
                                     zmm.get((h - 1, 3, 14 + kb)) if qc == 0 and kb < 2 else None,
                                     avmm.get((h - 1, 3, 14 + kb)) if qc == 0 and kb < 2 else None])
                if kb >= 4 * qc:
                    ee = core.run("vector",
                                  lambda e, o=eb, m=cst[f"mask{kb - 4 * qc}"]:
                                  e.tensor_mul(o, o, m),
                                  after=[ee])
                exp_ev[(h, qc, kb)] = ee
                zmm[(h, qc, kb)] = core.run("tensor",
                                            lambda e, o=eb, f=(kb == 0), s=(kb == 4 * qc + 3):
                                            e.matmul(psum[2][0:1, 0:512], ones1b, o, start=f, stop=s),
                                            after=[ee, (zevac.get((h, qc - 1)) or zevac.get((h - 1, 3))) if kb == 0 else None])
                avmm[(h, qc, kb)] = core.run("tensor",
                                             lambda e, o=eb, p=psum[3 + qc % 2],
                                             l=vh[:, sl][:, 128 * kb:128 * (kb + 1)],
                                             f=(kb == 0), s=(kb == 4 * qc + 3):
                                             e.matmul(p[:, 0:512], l, o, start=f, stop=s),
                                             after=[ee, lv, avn_ev.get((h, qc - 2)) if kb == 0 else None,
                                                    avn_ev.get((h - 1, qc + 2)) if kb == 0 and qc < 2 else None])
            z1 = core.run("vector", lambda e: e.tensor_copy(zrow[0:1, 0:512], psum[2][0:1, 0:512]),
                          after=[zmm[(h, qc, 4 * qc + 3)]])
            zevac[(h, qc)] = z1
            z2 = core.run("vector", lambda e: e.reciprocal(zrow[0:1, 0:512], zrow[0:1, 0:512]),
                          after=[z1])
            z3 = core.run("vector", lambda e: e.tensor_copy(zb16[0:1, 0:512], zrow[0:1, 0:512]),
                          after=[z2, zmm4.get((h, qc - 1)) or zmm4.get((h - 1, 3))])
            z4 = core.run("tensor", lambda e: e.matmul(psum[5][:, 0:512], ones1r, zb16[0:1, 0:512],
                                                       start=True, stop=True),
                          after=[z3, zbc_ev.get((h, qc - 1)) or zbc_ev.get((h - 1, 3))])
            z5 = core.run("vector", lambda e: e.tensor_copy(zbc[:, 0:512], psum[5][:, 0:512]),
                          after=[z4, avn_ev.get((h, qc - 1)) or avn_ev.get((h - 1, 3))])
            zbc_ev[(h, qc)] = z5
            zmm4[(h, qc)] = z4
            avn = core.run("vector",
                           lambda e, o=av_arena[:, T * h + 512 * qc:T * h + 512 * (qc + 1)],
                           p=psum[3 + qc % 2]:
                           e.tensor_mul(o, p[:, 0:512], zbc[:, 0:512]),
                           after=[z5, avmm[(h, qc, 4 * qc + 3)]])
            avn_ev[(h, qc)] = avn
            last_mm = avmm[(h, qc, 4 * qc + 3)]
        mm_h[h] = last_mm
    phC = [avn_ev[(15, qc)] for qc in range(NTB)] + [avn_ev[(14, qc)] for qc in range(NTB)]

    cv.reset(sub)
    osb = cv.take(4 * 512, F32)
    cb_sb = cv.take(16, F32)
    bvb = cv.take(16, BF16)
    bde = cv.take(16, F32)
    l1 = core.dma("sync", cb_sb, dram["bv"], stream="cst", after=phC[:1])
    l2 = core.dma("sync", bde, dram["bdense"], stream="cst", after=phC[:1])
    bc = core.run("vector", lambda e: e.tensor_copy(bvb, cb_sb), after=[l1])
    cbev = {}

    def mk_extra(mt, tb, bank, ev):
        if tb != 0:
            return ev
        dep = [bc, cbev.get(mt - 1)]
        kw = NCT * 128
        b = mt % 3
        last = None
        for ct in range(NCT):
            last = core.run("tensor",
                            lambda e, l=_gw["buf"][:, b * kw + ct * 128:b * kw + (ct + 1) * 128],
                            r=bvb[:, ct:ct + 1], f=(ct == 0), s=(ct == NCT - 1):
                            e.matmul(psum[6][:, 0:1], l, r, start=f, stop=s),
                            after=dep if ct == 0 else [], sig=(ct == NCT - 1))
        e1 = core.run("vector",
                      lambda e: e.scalar_tensor_tensor(cb_sb[:, mt:mt + 1], bde[:, mt:mt + 1],
                                                       1.0, psum[6][:, 0:1], ALU.mult, ALU.add),
                      after=[last, l2])
        cbev[mt] = e1
        return ev

    _gw = {}
    res_ev = []

    def epi_dense(mt, tb, bank, ev):
        e1 = core.run("vector",
                      lambda e, o=osb[:, 512 * tb:512 * (tb + 1)], p=bank:
                      e.tensor_scalar(o, p[:, 0:512], cb_sb[:, mt:mt + 1], None, ALU.add),
                      after=[ev, cbev[mt], res_ev[-4] if len(res_ev) >= 4 else None])
        st = core.dma("gpsimd", dram["x_resid"][128 * mt:128 * (mt + 1), 512 * tb:512 * (tb + 1)],
                      osb[:, 512 * tb:512 * (tb + 1)], stream=f"res{tb}",
                      after=[e1] + list(ln_guard), accum_op=ALU.add)
        res_ev.append(st)
        return e1

    def rhs_av(ct, tb):
        return av_arena[:, T * ct + 512 * tb: T * ct + 512 * tb + 512]

    wbufD = cv.take(3 * NCT * 128, BF16)
    _gw["buf"] = wbufD
    emit_gemm(core, "dens", dram["wd_slab"], 16, NCT, rhs_av, epi_dense,
              psum, (0, 1, 2, 3), wbuf=wbufD, after=phC, mm_extra=mk_extra)
    assert cv.off - base <= 70 * 1024 + 16 * T * 2
    return res_ev



def emit_moe(core, tag, xn_arena, cst, psum, dram, cv, after=(), ln_guard=()):
    base = cv.off
    ones1b, ones1r, identb = cst["ones1b"], cst["ones1r"], cst["identb"]

    glog = cv.take(2048, F32)
    gat = cv.take(16 * 16, F32)
    gatb = cv.take(16, BF16)
    gtmp = cv.take(72, F32)
    cwT = cv.take(2048, BF16)
    gw_sb = cv.take(16, BF16)
    gmm = {}
    gwld = {}
    for ct in range(NCT):
        gwld[ct] = core.dma("gpsimd", gw_sb, dram["gate_slab"][ct], stream="gw",
                            after=list(after) if ct == 0 else [gmm.get(3)])
        for tb in range(NTB):
            f, l = ct == 0, ct == NCT - 1
            gmm[tb] = core.run("tensor",
                               lambda e, p=psum[tb], l2=gw_sb,
                               r=xn_arena[:, T * ct + 512 * tb:T * ct + 512 * (tb + 1)],
                               f=f, s=l:
                               e.matmul(p[0:16, 0:512], l2, r, start=f, stop=s),
                               after=[gwld[ct]])
    eev = {}
    for tb in range(NTB):
        eev[tb] = core.run("scalar",
                           lambda e, p=psum[tb], o=glog[0:16, 512 * tb:512 * (tb + 1)]:
                           e.activation(o, p[0:16, 0:512], AF.Exp, bias=cst["neg8"][0:16, 0:1]),
                           after=[gmm[tb]])
    tev = {}
    ctev = {}
    cwe = {}
    for ch in range(16):
        pb = psum[4]
        pe = core.run("tensor",
                      lambda e, i=glog[0:16, 128 * ch:128 * (ch + 1)]:
                      e.transpose(pb[0:128, 0:16], i, cst["identf"][0:16, 0:16]),
                      after=[eev[ch // 4], tev.get(ch - 1)])
        tev[ch] = core.run("vector",
                           lambda e, o=gat[:, 16 * ch:16 * (ch + 1)], p=pb[0:128, 0:16]:
                           e.tensor_copy(o, p), after=[pe])
        g = gat[:, 16 * ch:16 * (ch + 1)]
        g3 = g.rearrange("p (a b) -> p a b", b=4)
        m01, n01 = gtmp[:, 0:4], gtmp[:, 4:8]
        m23, n23 = gtmp[:, 8:12], gtmp[:, 12:16]
        gm = gtmp[:, 16:20]
        kp = gtmp[:, 20:24]
        sec, t0 = gtmp[:, 24:25], gtmp[:, 25:26]
        mx = gtmp[:, 26:27]
        w16 = gtmp[:, 32:48]
        eqb = gtmp[:, 48:64]
        dep0 = [tev[ch], ctev.get(ch - 1)]
        e1 = core.run("vector", lambda e, a=g3: e.tensor_tensor(m01, a[:, :, 0], a[:, :, 1], ALU.max), after=dep0)
        e2 = core.run("vector", lambda e, a=g3: e.tensor_tensor(n01, a[:, :, 0], a[:, :, 1], ALU.min), after=dep0)
        e3 = core.run("vector", lambda e, a=g3: e.tensor_tensor(m23, a[:, :, 2], a[:, :, 3], ALU.max), after=dep0)
        e4 = core.run("vector", lambda e, a=g3: e.tensor_tensor(n23, a[:, :, 2], a[:, :, 3], ALU.min), after=dep0)
        e5 = core.run("vector", lambda e: e.tensor_tensor(gm, m01, m23, ALU.max), after=[e1, e3])
        f1 = core.run("vector", lambda e: e.tensor_tensor(gtmp[:, 27:28], gm[:, 0:1], gm[:, 1:2], ALU.max), after=[e5])
        f2 = core.run("vector", lambda e: e.tensor_tensor(gtmp[:, 28:29], gm[:, 0:1], gm[:, 1:2], ALU.min), after=[e5])
        f3 = core.run("vector", lambda e: e.tensor_tensor(gtmp[:, 29:30], gm[:, 2:3], gm[:, 3:4], ALU.max), after=[e5])
        f4 = core.run("vector", lambda e: e.tensor_tensor(gtmp[:, 30:31], gm[:, 2:3], gm[:, 3:4], ALU.min), after=[e5])
        f5 = core.run("vector", lambda e: e.tensor_tensor(t0, gtmp[:, 27:28], gtmp[:, 29:30], ALU.min), after=[f1, f3])
        f6 = core.run("vector", lambda e: e.tensor_tensor(sec, gtmp[:, 28:29], gtmp[:, 30:31], ALU.max), after=[f2, f4])
        f7 = core.run("vector", lambda e: e.tensor_tensor(sec, sec, t0, ALU.max), after=[f5, f6])
        e6 = core.run("vector", lambda e: e.tensor_scalar(kp, gm, sec, None, ALU.is_ge), after=[f7])
        e7 = core.run("vector",
                      lambda e, a=g3: e.tensor_tensor(
                          a, a, kp.rearrange("p (a o) -> p a o", o=1).broadcast_to([128, 4, 4]),
                          ALU.mult),
                      after=[e6])
        cp = core.run("vector", lambda e, g=g: e.tensor_copy(w16, g), after=[e7])
        prev = cp
        for it in range(3):
            r1 = core.run("vector", lambda e: e.reduce_max(mx, w16, axis=AX.X), after=[prev])
            r2 = core.run("vector", lambda e: e.tensor_scalar(eqb, w16, mx, None, ALU.is_lt), after=[r1])
            prev = core.run("vector", lambda e: e.tensor_mul(w16, w16, eqb), after=[r2])
        r4 = core.run("vector", lambda e: e.reduce_max(mx, w16, axis=AX.X), after=[prev])
        k1 = core.run("vector", lambda e, g=g: e.tensor_scalar(eqb, g, mx, None, ALU.is_ge), after=[r4])
        k2 = core.run("vector", lambda e, g=g: e.tensor_mul(g, g, eqb), after=[k1])
        k3 = core.run("vector", lambda e, g=g: e.reduce_sum(mx, g, axis=AX.X), after=[k2])
        k4 = core.run("vector", lambda e: e.tensor_scalar(mx, mx, 1e-20, None, ALU.add), after=[k3])
        k5 = core.run("vector", lambda e: e.reciprocal(mx, mx), after=[k4])
        k6 = core.run("vector", lambda e, g=g: e.tensor_scalar(gatb, g, mx, None, ALU.mult), after=[k5, ctev.get(ch - 1)])
        pe2 = core.run("tensor",
                       lambda e: e.transpose(psum[5].bitcast(BF16)[0:16, 0:128], gatb, identb),
                       after=[k6, ctev.get(ch - 1)])
        ctev[ch] = core.run("vector",
                            lambda e, o=cwT[0:16, 128 * ch:128 * (ch + 1)],
                            p=psum[5].bitcast(BF16)[0:16, 0:128]:
                            e.tensor_copy(o, p), after=[pe2])
        cwe[ch] = k6
    cwf = core.dram_tmp(f"cwf{tag}", (16, 2048), BF16)
    fl1 = core.dma("sync", cwf[:, :], cwT[0:16, :], stream="cwf",
                   after=[ctev[15]])
    ph_gate = [fl1, ctev[15]]

    cv.reset(base)
    hgrp = cv.take(33 * 1024, BF16)
    hsil = cv.take(1024, BF16)
    wup = cv.take(3 * 2048, BF16)
    wdn = cv.take(3 * 4224, BF16)
    otmp = cv.take(2 * 512, F32)
    cwrow = cv.take(1024, BF16)[0:1, :]
    cwb = cv.take(1024, BF16)
    groups = [[0, 1, 2], [3, 4, 5], [6, 7, 8], [9, 10, 11], [12, 13, 14], [15, "sh"]]
    res_ev = []
    evev = {}
    upmm = {}
    dnmm = {}
    silev = {}
    cw_c1 = {"prev": None, "mm0": None, "mm1": None, "ev0": None, "ev1": None}
    bank_free = {}
    hsil_free = {}
    otmp_free = {}
    hrd = {}
    last_sil = {}
    for tbh in range(2):
        for gi, grp in enumerate(groups):
            hoff = 0
            guards = [hrd.get((tbh, gi - 1)), hrd.get((tbh - 1, len(groups) - 1)) if gi == 0 else None]
            for e_ in grp:
                sh = e_ == "sh"
                n_it = 22 if sh else 11
                slab = dram["sup_slab"] if sh else dram["up_slab"][e_]
                if not sh:
                    lcw = core.dma("sync", cwrow,
                                   cwf[e_:e_ + 1, 1024 * tbh:1024 * (tbh + 1)],
                                   stream="cw", after=ph_gate + [cw_c1["prev"]])
                    c2 = core.run("tensor",
                                  lambda e: e.matmul(psum[6][:, 0:512], ones1r, cwrow[0:1, 0:512],
                                                     start=True, stop=True),
                                  after=[lcw, cw_c1["ev0"]])
                    c3 = core.run("tensor",
                                  lambda e: e.matmul(psum[7][:, 0:512], ones1r, cwrow[0:1, 512:1024],
                                                     start=True, stop=True),
                                  after=[lcw, cw_c1["ev1"]])
                    c4 = core.run("vector", lambda e: e.tensor_copy(cwb[:, 0:512], psum[6][:, 0:512]),
                                  after=[c2, last_sil.get("cw")])
                    c5 = core.run("vector", lambda e: e.tensor_copy(cwb[:, 512:1024], psum[7][:, 0:512]),
                                  after=[c3, last_sil.get("cw")])
                    cw_c1["prev"] = c3
                    cw_c1["ev0"], cw_c1["ev1"] = c4, c5
                    cwev = [c4, c5]
                else:
                    cwev = []
                for i in range(n_it):
                    for half, mt in ((0, i), (1, n_it + i)):
                        idx = 2 * i + half
                        b = idx % 3
                        sl = slice(b * 2048, (b + 1) * 2048)
                        wev = core.dma("gpsimd", wup[:, sl], slab[mt], stream=f"up{b}",
                                       after=[upmm.get((tbh, gi, e_, idx - 3)),
                                              upmm.get("last") if idx < 3 else None] +
                                       (list(after) if (tbh, gi, idx) < (0, 0, 3) else []))
                        for sub in range(2):
                            bank = psum[2 * half + sub]
                            dep = ([wev, bank_free.get(2 * half + sub)] +
                                   (guards if idx < 2 else []))
                            for ct in range(NCT):
                                ev = core.run("tensor",
                                              lambda e, p=bank,
                                              l=wup[:, b * 2048 + 128 * ct:b * 2048 + 128 * (ct + 1)],
                                              r=xn_arena[:, T * ct + 1024 * tbh + 512 * sub:
                                                         T * ct + 1024 * tbh + 512 * (sub + 1)],
                                              f=(ct == 0), s=(ct == NCT - 1):
                                              e.matmul(p[:, 0:512], l, r, start=f, stop=s),
                                              after=dep, sig=(ct == NCT - 1))
                                dep = []
                            upmm[(tbh, gi, e_, idx)] = ev
                            ssl = slice(512 * sub, 512 * (sub + 1))
                            if half == 0:
                                se = core.run(
                                    "scalar", lambda e, p=bank, o=hsil[:, ssl]:
                                    e.activation(o, p[:, 0:512], AF.Silu),
                                    after=[ev, hsil_free.get(sub)])
                                silev[(tbh, gi, e_, idx, sub)] = se
                                bank_free[2 * half + sub] = se
                            else:
                                hv = hgrp[:, 1024 * (hoff + i) + 512 * sub:
                                          1024 * (hoff + i) + 512 * (sub + 1)]
                                m1 = core.run("vector", lambda e, p=bank, o=hv, s2=hsil[:, ssl]:
                                              e.tensor_mul(o, s2, p[:, 0:512]),
                                              after=[ev, silev[(tbh, gi, e_, idx - 1, sub)]] +
                                              ([hrd.get((tbh, gi - 1)), hrd.get((tbh - 1, len(groups) - 1))] if idx < 2 else []))
                                bank_free[2 * half + sub] = m1
                                if not sh:
                                    m1 = core.run("vector", lambda e, o=hv, c=cwb[:, ssl]:
                                                  e.tensor_mul(o, o, c),
                                                  after=[m1] + cwev)
                                silev[(tbh, gi, e_, idx, sub)] = m1
                                bank_free[2 * half + sub] = m1
                                hsil_free[sub] = m1
                upmm["last"] = upmm[(tbh, gi, e_, 2 * n_it - 1)]
                last_sil["cw"] = silev[(tbh, gi, e_, 2 * n_it - 1, 1)]
                hoff += n_it
            grp_h = [silev[(tbh, gi, e_2, (43 if e_2 == "sh" else 21), s2)]
                     for e_2 in grp for s2 in range(2)]
            dnld = {}
            last_dn = None
            for ot in range(16):
                b2 = ot % 3
                woffs = {}
                wo = 0
                for e_ in grp:
                    sh = e_ == "sh"
                    wc = 2816 if sh else 1408
                    dslab = dram["sdn_slab"] if sh else dram["dn_slab"][e_]
                    woffs[e_] = wo
                    dnld[(e_, ot)] = core.dma("gpsimd",
                                              wdn[:, b2 * 4224 + wo:b2 * 4224 + wo + wc],
                                              dslab[ot], stream=f"dn{b2}",
                                              after=[dnmm.get((tbh, gi, ot - 3, 1))] +
                                              (grp_h if ot < 3 else []))
                    wo += wc
                for sub in range(2):
                    bank = psum[4 + sub]
                    first = True
                    dep = [evev.get((tbh, gi, ot - 1, sub)),
                           evev.get((tbh, gi - 1, 15, sub)) if ot == 0 else None,
                           evev.get((tbh - 1, len(groups) - 1, 15, sub)) if ot == 0 and gi == 0 else None] + grp_h
                    hoff2 = 0
                    ev = None
                    for e_ in grp:
                        sh = e_ == "sh"
                        n_it = 22 if sh else 11
                        wo = woffs[e_]
                        for it in range(n_it):
                            lv = wdn[:, b2 * 4224 + wo + 128 * it:b2 * 4224 + wo + 128 * (it + 1)]
                            ev = core.run("tensor",
                                          lambda e, p=bank, l=lv,
                                          r=hgrp[:, 1024 * (hoff2 + it) + 512 * sub:
                                                 1024 * (hoff2 + it) + 512 * (sub + 1)],
                                          f=first, s=(e_ is grp[-1] and it == n_it - 1):
                                          e.matmul(p[:, 0:512], l, r, start=f, stop=s),
                                          after=[dnld[(e_, ot)]] + (dep if first else []),
                                          sig=(e_ is grp[-1] and it == n_it - 1))
                            first = False
                        hoff2 += n_it
                    dnmm[(tbh, gi, ot, sub)] = ev
                    last_dn = ev
                    ov = core.run("vector",
                                  lambda e, p=bank, o=otmp[:, 512 * sub:512 * (sub + 1)]:
                                  e.tensor_copy(o, p[:, 0:512]),
                                  after=[ev, otmp_free.get(sub)])
                    evev[(tbh, gi, ot, sub)] = ov
                    res = core.dma("gpsimd",
                                   dram["x_resid"][128 * ot:128 * (ot + 1),
                                                   1024 * tbh + 512 * sub:
                                                   1024 * tbh + 512 * (sub + 1)],
                                   otmp[:, 512 * sub:512 * (sub + 1)],
                                   stream=f"mres{(2 * ot + sub) % 4}",
                                   after=[ov] + list(ln_guard), accum_op=ALU.add)
                    otmp_free[sub] = res
                    res_ev.append(res)
            hrd[(tbh, gi)] = last_dn
    return res_ev



def emit_l0_mixes(core, tag, xn_arena, cst, psum, dram, cv, after=()):
    base = cv.off
    tm6 = cv.take(96, F32)
    td = cv.take(16, F32)
    w1s = cv.take(16 * 160, BF16)
    w2s = cv.take(5 * 16 * 128, BF16)
    dw1s = cv.take(16 * 64, BF16)
    dw2s = cv.take(16 * 128, BF16)
    lds = []
    lds.append(core.dma("sync", tm6, dram["tm6"], stream="cst", after=list(after)))
    lds.append(core.dma("sync", td, dram["td"], stream="cst"))
    lds.append(core.dma("gpsimd", w1s, dram["w1m"], stream="wsm", after=list(after)))
    lds.append(core.dma("gpsimd", w2s[0:32, :], dram["w2m"], stream="wsm"))
    lds.append(core.dma("gpsimd", dw1s, dram["dw1"], stream="wsm"))
    lds.append(core.dma("gpsimd", dw2s[0:64, :], dram["dw2"], stream="wsm"))

    xxs = cv.take(16 * 512, BF16)
    xxx = cv.take(16 * 512, BF16)
    ajs = cv.take(5 * 512, BF16)
    a2 = cv.take(512, BF16)
    t1 = cv.take(2 * 512, F32)
    mslab = cv.take(16 * 512, BF16)
    wout = cv.take(16 * 512, BF16)
    phA_end = cv.off

    mixst = {}
    wst = {}
    amm = {}
    atan = {}
    xxev = {}
    xxxev = {}
    evjc = {}
    dtanev = {}
    t1_free = {}
    for tb in range(NTB):
        t0 = 512 * tb
        for ct in range(NCT):
            av = xn_arena[:, T * ct + t0:T * ct + t0 + 512]
            xv = xxs[:, 512 * ct:512 * (ct + 1)]
            dep = [mixst.get((tb - 1, 4)), lds[0]]
            if tb == 0:
                e0 = core.run("vector", lambda e, o=xv[:, 0:1], a=xn_arena[:, T * ct:T * ct + 1]:
                              e.tensor_scalar(o, a, -1.0, None, ALU.mult),
                              after=dep + [evjc.get((3, ct, "g"), None)])
                e1 = core.run("vector",
                              lambda e, o=xv[:, 1:512], a=xn_arena[:, T * ct:T * ct + 511],
                              b=xn_arena[:, T * ct + 1:T * ct + 512]:
                              e.tensor_sub(o, a, b), after=[e0])
            else:
                e1 = core.run("vector",
                              lambda e, o=xv, a=xn_arena[:, T * ct + t0 - 1:T * ct + t0 + 511],
                              b=av: e.tensor_sub(o, a, b),
                              after=dep + [evjc.get((tb - 1, ct, "g"), None)])
            xxev[ct] = e1
            xxxev[ct] = core.run("vector",
                                 lambda e, o=xxx[:, 512 * ct:512 * (ct + 1)], x=xv, a=av,
                                 s=tm6[:, 6 * ct:6 * ct + 1]:
                                 e.scalar_tensor_tensor(o, x, s, a, ALU.mult, ALU.add),
                                 after=[e1, atan.get((tb - 1, 4))])
        for j in range(5):
            for ct in range(NCT):
                f, l = ct == 0, ct == NCT - 1
                amm[j] = core.run("tensor",
                                  lambda e, p=psum[j], w=w1s[:, ct * 160 + 32 * j:ct * 160 + 32 * (j + 1)],
                                  r=xxx[:, 512 * ct:512 * (ct + 1)], f=f, s=l:
                                  e.matmul(p[0:32, 0:512], w, r, start=f, stop=s),
                                  after=[xxxev[ct], lds[2], atan.get((tb - 1, j)),
                                         dtanev.get(tb - 1) if j == 0 else None,
                                         evjc.get((tb - 1, 14, "we")) if j == 1 else None,
                                         evjc.get((tb - 1, 15, "we")) if j == 2 else None])
            atan[(tb, j)] = core.run("scalar",
                                     lambda e, p=psum[j], o=ajs[0:32, 512 * j:512 * (j + 1)]:
                                     e.activation(o, p[0:32, 0:512], AF.Tanh),
                                     after=[amm[j], evjc.get((tb - 1, 15, "m1s")) if j == 0 else None])
        for j in range(5):
            for ct in range(NCT):
                bank = psum[5 + ct % 3]
                mm = core.run("tensor",
                              lambda e, p=bank, w=w2s[0:32, (j * 16 + ct) * 128:(j * 16 + ct + 1) * 128],
                              r=ajs[0:32, 512 * j:512 * (j + 1)]:
                              e.matmul(p[:, 0:512], w, r, start=True, stop=True),
                              after=[atan[(tb, j)], lds[3],
                                     evjc.get((j, ct - 3, "e1s")) if ct >= 3 else
                                     (evjc.get((j - 1, [15, 13, 14][ct], "e1s")) if j > 0 else
                                      evjc.get((4, [15, 13, 14][ct], "e1s")))])
                e1s = core.run("scalar",
                               lambda e, p=bank, o=t1[:, 512 * (ct % 2):512 * (ct % 2 + 1)],
                               b=tm6[:, 6 * ct + j + 1:6 * ct + j + 2]:
                               e.activation(o, p[:, 0:512], AF.Identity, bias=b),
                               after=[mm, lds[0], t1_free.get(ct % 2)])
                e2g = core.run("vector",
                               lambda e, o=mslab[:, 512 * ct:512 * (ct + 1)],
                               a=xxs[:, 512 * ct:512 * (ct + 1)],
                               b=t1[:, 512 * (ct % 2):512 * (ct % 2 + 1)]:
                               e.tensor_mul(o, a, b),
                               after=[e1s, mixst.get((tb, j - 1)) if j > 0 else mixst.get((tb - 1, 4)),
                                      evjc.get(("dmm", tb)) if j == 1 else None])
                e3v = core.run("vector",
                               lambda e, o=mslab[:, 512 * ct:512 * (ct + 1)],
                               a=xn_arena[:, T * ct + t0:T * ct + t0 + 512]:
                               e.tensor_add(o, o, a), after=[e2g])
                evjc[(j, ct, "e1s")] = e1s
                evjc[(j, ct, "e2g")] = e2g
                t1_free[ct % 2] = e2g
                evjc[(j, ct, "e3v")] = e3v
            mixst[(tb, j)] = core.dma("sync",
                                      dram["mix"][j].rearrange("p (c t) -> p c t", t=T)[:, :, t0:t0 + 512],
                                      mslab[:].rearrange("p (c t) -> p c t", t=512),
                                      stream=f"mx{j % 2}",
                                      after=[evjc[(j, c2, "e3v")] for c2 in range(NCT)])
            if j == 0:
                dmm = None
                for ct in range(NCT):
                    f, l = ct == 0, ct == NCT - 1
                    dmm = core.run("tensor",
                                   lambda e, p=psum[0], w=dw1s[:, ct * 64:(ct + 1) * 64],
                                   r=mslab[:, 512 * ct:512 * (ct + 1)], f=f, s=l:
                                   e.matmul(p[0:64, 0:512], w, r, start=f, stop=s),
                                   after=[evjc[(0, ct, "e3v")], lds[4]])
                dtan = core.run("scalar", lambda e: e.activation(a2[0:64, :], psum[0][0:64, 0:512], AF.Tanh),
                                after=[dmm])
                dtanev[tb] = dtan
                evjc[("dmm", tb)] = dmm
                for mt in range(NCT):
                    bank = psum[1 + mt % 2]
                    wm = core.run("tensor",
                                  lambda e, p=bank, w=dw2s[0:64, 128 * mt:128 * (mt + 1)]:
                                  e.matmul(p[:, 0:512], w, a2[0:64, :], start=True, stop=True),
                                  after=[dtan, lds[5], evjc.get((tb, mt - 2, "we"))])
                    we = core.run("scalar",
                                  lambda e, p=bank, o=wout[:, 512 * mt:512 * (mt + 1)],
                                  b=td[:, mt:mt + 1]:
                                  e.activation(o, p[:, 0:512], AF.Identity, bias=b),
                                  after=[wm, lds[1], wst.get(tb - 1)])
                    evjc[(tb, mt, "we")] = we
                wst[tb] = core.dma("sync",
                                   dram["wT"].rearrange("p (c t) -> p c t", t=T)[:, :, t0:t0 + 512],
                                   wout[:].rearrange("p (c t) -> p c t", t=512),
                                   stream="wtt",
                                   after=[evjc[(tb, m2, "we")] for m2 in range(NCT)])
    phA = [mixst[(NTB - 1, j)] for j in range(5)] + [wst[NTB - 1]]

    cv.reset(base)
    msl = cv.take(16 * 512, BF16)
    oslab = cv.take(16 * 512, BF16)
    wbufp = cv.take(3 * 2048, BF16)
    projs = [("rT", "wr_slab", 3, None), ("kT", "wk_slab", 1, None),
             ("vT", "wv_slab", 2, None), ("gsT", "wg_slab", 4, AF.Silu)]
    out_ev = list(phA)
    mld = {}
    pmm = {}
    wbuf_free = {}
    pev = {}
    ost = {}
    for (oname, wname, j, act) in projs:
        for tb in range(NTB):
            t0 = 512 * tb
            mld[(oname, tb)] = core.dma("sync",
                                        msl[:].rearrange("p (c t) -> p c t", t=512),
                                        dram["mix"][j].rearrange("p (c t) -> p c t", t=T)[:, :, t0:t0 + 512],
                                        stream="msl",
                                        after=phA + [pmm.get((oname, tb - 1, mt2)) for mt2 in (14, 15)])
            for mt in range(NCT):
                b = mt % 3
                sl = slice(b * 2048, (b + 1) * 2048)
                wev = core.dma("gpsimd", wbufp[:, sl], dram[wname][mt], stream=f"pw{b}",
                               after=[wbuf_free.get(b)] + (phA if (tb, mt) < (0, 3) else []))
                bank = psum[mt % 4]
                dep = [wev, mld[(oname, tb)], pev.get((oname, tb, mt - 4)),
                       pev.get((oname, tb - 1, 12 + mt)) if mt < 4 else None]
                ev = None
                for ct in range(NCT):
                    ev = core.run("tensor",
                                  lambda e, p=bank, w=wbufp[:, b * 2048 + 128 * ct:b * 2048 + 128 * (ct + 1)],
                                  r=msl[:, 512 * ct:512 * (ct + 1)], f=(ct == 0), s=(ct == NCT - 1):
                                  e.matmul(p[:, 0:512], w, r, start=f, stop=s),
                                  after=dep, sig=(ct == NCT - 1))
                    dep = []
                pmm[(oname, tb, mt)] = ev
                wbuf_free[b] = ev
                if act is None:
                    pe = core.run("vector",
                                  lambda e, p=bank, o=oslab[:, 512 * mt:512 * (mt + 1)]:
                                  e.tensor_copy(o, p[:, 0:512]),
                                  after=[ev, ost.get("prev")])
                else:
                    pe = core.run("scalar",
                                  lambda e, p=bank, o=oslab[:, 512 * mt:512 * (mt + 1)], a2=act:
                                  e.activation(o, p[:, 0:512], a2),
                                  after=[ev, ost.get("prev")])
                pev[(oname, tb, mt)] = pe
            ost["prev"] = ost[(oname, tb)] = core.dma("sync",
                                        dram[oname].rearrange("p (c t) -> p c t", t=T)[:, :, t0:t0 + 512],
                                        oslab[:].rearrange("p (c t) -> p c t", t=512),
                                        stream="ost",
                                        after=[pev[(oname, tb, m2)] for m2 in range(NCT)])
            out_ev.append(ost[(oname, tb)])
    return out_ev



def emit_wkv(core, tag, cst, psum, dram, cv, after=()):
    identb = cst["identb"]
    ufa = cv.take(16, F32)
    uld = core.dma("sync", ufa, dram["ufa"], stream="cst", after=list(after))
    rb = cv.take(2 * T, BF16)
    kb = cv.take(2 * T, BF16)
    vb = cv.take(2 * T, BF16)
    wb = cv.take(2 * T, BF16)
    ew = cv.take(T, F32)
    gc = cv.take(T, F32)
    lc = cv.take(T, F32)
    d1 = cv.take(T, F32)
    qf = cv.take(T, BF16)
    kf = cv.take(T, BF16)
    qt = cv.take(2 * T, BF16)
    kt = cv.take(2 * T, BF16)
    kh = cv.take(2 * T, BF16)
    rku = cv.take(2 * T, BF16)
    etot = cv.take(64, F32)
    alf = cv.take(4, F32)
    s32 = cv.take(64, F32)
    sbf = cv.take(64, BF16)
    vA = cv.take(128, BF16)
    vB = cv.take(128, BF16)
    hA = cv.take(128, BF16)
    hB = cv.take(128, BF16)
    abar = cv.take(128, BF16)
    ytm = cv.take(128, BF16)
    ybuf = cv.take(2 * T, BF16)

    ld = {}
    w1 = {}
    tile_mm_last = {}
    tile_sbf_last = {}
    tile_s32_last = {}
    st_ev = {}
    tile_done = {}
    prev = {}

    def emit_loads(ct):
        p = ct % 2
        tsl = slice(p * T, (p + 1) * T)
        csl = slice(2048 * ct, 2048 * (ct + 1))
        dep = [tile_done.get(ct - 2)] + (list(after) if ct < 2 else [])
        l_r = core.dma("sync", rb[:, tsl], dram["rT"][:, csl], stream=f"wr{p}", after=dep)
        l_k = core.dma("sync", kb[:, tsl], dram["kT"][:, csl], stream=f"wk{p}", after=dep)
        l_v = core.dma("gpsimd", vb[:, tsl], dram["vT"][:, csl], stream=f"wv{p}", after=dep)
        l_w = core.dma("gpsimd", wb[:, tsl], dram["wT"][:, csl], stream=f"ww{p}", after=dep)
        ld[ct] = (l_r, l_k, l_v, l_w)

    def emit_w1(ct):
        p = ct % 2
        tsl = slice(p * T, (p + 1) * T)
        pv = w1.get(ct - 1, {})
        pv2 = w1.get(ct - 2, {})
        g2 = tile_mm_last.get(ct - 2)
        l_r, l_k, l_v, l_w = ld[ct]
        d = {}
        d["ew"] = core.run("scalar", lambda e, i=wb[:, tsl]: e.activation(ew[:], i, AF.Exp),
                           after=[l_w, pv.get("gc"), pv.get("d1")])
        d["gc"] = core.run("vector", lambda e: e.tensor_tensor_scan(gc[:], ew[:], ew[:], 0.0,
                                                                   ALU.add, ALU.bypass),
                           after=[d["ew"], pv.get("lc1"), pv.get("lc2"), pv.get("etot")])
        d["lc1"] = core.run("vector", lambda e: e.tensor_copy(lc[:, 0:64], gc[:, 0:64]),
                            after=[d["gc"], pv.get("d1"), pv.get("kf"), pv.get("etot")])
        d["lc2"] = core.run("vector", lambda e: e.tensor_tensor(
            lc[:, 64:T].rearrange("p (c n) -> p c n", n=64),
            gc[:, 64:T].rearrange("p (c n) -> p c n", n=64),
            gc[:, 63:T - 64:64].rearrange("p (c n) -> p c n", n=1).broadcast_to([128, 31, 64]),
            ALU.subtract),
            after=[d["gc"], pv.get("d1")])
        d["d1"] = core.run("vector", lambda e: e.tensor_sub(d1[:], lc[:], ew[:]),
                           after=[d["lc1"], d["lc2"], pv.get("qf")])
        d["qf"] = core.run("scalar", lambda e: e.activation(qf[:], d1[:], AF.Exp, scale=-1.0),
                           after=[d["d1"], pv.get("qt")])
        d["kf"] = core.run("scalar", lambda e: e.activation(kf[:], lc[:], AF.Exp),
                           after=[d["lc1"], d["lc2"], pv.get("kt")])
        d["etot"] = core.run("scalar",
                             lambda e, o=etot[0:128, 32 * p:32 * p + 32]:
                             e.activation(o, lc[:, 63:T:64], AF.Exp, scale=-1.0),
                             after=[d["lc1"], d["lc2"], pv2.get("kh"),
                                    tile_s32_last.get(ct - 2), g2])
        d["qt"] = core.run("gpsimd", lambda e, o=qt[:, tsl], r=rb[:, tsl]:
                           e.tensor_mul(o, r, qf[:]),
                           after=[l_r, d["qf"], g2])
        d["kt"] = core.run("gpsimd", lambda e, o=kt[:, tsl], k=kb[:, tsl]:
                           e.tensor_mul(o, k, kf[:]),
                           after=[l_k, d["kf"], g2])
        d["kh"] = core.run("vector", lambda e, o=kh[:, tsl], k=kt[:, tsl],
                           et=etot[0:128, 32 * p:32 * p + 32]:
                           e.tensor_tensor(
                               o.rearrange("p (c n) -> p c n", n=64),
                               k.rearrange("p (c n) -> p c n", n=64),
                               et.rearrange("p (c n) -> p c n", n=1).broadcast_to([128, 32, 64]),
                               ALU.mult),
                           after=[d["kt"], d["etot"], g2])
        d["rk"] = core.run("gpsimd", lambda e, o=rku[:, tsl], r=rb[:, tsl], k=kb[:, tsl]:
                           e.tensor_mul(o, r, k),
                           after=[l_r, l_k, g2])
        d["rku"] = core.run("vector", lambda e, o=rku[:, tsl], uc=ufa[:, ct:ct + 1]:
                            e.tensor_scalar(o, o, uc, None, ALU.mult),
                            after=[d["rk"], uld])
        w1[ct] = d

    def emit_w2(ct):
        p = ct % 2
        tsl = slice(p * T, (p + 1) * T)
        d = w1[ct]
        l_r, l_k, l_v, l_w = ld[ct]
        qtv, ktv, khv, rkuv = qt[:, tsl], kt[:, tsl], kh[:, tsl], rku[:, tsl]
        eto = etot[0:128, 32 * p:32 * p + 32]
        s32_ev = core.run("vector", lambda e: e.memset(s32[:], 0.0),
                          after=[tile_sbf_last.get(ct - 1), tile_mm_last.get(ct - 1)] +
                          (list(after) if ct == 0 else []))
        sbf_ev = core.run("scalar", lambda e: e.copy(sbf[:], s32[:]), after=[s32_ev])
        pA = psum[0]
        pVA, pVB = psum[1].bitcast(BF16), psum[2].bitcast(BF16)
        pHA, pHB = psum[3].bitcast(BF16), psum[4].bitcast(BF16)
        pY = [psum[5], psum[6]]
        pS = psum[7]
        pYT = psum[1].bitcast(BF16)
        ymm_last = None
        ev_y = {}
        for cp in range(16):
            c0, c1 = 2 * cp, 2 * cp + 1
            w128 = slice(128 * cp, 128 * (cp + 1))
            tvA = core.run("tensor", lambda e, i=vb[:, tsl][:, w128]:
                           e.transpose(pVA[:, 0:128], i, identb), after=[l_v, prev.get("vAe")])
            tvB0 = core.run("tensor", lambda e, i=vb[:, tsl][:, 128 * cp + 64:128 * cp + 128]:
                            e.transpose(pVB[0:64, 0:128], i, identb), after=[l_v, prev.get("vBe")])
            tvB1 = core.run("tensor", lambda e, i=vb[:, tsl][:, 128 * cp:128 * cp + 64]:
                            e.transpose(pVB[64:128, 0:128], i, identb, tile_position=(0, 64)),
                            after=[l_v, prev.get("vBe")])
            thA = core.run("tensor", lambda e, i=khv[:, w128]:
                           e.transpose(pHA[:, 0:128], i, identb), after=[d["kh"], prev.get("hAe")])
            thB0 = core.run("tensor", lambda e, i=khv[:, 128 * cp + 64:128 * cp + 128]:
                            e.transpose(pHB[0:64, 0:128], i, identb), after=[d["kh"], prev.get("hBe")])
            thB1 = core.run("tensor", lambda e, i=khv[:, 128 * cp:128 * cp + 64]:
                            e.transpose(pHB[64:128, 0:128], i, identb, tile_position=(0, 64)),
                            after=[d["kh"], prev.get("hBe")])
            vAe = core.run("scalar", lambda e: e.copy(vA[:], pVA[:, 0:128]), after=[tvA, prev.get("y_sel_done")])
            vBe = core.run("scalar", lambda e: e.copy(vB[:], pVB[:, 0:128]), after=[tvB0, tvB1, prev.get("y_sel_done")])
            hAe = core.run("scalar", lambda e: e.copy(hA[:], pHA[:, 0:128]), after=[thA, prev.get("su1")])
            hBe = core.run("scalar", lambda e: e.copy(hB[:], pHB[:, 0:128]), after=[thB0, thB1, prev.get("su2")])
            prev["vAe"], prev["vBe"] = vAe, vBe
            prev["hAe"], prev["hBe"] = hAe, hBe
            amms = []
            for h in range(2):
                for c, cc in ((0, c0), (1, c1)):
                    amms.append(core.run("tensor",
                                         lambda e, l=ktv[64 * h:64 * h + 64, 64 * cc:64 * cc + 64],
                                         r=qtv[64 * h:64 * h + 64, 64 * cc:64 * cc + 64],
                                         o=pA[64 * h:64 * h + 64, 64 * c:64 * c + 64],
                                         tp=(64 * h, 64 * h):
                                         e.matmul(o, l, r, start=True, stop=True, tile_position=tp),
                                         after=[d["qt"], d["kt"], prev.get("mask")]))
            msk = core.run("vector", lambda e: e.tensor_mul(abar[:], pA[:, 0:128], cst["maskU"]),
                           after=amms + [prev.get("ymm_last")])
            prev["mask"] = msk
            selmap = {(0, 0): vA, (0, 1): vB, (1, 0): vB, (1, 1): vA}
            helmap = {0: hA, 1: hB}
            for c, cc in ((0, c0), (1, c1)):
                for h in range(2):
                    i1 = core.run("tensor",
                                  lambda e, l=abar[64 * h:64 * h + 64, 64 * c:64 * c + 64],
                                  r=selmap[(h, c)][64 * h:64 * h + 64, 64 * h:64 * h + 64],
                                  o=pY[c][64 * h:64 * h + 64, 0:64], tp=(64 * h, 64 * h):
                                  e.matmul(o, l, r, start=True, stop=False, tile_position=tp),
                                  after=[msk, vAe, vBe, prev.get(("yev", c))])
                    i2 = core.run("tensor",
                                  lambda e, l=qtv[64 * h:64 * h + 64, 64 * cc:64 * cc + 64],
                                  r=sbf[64 * h:64 * h + 64, 0:64],
                                  o=pY[c][64 * h:64 * h + 64, 0:64], tp=(64 * h, 64 * h):
                                  e.matmul(o, l, r, start=False, stop=True, tile_position=tp),
                                  after=[sbf_ev])
                    al = core.run("tensor",
                                  lambda e, l=rkuv[64 * h:64 * h + 64, 64 * cc:64 * cc + 64],
                                  o=pY[c][64 * h:64 * h + 64, 64:65], tp=(64 * h, 64 * h),
                                  oc=cst["ones1b"][64 * h:64 * h + 64, 0:1]:
                                  e.matmul(o, l, oc, start=True, stop=True, tile_position=tp),
                                  after=[d["rku"], prev.get(("alev", c))])
                    ymm_last = al
                ale = core.run("scalar", lambda e, c=c: e.copy(alf[:, c:c + 1], pY[c][:, 64:65]),
                               after=[ymm_last, prev.get(("alrd", c))])
                prev[("alev", c)] = ale
                yevs = []
                for h in range(2):
                    yevs.append(core.run("vector",
                                         lambda e, v2=selmap[(h, c)][64 * h:64 * h + 64, 64 * h:64 * h + 64],
                                         a=alf[64 * h:64 * h + 64, c:c + 1],
                                         p2=pY[c][64 * h:64 * h + 64, 0:64],
                                         o=ytm[64 * h:64 * h + 64, 64 * c:64 * c + 64]:
                                         e.scalar_tensor_tensor(o, v2, a, p2, ALU.mult, ALU.add),
                                         after=[ale, prev.get(("ytev", c, h))]))
                prev[("yev", c)] = yevs[-1]
                prev[("alrd", c)] = yevs[-1]
                ev_y[(cp, c)] = yevs
                su = []
                for h in range(2):
                    su.append(core.run("tensor",
                                       lambda e, l=helmap[c][0:64, 64 * h:64 * h + 64],
                                       r=selmap[(0, c)][0:64, 64 * h:64 * h + 64],
                                       o=pS[64 * h:64 * h + 64, 0:64], tp=(0, 64 * h):
                                       e.matmul(o, l, r, start=True, stop=True, tile_position=tp),
                                       after=[hAe, hBe, vAe, vBe, prev.get("sue")]))
                s32_ev = core.run("vector",
                                  lambda e, sc=eto[0:128, c0 + c:c0 + c + 1]:
                                  e.scalar_tensor_tensor(s32[:], s32[:], sc, pS[:, 0:64],
                                                         ALU.mult, ALU.add),
                                  after=su + [s32_ev, sbf_ev])
                sbf_ev = core.run("scalar", lambda e: e.copy(sbf[:], s32[:]),
                                  after=[s32_ev])
                prev["sue"] = sbf_ev
                if c == 0:
                    prev["su1"] = su[-1]
                else:
                    prev["su2"] = su[-1]
            prev["ymm_last"] = ymm_last
            ytts = []
            for c in range(2):
                for h in range(2):
                    ytts.append(core.run("tensor",
                                         lambda e, i=ytm[64 * h:64 * h + 64, 64 * c:64 * c + 64],
                                         o=pYT[64 * h:64 * h + 64, 64 * c:64 * c + 64],
                                         idn=identb[64 * h:64 * h + 64, 64 * h:64 * h + 64],
                                         tp=(64 * h, 64 * h):
                                         e.transpose(o, i, idn, tile_position=tp),
                                         after=[ev_y[(cp, c)][h], prev.get("vAe2"), vAe]))
                    prev[("ytev", c, h)] = ytts[-1]
            yte = core.run("vector", lambda e, o=ybuf[:, tsl][:, w128]:
                           e.tensor_copy(o, pYT[:, 0:128]),
                           after=ytts + [st_ev.get(ct - 2)])
            prev["vAe2"] = yte
            prev["vAe"] = yte
            prev["y_sel_done"] = prev[("yev", 1)]
        tile_mm_last[ct] = ymm_last
        tile_sbf_last[ct] = sbf_ev
        tile_s32_last[ct] = s32_ev
        st_ev[ct] = core.dma("sync", dram["yT"][:, 2048 * ct:2048 * (ct + 1)], ybuf[:, tsl],
                             stream=f"yst{p}", after=[yte])
        tile_done[ct] = st_ev[ct]

    for ct in range(NCT):
        emit_loads(ct)
        emit_w1(ct)
        if ct:
            emit_w2(ct - 1)
    emit_w2(NCT - 1)
    return [st_ev[c9] for c9 in range(NCT)]



def emit_l0_post(core, tag, cst, psum, dram, cv, after=(), ln_guard=(), resid_q="gpsimd"):
    base = cv.off
    EPS = 1e-5 * 64.0
    gnw = cv.take(32, F32)
    gl = core.dma("sync", gnw, dram["gnw"], stream="cst", after=list(after))
    yblk = cv.take(16 * 512, BF16)
    gblk = cv.take(16 * 512, BF16)
    sq = cv.take(512, BF16)
    zsl = cv.take(16 * 512, BF16)
    rows = cv.take(8 * 512, F32)
    rowb = cv.take(2 * 512, BF16)
    wbuf = cv.take(3 * 2048, BF16)
    osb = cv.take(2 * 512, F32)
    t1b = cv.take(512, F32)

    res_ev = []
    ldprev = {}
    zev = {}
    mmo = {}
    sqev = {}
    stev = {}
    bcev = {}
    t1ev = {}
    oev = {}
    for tb in range(NTB):
        ysrc = dram["yT"].rearrange("p (c t) -> p c t", t=T)[:, :, 512 * tb:512 * (tb + 1)]
        gsrc = dram["gsT"].rearrange("p (c t) -> p c t", t=T)[:, :, 512 * tb:512 * (tb + 1)]
        ly = core.dma("sync", yblk.rearrange("p (c t) -> p c t", t=512), ysrc, stream="ypl",
                      after=[zev.get((tb - 1, 15))] + (list(after) if tb == 0 else []))
        lg = core.dma("gpsimd", gblk.rearrange("p (c t) -> p c t", t=512), gsrc, stream="gpl",
                      after=[zev.get((tb - 1, 15))] + (list(after) if tb == 0 else []))
        for ct in range(NCT):
            yv = yblk[:, 512 * ct:512 * (ct + 1)]
            e_sq = core.run("scalar", lambda e, i=yv: e.activation(sq[:], i, AF.Square),
                            after=[ly, sqev.get((tb, ct - 1))])
            m_ev = []
            for q, (rhs_, oc) in enumerate(((yv, "onesh0"), (yv, "onesh1"),
                                            (sq[:], "onesh0"), (sq[:], "onesh1"))):
                dep = [ly if q < 2 else e_sq,
                       stev.get((tb, ct - 1, q)) if ct else stev.get((tb - 1, 15, q))]
                m_ev.append(core.run("tensor",
                                     lambda e, p=psum[q], r=rhs_, o=cst[oc]:
                                     e.matmul(p[0:1, 0:512], o, r, start=True, stop=True),
                                     after=dep))
            sqev[(tb, ct)] = e_sq
            ev_r = []
            for hh in range(2):
                mu = rows[0:1, 512 * hh:512 * (hh + 1)]
                qq = rows[0:1, 1024 + 512 * hh:1536 + 512 * hh]
                A = rows[0:1, 2048 + 512 * hh:2560 + 512 * hh]
                B = rows[0:1, 3072 + 512 * hh:3584 + 512 * hh]
                r1 = core.run("vector", lambda e, o=mu, p=psum[hh]:
                              e.tensor_scalar(o, p[0:1, 0:512], 1.0 / 64, None, ALU.mult),
                              after=[m_ev[hh], bcev.get((tb, ct - 1))])
                stev[(tb, ct, hh)] = r1
                r2 = core.run("vector", lambda e, o=qq, p=psum[2 + hh]:
                              e.tensor_scalar(o, p[0:1, 0:512], 1.0 / 64, None, ALU.mult),
                              after=[m_ev[2 + hh], bcev.get((tb, ct - 1))])
                stev[(tb, ct, 2 + hh)] = r2
                r3 = core.run("scalar", lambda e, o=A, i=mu: e.activation(o, i, AF.Square),
                              after=[r1, bcev.get((tb, ct - 1))])
                r4 = core.run("vector", lambda e, o=A, i=qq: e.tensor_sub(o, i, o), after=[r3, r2])
                r5 = core.run("vector", lambda e, o=A: e.tensor_scalar(o, o, EPS, None, ALU.add),
                              after=[r4])
                r6 = core.run("scalar", lambda e, o=A: e.activation(o, o, AF.Sqrt), after=[r5])
                r7 = core.run("vector", lambda e, o=A: e.reciprocal(o, o), after=[r6])
                r8 = core.run("vector", lambda e, o=B, i=mu, a=A: e.tensor_mul(o, i, a), after=[r7])
                ev_r.append(r8)
            c1 = core.run("vector", lambda e: e.tensor_copy(rowb[0:1, 0:512], rows[0:1, 2048:2560]),
                          after=ev_r + [bcev.get((tb, ct - 1))])
            c2 = core.run("vector", lambda e: e.tensor_copy(rowb[0:1, 512:1024], rows[0:1, 2560:3072]),
                          after=ev_r)
            c3 = core.run("vector", lambda e: e.tensor_copy(rowb[0:1, 0:512][0:1, 0:512], rowb[0:1, 0:512]),
                          after=[]) if False else None
            c4 = core.run("vector", lambda e: e.tensor_copy(t1b[0:1, 0:512], rows[0:1, 3072:3584]),
                          after=ev_r) if False else None
            bA, bB = [], []
            for hh in range(2):
                bA.append(core.run("tensor",
                                   lambda e, r=rowb[0:1, 512 * hh:512 * (hh + 1)], tp=(0, 64 * hh),
                                   o=psum[4][64 * hh:64 * hh + 64, 0:512],
                                   l=cst["ones1r"][0:1, 0:64]:
                                   e.matmul(o, l, r, start=True, stop=True, tile_position=tp),
                                   after=[c1, c2, t1ev.get((tb, ct - 1)) if ct else t1ev.get((tb - 1, 15))]))
            cB1 = core.run("vector", lambda e: e.tensor_copy(rowb[0:1, 0:512], rows[0:1, 3072:3584]),
                           after=[bA[0], bA[1]])
            cB2 = core.run("vector", lambda e: e.tensor_copy(rowb[0:1, 512:1024], rows[0:1, 3584:4096]),
                           after=[bA[0], bA[1]])
            for hh in range(2):
                bB.append(core.run("tensor",
                                   lambda e, r=rowb[0:1, 512 * hh:512 * (hh + 1)], tp=(0, 64 * hh),
                                   o=psum[5][64 * hh:64 * hh + 64, 0:512],
                                   l=cst["ones1r"][0:1, 0:64]:
                                   e.matmul(o, l, r, start=True, stop=True, tile_position=tp),
                                   after=[cB1, cB2, (t1ev.get((tb, ct - 1, "b")) if ct else t1ev.get((tb - 1, 15, "b")))]))
            bcev[(tb, ct)] = bB[1]
            e1 = core.run("vector", lambda e, i=yv: e.tensor_mul(t1b[:], i, psum[4][:, 0:512]),
                          after=bA + [t1ev.get((tb, ct - 1))])
            t1ev[(tb, ct)] = e1
            e2 = core.run("vector", lambda e: e.tensor_sub(t1b[:], t1b[:], psum[5][:, 0:512]),
                          after=[e1] + bB)
            t1ev[(tb, ct, "b")] = e2
            e3 = core.run("vector",
                          lambda e, g2=gnw[:, ct:ct + 1], b2=gnw[:, 16 + ct:17 + ct]:
                          e.tensor_scalar(t1b[:], t1b[:], g2, b2, ALU.mult, ALU.add),
                          after=[e2, gl])
            e4 = core.run("vector",
                          lambda e, o=zsl[:, 512 * ct:512 * (ct + 1)], g3=gblk[:, 512 * ct:512 * (ct + 1)]:
                          e.tensor_mul(o, t1b[:], g3),
                          after=[e3, lg, mmo.get((tb - 1, 15)) if ct == 0 else None])
            zev[(tb, ct)] = e4
        for mt in range(NCT):
            b = mt % 3
            sl = slice(b * 2048, (b + 1) * 2048)
            wev = core.dma("gpsimd", wbuf[:, sl], dram["wo_slab"][mt], stream=f"wo{b}",
                           after=[mmo.get((tb, mt - 3)), mmo.get((tb - 1, 13 + mt)) if mt < 3 else None] +
                           (list(after) if (tb, mt) < (0, 3) else []))
            bank = psum[6 + mt % 2]
            dep = [wev, oev.get((tb, mt - 2)), oev.get((tb - 1, 14 + mt)) if mt < 2 else None] + \
                  [zev[(tb, c2b)] for c2b in range(NCT)]
            ev = None
            for ctk in range(NCT):
                ev = core.run("tensor",
                              lambda e, p=bank, l=wbuf[:, b * 2048 + 128 * ctk:b * 2048 + 128 * (ctk + 1)],
                              r=zsl[:, 512 * ctk:512 * (ctk + 1)], f=(ctk == 0), s=(ctk == NCT - 1):
                              e.matmul(p[:, 0:512], l, r, start=f, stop=s),
                              after=dep, sig=(ctk == NCT - 1))
                dep = []
            mmo[(tb, mt)] = ev
            oe = core.run("vector", lambda e, p=bank, o=osb[:, 512 * (mt % 2):512 * (mt % 2 + 1)]:
                          e.tensor_copy(o, p[:, 0:512]),
                          after=[ev, res_ev[-2] if len(res_ev) >= 2 else None])
            oev[(tb, mt)] = oe
            res_ev.append(core.dma(resid_q,
                                   dram["x_resid"][128 * mt:128 * (mt + 1), 512 * tb:512 * (tb + 1)],
                                   osb[:, 512 * (mt % 2):512 * (mt % 2 + 1)],
                                   stream=f"ores{mt % 2}",
                                   after=[oe] + list(ln_guard), accum_op=ALU.add))
    return res_ev



def host_prep(inputs):
    f = {}
    tms = [np.asarray(inputs[f"L0_attn_time_maa_{n}"]).reshape(-1)
           for n in ["x", "w", "k", "v", "r", "g"]]
    f["tm6"] = np.ascontiguousarray(
        np.stack(tms, 1).reshape(16, 128, 6).transpose(1, 0, 2).reshape(128, 96)).astype(np.float32)
    f["td"] = np.ascontiguousarray(
        np.asarray(inputs["L0_attn_time_decay"]).reshape(-1).reshape(16, 128).T).astype(np.float32)
    f["w1m"] = _bf(np.asarray(inputs["L0_attn_time_maa_w1"])
                   .reshape(16, 128, 160).transpose(1, 0, 2).reshape(128, 2560))
    f["w2m"] = _bf(np.asarray(inputs["L0_attn_time_maa_w2"])
                   .reshape(5, 32, 16, 128).transpose(1, 0, 2, 3).reshape(32, 10240))
    f["dw1"] = _bf(np.asarray(inputs["L0_attn_time_decay_w1"])
                   .reshape(16, 128, 64).transpose(1, 0, 2).reshape(128, 1024))
    f["dw2"] = _bf(np.asarray(inputs["L0_attn_time_decay_w2"])
                   .reshape(64, 16, 128).transpose(1, 0, 2).reshape(64, 2048))
    for nm in ["r", "k", "v", "g", "o"]:
        f[f"w{nm}_slab"] = pack_lhsT(np.asarray(inputs[f"L0_attn_W_{nm}"]), BF)
    f["ufa"] = np.ascontiguousarray(
        np.asarray(inputs["L0_attn_time_faaaa"]).reshape(-1).reshape(16, 128).T).astype(np.float32)
    f["gnw"] = np.ascontiguousarray(np.concatenate(
        [np.asarray(inputs["L0_attn_ln_gamma"]).reshape(16, 128).T,
         np.asarray(inputs["L0_attn_ln_beta"]).reshape(16, 128).T], 1)).astype(np.float32)
    f["gu_bytes"], f["gu_sc"] = pack_bytes_fp4(np.asarray(inputs["L0_ffn_gate_up_packed"]),
                                               np.asarray(inputs["L0_ffn_gate_up_scales"]))
    f["dn_bytes"], f["dn_sc"] = pack_bytes_fp4(np.asarray(inputs["L0_ffn_down_packed"]),
                                               np.asarray(inputs["L0_ffn_down_scales"]))
    wqkv = np.asarray(inputs["L1_attn_w_qkv"])
    bqkv = np.asarray(inputs["L1_attn_b_qkv"])
    f["wqkv_slab"] = pack_lhsT(wqkv[0:4096], BF)
    f["wv_rhs"] = _bf(np.ascontiguousarray(
        wqkv[4096:].T.reshape(16, 128, 4, 512).transpose(2, 1, 0, 3)).reshape(4, 128, 8192))
    f["wd_slab"] = pack_lhsT(np.asarray(inputs["L1_attn_w_dense"]), BF)
    f["bqk"] = np.ascontiguousarray(bqkv[0:4096].reshape(32, 128).T)
    f["bv"] = np.ascontiguousarray(bqkv[4096:].reshape(16, 128).T)
    f["bdense"] = np.ascontiguousarray(np.asarray(inputs["L1_attn_b_dense"]).reshape(16, 128).T)
    f["gate_slab"] = _bf(np.asarray(inputs["L1_ffn_gate_w"]).T.reshape(16, 128, 16))
    up_w = np.asarray(inputs["L1_ffn_expert_up_w"])
    dn_w = np.asarray(inputs["L1_ffn_expert_down_w"])
    f["up_slab"] = np.stack([pack_lhsT(up_w[e], BF) for e in range(16)])
    f["dn_slab"] = np.stack([pack_lhsT(dn_w[e], BF) for e in range(16)])
    f["sup_slab"] = pack_lhsT(np.asarray(inputs["L1_ffn_shared_up_w"]), BF)
    f["sdn_slab"] = pack_lhsT(np.asarray(inputs["L1_ffn_shared_down_w"]), BF)

    c = host_constants()
    S = T
    inv = 1.0 / (10000.0 ** (np.arange(0, 128, 2) / 128.0))
    fr = np.arange(S)[:, None] * inv[None, :]
    emb = np.concatenate([fr, fr], 1)
    c["cosb"] = _bf(np.cos(emb).T)
    c["sinb"] = _bf(np.sin(emb).T)
    R = np.zeros((128, 128), np.float32)
    for dr in range(64):
        R[dr, dr + 64] = -1.0
        R[dr + 64, dr] = 1.0
    c["rt"] = _bf(R.T)
    c["neg20"] = np.full((128, 1), -20.0, np.float32)
    c["neg8"] = np.full((128, 1), -8.0, np.float32)
    for j in range(4):
        kl = np.arange(128)[:, None]
        ql = np.arange(512)[None, :]
        c[f"mask{j}"] = _bf((128 * j + kl <= ql).astype(np.float32))
    rr = np.arange(128)[:, None] % 64
    cc2 = np.arange(128)[None, :] % 64
    c["maskU"] = _bf((rr < cc2).astype(np.float32))
    o0 = np.zeros((128, 1), np.float32); o0[0:64] = 1
    o1 = np.zeros((128, 1), np.float32); o1[64:128] = 1
    c["onesh0"] = _bf(o0)
    c["onesh1"] = _bf(o1)
    return f, c


_DTYPES = {"gu_bytes": U8, "dn_bytes": U8, "gu_sc": F16, "dn_sc": F16}
for _n in ["w1m", "w2m", "dw1", "dw2", "wr_slab", "wk_slab", "wv_slab", "wg_slab",
           "wo_slab", "wqkv_slab", "wv_rhs", "wd_slab", "gate_slab", "up_slab",
           "dn_slab", "sup_slab", "sdn_slab"]:
    _DTYPES[_n] = BF16


def _declare(core, f, names):
    d = {}
    for n in names:
        arr = f[n]
        dt = _DTYPES.get(n, F32)
        d[n] = core.dram_in(n, arr.shape, dt)
    return d


CSTDT = {"shift01": U16, "six16": U16, "neg20": F32, "neg8": F32, "identf": F32,
         "ones1b": BF16}


def _run(nc, feeds_list, n):
    return run_bass_kernel_spmd(nc, feeds_list, core_ids=list(range(n)))


def _mk_core():
    nc = bass.Bass("TRN2", target_bir_lowering=False)
    core = Core(nc)
    psum = [core.psb(f"p{i}") for i in range(8)]
    return nc, core, psum


ALL_CST = ["ones1b", "ones1r", "identb", "identf", "shift01", "neg20", "neg8",
           "cosb", "sinb", "rt", "mask0", "mask1", "mask2", "mask3", "maskU",
           "onesh0", "onesh1"]
ALL_W = ["tm6", "td", "w1m", "w2m", "dw1", "dw2", "wr_slab", "wk_slab", "wv_slab",
         "wg_slab", "wo_slab", "ufa", "gnw", "gu_bytes", "gu_sc", "dn_bytes",
         "dn_sc", "wqkv_slab", "wv_rhs", "wd_slab", "bqk", "bv", "bdense",
         "gate_slab", "up_slab", "dn_slab", "sup_slab", "sdn_slab"]


def build_fused():
    nc = bass.Bass("TRN2", target_bir_lowering=False)
    core = Core(nc)
    psum = [core.psb(f"p{i}") for i in range(8)]
    xd = core.dram_in("x", (C, T), F32)
    out = core.dram_out("out", (C, T), F32)
    dram = {}
    shapes = {"tm6": (128, 96), "td": (128, 16), "w1m": (128, 2560), "w2m": (32, 10240),
              "dw1": (128, 1024), "dw2": (64, 2048), "ufa": (128, 16), "gnw": (128, 32),
              "gu_bytes": (88, 128, 2048), "gu_sc": (88, 128, 2048),
              "dn_bytes": (16, 128, 5632), "dn_sc": (16, 128, 5632),
              "wqkv_slab": (32, 128, 2048), "wv_rhs": (4, 128, 8192),
              "wd_slab": (16, 128, 2048), "bqk": (128, 32), "bv": (128, 16),
              "bdense": (128, 16), "gate_slab": (16, 128, 16),
              "up_slab": (16, 22, 128, 2048), "dn_slab": (16, 16, 128, 1408),
              "sup_slab": (44, 128, 2048), "sdn_slab": (16, 128, 2816)}
    for nm in ["wr_slab", "wk_slab", "wv_slab", "wg_slab", "wo_slab"]:
        shapes[nm] = (16, 128, 2048)
    for nm in ALL_W:
        dram[nm] = core.dram_in(nm, shapes[nm], _DTYPES.get(nm, F32))
    dram["mix"] = core.dram_tmp("mix", (5, 128, NCT * T), BF16)
    for nm in ["rT", "kT", "vT", "gsT", "wT", "yT"]:
        dram[nm] = core.dram_tmp(nm, (128, NCT * T), BF16)
    dram["h_scr"] = core.dram_tmp("h_scr", (128, 44 * T), BF16)
    dram["qr"] = core.dram_tmp("qr", (16, 128, T), BF16)
    dram["kr"] = core.dram_tmp("kr", (16, 128, T), BF16)
    dram["vtm"] = core.dram_tmp("vtm", (128, 16 * T), BF16)
    dram["x_resid"] = out

    cst, cev = load_constants(core, {**host_constants(), **_fused_cst_stub()}, ALL_CST)
    arena = core.sb("arena", (128, NCT * T), BF16)
    cv = Carver(core, nbytes=120 * 1024)

    iev = core.dma("gpsimd", out[:, :], xd[:, :], stream="init")
    cv.reset(0)
    ln0 = emit_ln(core, "0", xd, arena[:], cst, psum, cv, after=cev)
    cv.reset(0)
    mx = emit_l0_mixes(core, "0", arena[:], cst, psum, dram, cv,
                       after=ln0 + core.stream_tails())
    cv.reset(0)
    wk = emit_wkv(core, "0", cst, psum, dram, cv, after=mx + core.stream_tails())
    cv.reset(0)
    po = emit_l0_post(core, "0", cst, psum, dram, cv,
                      after=wk + core.stream_tails(), ln_guard=[iev])
    cv.reset(0)
    ff = emit_ffn_fp4(core, "0", arena[:], cst, psum, dram, cv,
                      after=po + core.stream_tails(), resid_q="gpsimd")
    cv.reset(0)
    ln1 = emit_ln(core, "1", out, arena[:], cst, psum, cv,
                  after=ff + po + [iev] + core.stream_tails())
    cv.reset(0)
    at = emit_l1_attn(core, "1", arena[:], cst, psum, dram, cv,
                      after=ln1 + core.stream_tails(), ln_guard=ln1)
    cv.reset(0)
    mo = emit_moe(core, "1", arena[:], cst, psum, dram, cv,
                  after=at + core.stream_tails(), ln_guard=ln1)
    core.wait("sync", mo + at + ff + po + [iev])
    core.build_block()
    core.finish()
    return nc


_CST_CACHE = {}


def _fused_cst_stub():
    c = {}
    S = T
    inv = 1.0 / (10000.0 ** (np.arange(0, 128, 2) / 128.0))
    fr = np.arange(S)[:, None] * inv[None, :]
    emb = np.concatenate([fr, fr], 1)
    c["cosb"] = _bf(np.cos(emb).T)
    c["sinb"] = _bf(np.sin(emb).T)
    R = np.zeros((128, 128), np.float32)
    for dr in range(64):
        R[dr, dr + 64] = -1.0
        R[dr + 64, dr] = 1.0
    c["rt"] = _bf(R.T)
    c["neg20"] = np.full((128, 1), -20.0, np.float32)
    c["neg8"] = np.full((128, 1), -8.0, np.float32)
    for j in range(4):
        kl = np.arange(128)[:, None]
        ql = np.arange(512)[None, :]
        c[f"mask{j}"] = _bf((128 * j + kl <= ql).astype(np.float32))
    rr = np.arange(128)[:, None] % 64
    cc2 = np.arange(128)[None, :] % 64
    c["maskU"] = _bf((rr < cc2).astype(np.float32))
    o0 = np.zeros((128, 1), np.float32); o0[0:64] = 1
    o1 = np.zeros((128, 1), np.float32); o1[64:128] = 1
    c["onesh0"] = _bf(o0)
    c["onesh1"] = _bf(o1)
    return c


def kernel(**inputs):
    f, cst_host = host_prep(inputs)
    x = np.asarray(inputs["L0_attn_x"])
    ncores = x.shape[0]
    nc = build_fused()
    feeds = []
    for b in range(ncores):
        m = {n: f[n] for n in ALL_W}
        for n in ALL_CST:
            m[f"cst_{n}"] = cst_host[n]
        m["x"] = np.ascontiguousarray(x[b].T).astype(np.float32)
        feeds.append(m)
    res = run_bass_kernel_spmd(nc, feeds, core_ids=list(range(ncores)))
    return np.stack([np.ascontiguousarray(res.results[b]["out"].T)
                     for b in range(ncores)]).astype(np.float32)


# version 09 (final)
# speedup vs baseline: 1.0509x; 1.0509x over previous
import contextlib
import numpy as np
import ml_dtypes
import concourse.bass as bass
import concourse.mybir as mybir
from concourse.bass_utils import run_bass_kernel_spmd

F32 = mybir.dt.float32
BF16 = mybir.dt.bfloat16
U8 = mybir.dt.uint8
U16 = mybir.dt.uint16
I32 = mybir.dt.int32
F16 = mybir.dt.float16
AF = mybir.ActivationFunctionType
ALU = mybir.AluOpType
AX = mybir.AxisListType
BF = ml_dtypes.bfloat16

C = 2048
T = 2048
NCT = C // 128
NTB = T // 512
NCORES = 8



class Ev:
    __slots__ = ("sem", "val", "eng")

    def __init__(self, sem, val, eng):
        self.sem, self.val, self.eng = sem, val, eng


class Core:

    ENGINES = ("sync", "scalar", "vector", "tensor", "gpsimd")

    def __init__(self, nc):
        self.nc = nc
        self.es = contextlib.ExitStack()
        self.ops = []
        self.dram = {}
        self.esem, self.ecnt = {}, {}
        for e in self.ENGINES:
            self.esem[e] = self.es.enter_context(nc.semaphore(f"E_{e}"))
            self.ecnt[e] = 0
        self.waited = {e: {} for e in self.ENGINES}
        self.streams = {}

    def dram_in(self, name, shape, dtype):
        t = self.nc.dram_tensor(name, list(shape), dtype, kind="ExternalInput").ap()
        self.dram[name] = t
        return t

    def dram_out(self, name, shape, dtype):
        t = self.nc.dram_tensor(name, list(shape), dtype, kind="ExternalOutput").ap()
        self.dram[name] = t
        return t

    def dram_tmp(self, name, shape, dtype):
        t = self.nc.dram_tensor(name, list(shape), dtype, kind="Internal").ap()
        self.dram[name] = t
        return t

    def sb(self, name, shape, dtype):
        return self.es.enter_context(self.nc.sbuf_tensor(name, list(shape), dtype))

    def psb(self, name):
        return self.es.enter_context(self.nc.psum_tensor(name, [128, 512], F32))

    def _emit_waits(self, eng, after):
        need = {}
        for ev in after:
            if ev is None:
                continue
            key = id(ev.sem)
            if key not in need or need[key][1] < ev.val:
                need[key] = (ev.sem, ev.val)
        out = []
        wm = self.waited[eng]
        for key, (sem, val) in need.items():
            if wm.get(key, -1) >= val:
                continue
            wm[key] = val
            out.append((sem, val))
        return out

    def run(self, eng, emit, after=(), sig=True):
        waits = self._emit_waits(eng, after)
        ev = None
        if sig:
            self.ecnt[eng] += 1
            ev = Ev(self.esem[eng], self.ecnt[eng], eng)

        def fn(e, waits=waits, emit=emit, ev=ev):
            for (sem, val) in waits:
                e.wait_ge(sem, val)
            inst = emit(e)
            if ev is not None:
                assert inst is not None
                inst.then_inc(ev.sem, 1)
        self.ops.append((eng, fn))
        return ev

    def dma(self, q, out, in_, stream=None, after=(), **kw):
        assert stream is not None
        st = self.streams.get(stream)
        if st is None:
            st = [self.es.enter_context(self.nc.semaphore(f"S_{stream}")), 0, None]
            self.streams[stream] = st
        waits = self._emit_waits(q, list(after) + [st[2]])
        st[1] += 16
        ev = Ev(st[0], st[1], None)
        st[2] = ev

        def fn(e, waits=waits, out=out, in_=in_, ev=ev, kw=kw):
            for (sem, val) in waits:
                e.wait_ge(sem, val)
            e.dma_start(out=out, in_=in_, **kw).then_inc(ev.sem, 16)
        self.ops.append((q, fn))
        return ev

    def stream_tails(self):
        return [st[2] for st in self.streams.values() if st[2] is not None]

    def wait(self, eng, after):
        waits = self._emit_waits(eng, after)
        if not waits:
            return

        def fn(e, waits=waits):
            for (sem, val) in waits:
                e.wait_ge(sem, val)
        self.ops.append((eng, fn))

    def build_block(self):
        block = self.es.enter_context(self.nc.Block())
        ops = self.ops

        def mk(name):
            def closure(e):
                for eng, fn in ops:
                    if eng == name:
                        fn(e)
            return closure
        block.sync(mk("sync"))
        block.scalar(mk("scalar"))
        block.vector(mk("vector"))
        block.tensor(mk("tensor"))
        block.gpsimd(mk("gpsimd"))

    def finish(self):
        self.es.close()


class Carver:

    def __init__(self, core, nbytes=140 * 1024, name="work"):
        self.buf = core.sb(name, (128, nbytes), U8)
        self.nbytes = nbytes
        self.off = 0

    def reset(self, off=0):
        self.off = off
        return off

    def take(self, cols, dtype):
        size = {F32: 4, BF16: 2, U16: 2, F16: 2, U8: 1, I32: 4}[dtype]
        self.off = (self.off + 511) // 512 * 512
        o = self.off
        self.off += cols * size
        assert self.off <= self.nbytes, f"work arena overflow: {self.off}"
        if dtype == U8:
            return self.buf[:, o:o + cols]
        return self.buf[:, o:o + cols * size].bitcast(dtype)

    def take2(self, cols, dt1, dt2):
        size = {F32: 4, BF16: 2, U16: 2, F16: 2, I32: 4}[dt1]
        self.off = (self.off + 511) // 512 * 512
        o = self.off
        self.off += cols * size
        assert self.off <= self.nbytes, f"work arena overflow: {self.off}"
        raw = self.buf[:, o:o + cols * size]
        return raw.bitcast(dt1), raw.bitcast(dt2)



def _bf(x):
    return np.ascontiguousarray(np.asarray(x, BF))


def _f32(x):
    return np.ascontiguousarray(np.asarray(x, np.float32))


def pack_lhsT(W, dtype=np.float32):
    F, K = W.shape
    assert F % 128 == 0 and K % 128 == 0
    Mct, Kct = F // 128, K // 128
    a = np.asarray(W, dtype).reshape(Mct, 128, Kct, 128)
    a = np.ascontiguousarray(a.transpose(0, 3, 2, 1))
    return a.reshape(Mct, 128, Kct * 128)


def pack_bytes_fp4(packed_i32, group_scales_f16):
    Fo, Kp = packed_i32.shape
    K = Kp * 8
    by = np.ascontiguousarray(packed_i32).view(np.uint8).reshape(Fo, K // 2)
    dup = np.repeat(by.T, 2, axis=0)
    Kct, Mct = K // 128, Fo // 128
    a = dup.reshape(Kct, 128, Mct, 128)
    bslab = np.ascontiguousarray(a.transpose(2, 1, 0, 3)).reshape(Mct, 128, Kct * 128)
    sc = np.asarray(group_scales_f16, np.float16)
    s = sc.T.reshape(Mct, 128, Kct)
    s = np.broadcast_to(s.transpose(0, 2, 1)[:, None, :, :], (Mct, 128, Kct, 128))
    sslab = np.ascontiguousarray(s).reshape(Mct, 128, Kct * 128)
    return bslab, sslab


def host_constants():
    c = {}
    c["ones1b"] = _bf(np.ones((128, 1)))
    c["ones1r"] = _bf(np.ones((1, 128)))
    c["identb"] = _bf(np.eye(128))
    c["identf"] = _f32(np.eye(128))
    c["shift01"] = np.ascontiguousarray(
        np.tile(np.array([0, 4], np.uint16), 64).reshape(128, 1))
    c["six16"] = np.full((128, 1), 6, np.uint16)
    return c


def load_constants(core, cst_host, names):
    out, evs = {}, []
    for n in names:
        h = cst_host[n]
        dt = {np.dtype(np.uint16): U16, np.dtype(np.float32): F32,
              np.dtype(ml_dtypes.bfloat16): BF16, np.dtype(np.float16): F16}[h.dtype]
        d = core.dram_in(f"cst_{n}", h.shape, dt)
        s = core.sb(f"cs_{n}", h.shape, dt)
        evs.append(core.dma("sync", s[:], d[:], stream="cst"))
        out[n] = s[:]
    return out, evs



def emit_ln(core, tag, x_dram, xn_arena, cst, psum, cv, after=()):
    xt = cv.take(2 * T, F32)
    xb = cv.take(2 * T, BF16)
    sq = cv.take(2 * T, BF16)
    stats = cv.take(4 * T, F32)
    sums = stats[0:1, 0:2 * T]
    rstd = stats[0:1, 2 * T:4 * T]
    rsb = cv.take(2 * T, BF16)[0:1, :]
    bcast = cv.take(2 * T, F32)
    ones1b, ones1r = cst["ones1b"], cst["ones1r"]

    lastmm, rdev, mmev = {}, {}, {}
    for ct in range(NCT):
        b = ct % 2
        xv = xt[:, b * T:(b + 1) * T]
        xbv = xb[:, b * T:(b + 1) * T]
        sqv = sq[:, b * T:(b + 1) * T]
        dep = [lastmm.get(ct - 2)] + (list(after) if ct < 2 else [])
        lev = core.dma("sync", xv, x_dram[128 * ct:128 * (ct + 1), :], stream=f"ln{tag}x{b}", after=dep)
        cev = core.run("vector", lambda e, a=xbv, c=xv: e.tensor_copy(a, c), after=[lev])
        qev = core.run("scalar", lambda e, a=sqv, c=xv: e.activation(a, c, AF.Square), after=[lev])
        rdev[ct] = [cev, qev]
        for tb in range(NTB):
            f, l = ct == 0, ct == NCT - 1
            sl = slice(512 * tb, 512 * (tb + 1))
            mmev[(0, tb)] = core.run("tensor",
                                     lambda e, p=psum[tb], r=xbv[:, sl], f=f, l=l:
                                     e.matmul(p[0:1, 0:512], ones1b, r, start=f, stop=l),
                                     after=[cev])
            mmev[(1, tb)] = core.run("tensor",
                                     lambda e, p=psum[4 + tb], r=sqv[:, sl], f=f, l=l:
                                     e.matmul(p[0:1, 0:512], ones1b, r, start=f, stop=l),
                                     after=[qev])
        lastmm[ct] = mmev[(1, NTB - 1)]
    sev = []
    for tb in range(NTB):
        sl = slice(512 * tb, 512 * (tb + 1))
        s2 = slice(T + 512 * tb, T + 512 * (tb + 1))
        sev.append(core.run("vector",
                            lambda e, a=sums[0:1, sl], p=psum[tb]:
                            e.tensor_scalar(a, p[0:1, 0:512], 1.0 / C, None, ALU.mult),
                            after=[mmev[(0, tb)]]))
        sev.append(core.run("vector",
                            lambda e, a=sums[0:1, s2], p=psum[4 + tb]:
                            e.tensor_scalar(a, p[0:1, 0:512], 1.0 / C, None, ALU.mult),
                            after=[mmev[(1, tb)]]))
    e1 = core.run("scalar", lambda e: e.activation(rstd[0:1, 0:T], sums[0:1, 0:T], AF.Square),
                  after=sev)
    e2 = core.run("vector", lambda e: e.tensor_sub(rstd[0:1, 0:T], sums[0:1, T:2 * T], rstd[0:1, 0:T]),
                  after=[e1] + sev)
    e2b = core.run("vector", lambda e: e.tensor_scalar(rstd[0:1, 0:T], rstd[0:1, 0:T], 1e-5, None, ALU.add),
                   after=[e2])
    e3 = core.run("scalar", lambda e: e.activation(rstd[0:1, 0:T], rstd[0:1, 0:T], AF.Sqrt),
                  after=[e2b])
    e4 = core.run("vector", lambda e: e.reciprocal(rstd[0:1, 0:T], rstd[0:1, 0:T]), after=[e3])
    e5 = core.run("vector", lambda e: e.tensor_mul(rstd[0:1, T:2 * T], sums[0:1, 0:T], rstd[0:1, 0:T]),
                  after=[e4])
    e6 = core.run("vector", lambda e: e.tensor_copy(rsb[0:1, :], rstd[0:1, :]), after=[e5])
    bev, pprev = [], {}
    for blk in range(2 * NTB):
        sl = slice(512 * blk, 512 * (blk + 1))
        bank = psum[6 + (blk % 2)]
        pe = core.run("tensor",
                      lambda e, p=bank, r=rsb[0:1, sl]:
                      e.matmul(p[:, 0:512], ones1r, r, start=True, stop=True),
                      after=[e6, pprev.get(blk - 2)])
        ev = core.run("vector",
                      lambda e, a=bcast[:, sl], p=bank: e.tensor_copy(a, p[:, 0:512]),
                      after=[pe])
        pprev[blk] = ev
        bev.append(ev)
    done = []
    for ct in range(NCT):
        b = ct % 2
        xv = xt[:, b * T:(b + 1) * T]
        dep = [done[ct - 2]] if ct >= 2 else rdev[NCT - 2 + ct]
        lev = core.dma("sync", xv, x_dram[128 * ct:128 * (ct + 1), :], stream=f"ln{tag}x{b}", after=dep)
        xnv = xn_arena[:, T * ct:T * (ct + 1)]
        m1 = core.run("vector", lambda e, o=xnv, x=xv: e.tensor_mul(o, x, bcast[:, 0:T]),
                      after=[lev] + bev)
        done.append(core.run("vector", lambda e, o=xnv: e.tensor_sub(o, o, bcast[:, T:2 * T]),
                             after=[m1]))
    return done



def emit_dequant2(core, bbuf, scbuf, deq, nib, sgn, cst, ev_in):
    W = deq.shape[1]
    e1 = core.run("vector", lambda e: e.tensor_tensor(nib, bbuf, cst["shift01"].broadcast_to([128, W]),
                                                      ALU.logical_shift_right),
                  after=ev_in)
    e2 = core.run("vector", lambda e: e.tensor_scalar(sgn, nib, 8, 12,
                                                      ALU.bitwise_and, ALU.logical_shift_left),
                  after=[e1])
    e3 = core.run("vector", lambda e: e.tensor_scalar(nib, nib, 7, None, ALU.bitwise_and),
                  after=[e2])
    e4 = core.run("vector", lambda e: e.tensor_scalar(bbuf, nib, 252, None, ALU.add),
                  after=[e3])
    e5 = core.run("vector", lambda e: e.scalar_tensor_tensor(bbuf, nib, 252.0, bbuf,
                                                             ALU.mult, ALU.min),
                  after=[e4])
    e6 = core.run("vector", lambda e: e.scalar_tensor_tensor(nib, bbuf, 64.0, sgn,
                                                             ALU.mult, ALU.add),
                  after=[e5])
    e7 = core.run("vector", lambda e: e.tensor_mul(deq, deq, scbuf),
                  after=[e6])
    return e6, e7



def emit_ffn_fp4(core, tag, xn_arena, cst, psum, dram, cv, after=(), resid_q="gpsimd"):
    IT = 44
    ones1b, ones1r = cst["ones1b"], cst["ones1r"]
    base = cv.off

    wb = cv.take(2 * 2048, U16)
    wsc = cv.take(2 * 2048, F16)
    wdq, wnib = cv.take2(2 * 2048, BF16, U16)
    wsg = cv.take(2 * 2048, U16)
    hsil = cv.take(T, BF16)
    hbuf = cv.take(T, BF16)

    def rhs(ct, tb):
        return xn_arena[:, T * ct + 512 * tb: T * ct + 512 * tb + 512]

    hev = {}
    mm_last = {}
    silu_ev = {}
    dq6, dq7 = {}, {}

    def emit_mult(i):
        for tb in range(NTB):
            silu_ev[(1, i, tb)] = core.run(
                "vector", lambda e, p=psum[4 + tb], o=hbuf[:, 512 * tb:512 * (tb + 1)],
                s=hsil[:, 512 * tb:512 * (tb + 1)]:
                e.tensor_mul(o, s, p[:, 0:512]),
                after=[mm_last[(2 * i + 1, tb)], silu_ev[(0, i, tb)],
                       hev.get(i - 1)])
        hev[i] = core.dma("sync", dram["h_scr"][:, 2048 * i:2048 * (i + 1)], hbuf[:],
                          stream="hst", after=[silu_ev[(1, i, tb)] for tb in range(NTB)])

    def emit_mms(idx, b, d7, bank0):
        for tb in range(NTB):
            bank = psum[bank0 + tb]
            mmdep = [d7, silu_ev.get((bank0 // 4, (idx - 2) // 2, tb))]
            for ct in range(NCT):
                ev = core.run("tensor",
                              lambda e, p=bank, l=wdq[:, b * 2048 + ct * 128:b * 2048 + (ct + 1) * 128],
                              r=rhs(ct, tb), f=(ct == 0), s=(ct == NCT - 1):
                              e.matmul(p[:, 0:512], l, r, start=f, stop=s),
                              after=mmdep, sig=(ct == NCT - 1))
                mmdep = []
            mm_last[(idx, tb)] = ev
        mm_last[idx] = ev

    def emit_dq(idx, mt, b):
        sl = slice(b * 2048, (b + 1) * 2048)
        dep0 = list(after) if idx < 2 else []
        eb = core.dma("gpsimd", wb[:, sl], dram["gu_bytes"][mt], stream=f"gub{b}",
                      after=[dq6.get(idx - 2)] + dep0)
        es = core.dma("sync", wsc[:, sl], dram["gu_sc"][mt], stream=f"gus{b}",
                      after=[dq7.get(idx - 2)] + dep0)
        d6, d7 = emit_dequant2(core, wb[:, sl], wsc[:, sl], wdq[:, sl],
                               wnib[:, sl], wsg[:, sl], cst,
                               [eb, es, mm_last.get(idx - 2)])
        dq6[idx], dq7[idx] = d6, d7
        return d7

    for i in range(IT):
        d7 = emit_dq(2 * i, i, 0)
        emit_mms(2 * i, 0, d7, 0)
        if i:
            emit_mult(i - 1)
        for tb in range(NTB):
            silu_ev[(0, i, tb)] = core.run(
                "scalar", lambda e, p=psum[tb], o=hsil[:, 512 * tb:512 * (tb + 1)]:
                e.activation(o, p[:, 0:512], AF.Silu),
                after=[mm_last[(2 * i, tb)], silu_ev.get((1, i - 1, tb))])
        d7 = emit_dq(2 * i + 1, IT + i, 1)
        emit_mms(2 * i + 1, 1, d7, 4)
    emit_mult(IT - 1)
    hev = [hev[i] for i in range(IT)]

    cv.reset(base)
    dnb = cv.take(2 * 5632, U16)
    dnsc = cv.take(2 * 5632, F16)
    dndq, dnnib = cv.take2(2 * 5632, BF16, U16)
    dnsg = cv.take(5632, U16)
    y2 = cv.take(16 * 512, F32)
    y2q = cv.take(512, BF16)
    rmsrow = cv.take(512, F32)[0:1, :]
    rmsb = cv.take(512, BF16)[0:1, :]
    rbc = cv.take(512, F32)
    hslab = xn_arena[:, 0:IT * 512]
    LAST = 2 * IT - 1
    ph1_done = [dq7[LAST], mm_last[LAST], hev[IT - 1],
                silu_ev[(0, IT - 1, NTB - 1)], silu_ev[(1, IT - 1, NTB - 1)]]

    out_ev = []
    prev_r6 = [None]
    prev_r5 = [None]
    dmm = {}
    de6, de7 = {}, {}
    e1ev = {}
    sqev = {}
    hld = {}

    def dn_epi(s):
        tb, ot = divmod(s, 16)
        e1 = core.run("vector", lambda e, p=psum[ot % 2], o=y2[:, 512 * ot:512 * (ot + 1)]:
                      e.tensor_copy(o, p[:, 0:512]),
                      after=[dmm[s], out_ev[-16 + ot] if tb else None])
        e1ev[s] = e1
        e2 = core.run("scalar", lambda e, o=y2[:, 512 * ot:512 * (ot + 1)]:
                      e.activation(y2q[:], o, AF.Square),
                      after=[e1, sqev.get(s - 1)])
        sqev[s] = core.run("tensor", lambda e, f=(ot == 0), l=(ot == 15):
                           e.matmul(psum[7][0:1, 0:512], ones1b, y2q[:], start=f, stop=l),
                           after=[e2, prev_r6[0] if ot == 0 else None,
                                  silu_ev[(1, IT - 1, NTB - 1)] if s == 0 else None])

    def emit_rms(tbx):
        s0x = 16 * tbx
        r1 = core.run("vector", lambda e: e.tensor_scalar(rmsrow, psum[7][0:1, 0:512],
                                                          1.0 / 2048, 1e-6, ALU.mult, ALU.add),
                      after=[sqev[s0x + 15]])
        r2 = core.run("scalar", lambda e: e.activation(rmsrow, rmsrow, AF.Sqrt), after=[r1])
        r3 = core.run("vector", lambda e: e.reciprocal(rmsrow, rmsrow), after=[r2])
        r4 = core.run("vector", lambda e: e.tensor_copy(rmsb, rmsrow), after=[r3, prev_r5[0]])
        r5 = core.run("tensor", lambda e: e.matmul(psum[7][:, 0:512], ones1r, rmsb,
                                                   start=True, stop=True), after=[r4, r1])
        r6 = core.run("vector", lambda e: e.tensor_copy(rbc[:], psum[7][:, 0:512]), after=[r5])
        prev_r6[0] = r6
        prev_r5[0] = r5
        for ot in range(16):
            sc = core.run("vector", lambda e, o=y2[:, 512 * ot:512 * (ot + 1)]:
                          e.tensor_mul(o, o, rbc[:]), after=[r6, e1ev[s0x + ot]])
            out_ev.append(core.dma(resid_q,
                                   dram["x_resid"][128 * ot:128 * (ot + 1), 512 * tbx:512 * (tbx + 1)],
                                   y2[:, 512 * ot:512 * (ot + 1)], stream=f"res{ot % 4}",
                                   after=[sc], accum_op=ALU.add))

    for s in range(16 * NTB):
        tb, ot = divmod(s, 16)
        b = ot % 2
        if ot == 0:
            hsrc = dram["h_scr"].rearrange("p (i t) -> p i t", t=2048)[:, :, 512 * tb:512 * (tb + 1)]
            hld[tb] = core.dma("sync", hslab.rearrange("p (i t) -> p i t", t=512), hsrc,
                               stream="hsl",
                               after=(ph1_done if tb == 0 else []) + [dmm.get(s - 1)] + hev[-1:])
        sl = slice(b * 5632, (b + 1) * 5632)
        eb = core.dma("gpsimd", dnb[:, sl], dram["dn_bytes"][ot], stream=f"dnb{b}",
                      after=[de6.get(s - 2)] + (ph1_done if s < 2 else []))
        es = core.dma("sync", dnsc[:, sl], dram["dn_sc"][ot], stream=f"dns{b}",
                      after=[de7.get(s - 2)] + (ph1_done if s < 2 else []))
        d6, d7 = emit_dequant2(core, dnb[:, sl], dnsc[:, sl], dndq[:, sl],
                               dnnib[:, sl], dnsg[:], cst,
                               [eb, es, dmm.get(s - 2)])
        de6[s], de7[s] = d6, d7
        bank = psum[b]
        dep = [d7, hld[tb], e1ev.get(s - 2)]
        if s < 2:
            dep += [silu_ev[(0, IT - 1, b)]]
        for it in range(IT):
            ev = core.run("tensor",
                          lambda e, p=bank, l=dndq[:, b * 5632 + it * 128:b * 5632 + (it + 1) * 128],
                          r=hslab[:, it * 512:(it + 1) * 512], f=(it == 0), st=(it == IT - 1):
                          e.matmul(p[:, 0:512], l, r, start=f, stop=st),
                          after=dep, sig=(it == IT - 1))
            dep = []
        dmm[s] = ev
        if s:
            dn_epi(s - 1)
        if ot == 0 and tb:
            emit_rms(tb - 1)
    dn_epi(16 * NTB - 1)
    emit_rms(NTB - 1)
    return out_ev



def emit_gemm(core, tag, wt_dram, n_mt, n_kt, rhs_fn, epi, psum, banks, cv=None,
              wbuf=None, after=(), wq="gpsimd", ntb=NTB, mm_extra=None):
    kw = n_kt * 128
    if wbuf is None:
        wbuf = cv.take(3 * kw, BF16)
    mm_last = {}
    freeev = {}
    out = []
    wev = {}
    for mt in range(n_mt):
        b = mt % 3
        sl = slice(b * kw, (b + 1) * kw)
        wev[mt] = core.dma(wq, wbuf[:, sl], wt_dram[mt], stream=f"{tag}w{b}",
                           after=[mm_last.get(mt - 3)] + (list(after) if mt < 3 else []))
        for tb in range(ntb):
            slot = (mt * ntb + tb) % len(banks)
            bank = psum[banks[slot]]
            dep = [wev[mt], freeev.get(slot)]
            for ct in range(n_kt):
                ev = core.run("tensor",
                              lambda e, p=bank, l=wbuf[:, b * kw + ct * 128:b * kw + (ct + 1) * 128],
                              r=rhs_fn(ct, tb), f=(ct == 0), s=(ct == n_kt - 1):
                              e.matmul(p[:, 0:512], l, r, start=f, stop=s),
                              after=dep, sig=(ct == n_kt - 1))
                dep = []
            if mm_extra is not None:
                ev = mm_extra(mt, tb, bank, ev)
            mm_last[mt] = ev
            fe = epi(mt, tb, bank, ev)
            freeev[slot] = fe
            out.append(fe)
    return out



def emit_l1_attn(core, tag, xn_arena, cst, psum, dram, cv, after=(), ln_guard=()):
    base = cv.off
    av_arena = cv.take(16 * T, BF16)
    sub = cv.off
    ones1b, ones1r = cst["ones1b"], cst["ones1r"]
    cosb, sinb = cst["cosb"], cst["sinb"]

    qsb = cv.take(T, BF16)
    rsb = cv.take(T, BF16)
    t1 = cv.take(4 * 512, F32)
    bq_sb = cv.take(32, F32)
    bl = core.dma("sync", bq_sb, dram["bqk"], stream="cst", after=list(after))
    st_ev = {}
    rd_ev = {}

    def rhs(ct, tb):
        return xn_arena[:, T * ct + 512 * tb: T * ct + 512 * tb + 512]

    rope_mm = {}

    def epi_qk(mt, tb, bank, ev):
        sl = slice(512 * tb, 512 * (tb + 1))
        e1 = core.run("vector",
                      lambda e, o=qsb[:, sl], p=bank:
                      e.tensor_scalar(o, p[:, 0:512], bq_sb[:, mt:mt + 1], None, ALU.add),
                      after=[ev, bl, rope_mm.get((mt - 1, tb)), rd_ev.get((mt - 1, tb))])
        b2 = psum[4 + tb % 2]
        prev_b2 = (mt, tb - 2) if tb >= 2 else (mt - 1, tb + 2)
        e2 = core.run("tensor",
                      lambda e, p=b2, r=qsb[:, sl]:
                      e.matmul(p[:, 0:512], cst["rt"], r, start=True, stop=True),
                      after=[e1, rd_ev.get(prev_b2)])
        rope_mm[(mt, tb)] = e2
        e3 = core.run("vector",
                      lambda e, p=b2, o=t1[:, sl]: e.tensor_mul(o, p[:, 0:512], sinb[:, sl]),
                      after=[e2])
        e4 = core.run("vector",
                      lambda e, o=rsb[:, sl]: e.tensor_mul(o, qsb[:, sl], cosb[:, sl]),
                      after=[e1, st_ev.get((mt - 1, tb))])
        e5 = core.run("vector",
                      lambda e, o=rsb[:, sl]: e.tensor_add(o, o, t1[:, sl]),
                      after=[e3, e4])
        rd_ev[(mt, tb)] = e5
        dst = dram["qr"] if mt < 16 else dram["kr"]
        st_ev[(mt, tb)] = core.dma("sync", dst[mt % 16][:, sl], rsb[:, sl],
                                   stream=f"qk{tb % 2}", after=[e5])
        return e1

    ev_qk = emit_gemm(core, "qkv", dram["wqkv_slab"], 32, NCT, rhs, epi_qk,
                      psum, (0, 1, 2, 3), cv=cv, after=after)
    phA = [st_ev[(31, tb)] for tb in range(NTB)] + [rd_ev[(31, tb)] for tb in range(NTB)]

    cv.reset(sub)
    wvb = cv.take(2 * 8192, BF16)
    vts = cv.take(2 * 512, BF16)
    vstore = {}
    mmv = {}
    for ib in range(4):
        b = ib % 2
        sl = slice(b * 8192, (b + 1) * 8192)
        wvld = core.dma("gpsimd", wvb[:, sl], dram["wv_rhs"][ib],
                        stream=f"wv{b}", after=[mmv.get((ib - 2, 15))] + (phA if ib < 2 else []))
        for tc in range(16):
            bank = psum[6 + tc % 2]
            dep = [wvld, vstore.get((ib, tc - 2)), phA[0] if ib == 0 and tc < 2 else None,
                   vstore.get((ib - 1, 14 + tc)) if tc < 2 else None]
            for ct in range(NCT):
                ev = core.run("tensor",
                              lambda e, p=bank,
                              l=xn_arena[:, T * ct + 128 * tc:T * ct + 128 * tc + 128],
                              r=wvb[:, b * 8192 + 512 * ct:b * 8192 + 512 * (ct + 1)],
                              f=(ct == 0), s=(ct == NCT - 1):
                              e.matmul(p[:, 0:512], l, r, start=f, stop=s),
                              after=dep, sig=(ct == NCT - 1))
                dep = []
            mmv[(ib, tc)] = ev
            vb = vts[:, 512 * (tc % 2):512 * (tc % 2 + 1)]
            ve = core.run("vector", lambda e, p=bank, o=vb: e.tensor_copy(o, p[:, 0:512]),
                          after=[ev, vstore.get((ib, tc - 2)),
                                 vstore.get((ib - 1, 14 + tc)) if tc < 2 else None])
            vstore[(ib, tc)] = core.dma(
                "sync",
                dram["vtm"].rearrange("p (tc i) -> p tc i", i=2048)[:, tc, 512 * ib:512 * (ib + 1)],
                vb, stream=f"vt{tc % 2}", after=[ve])
    phB = [vstore[(3, tc)] for tc in range(16)] + [mmv[(3, 15)], mmv[(3, 14)]]

    cv.reset(sub)
    qh = cv.take(2 * T, BF16)
    kh = cv.take(2 * T, BF16)
    vh = cv.take(2 * T, BF16)
    expb = cv.take(2 * 512, BF16)
    zrow = cv.take(512, F32)
    zb16 = cv.take(512, BF16)
    zbc = cv.take(512, F32)
    hld = {}
    mm_h = {}
    avn_ev = {}
    exp_ev = {}
    zmm = {}
    avmm = {}
    zevac = {}
    zbc_ev = {}
    zmm4 = {}
    for h in range(16):
        b = h % 2
        sl = slice(b * T, (b + 1) * T)
        pdep = [mm_h.get(h - 2)] + (phA + phB if h < 2 else [])
        lq = core.dma("sync", qh[:, sl], dram["qr"][h], stream=f"hq{b}", after=pdep)
        lk = core.dma("sync", kh[:, sl], dram["kr"][h], stream=f"hk{b}", after=pdep)
        lv = core.dma("gpsimd",
                      vh[:, sl].rearrange("p (tc j) -> p tc j", j=128),
                      dram["vtm"].rearrange("p (tc i) -> p tc i", i=2048)[:, :, 128 * h:128 * (h + 1)],
                      stream=f"hv{b}", after=pdep)
        last_mm = None
        for qc in range(NTB):
            qsl = slice(512 * qc, 512 * (qc + 1))
            for kb in range(4 * qc + 4):
                sbank = psum[kb % 2]
                smm = core.run("tensor",
                               lambda e, p=sbank, l=kh[:, sl][:, 128 * kb:128 * (kb + 1)],
                               r=qh[:, sl][:, qsl]:
                               e.matmul(p[:, 0:512], l, r, start=True, stop=True),
                               after=[lq, lk, exp_ev.get((h, qc, kb - 2), exp_ev.get((h, qc - 1, 4 * qc - 2 + kb)) if qc else None)])
                eb = expb[:, 512 * (kb % 2):512 * (kb % 2 + 1)]
                ee = core.run("scalar",
                              lambda e, p=sbank, o=eb:
                              e.activation(o, p[:, 0:512], AF.Exp, bias=cst["neg20"],
                                           scale=0.08838834764831845),
                              after=[smm, zmm.get((h, qc, kb - 2)), avmm.get((h, qc, kb - 2)),
                                     zmm.get((h, qc - 1, 4 * qc - 2 + kb)) if qc else None,
                                     avmm.get((h, qc - 1, 4 * qc - 2 + kb)) if qc else None,
                                     zmm.get((h - 1, 3, 14 + kb)) if qc == 0 and kb < 2 else None,
                                     avmm.get((h - 1, 3, 14 + kb)) if qc == 0 and kb < 2 else None])
                if kb >= 4 * qc:
                    ee = core.run("vector",
                                  lambda e, o=eb, m=cst[f"mask{kb - 4 * qc}"]:
                                  e.tensor_mul(o, o, m),
                                  after=[ee])
                exp_ev[(h, qc, kb)] = ee
                zmm[(h, qc, kb)] = core.run("tensor",
                                            lambda e, o=eb, f=(kb == 0), s=(kb == 4 * qc + 3):
                                            e.matmul(psum[2][0:1, 0:512], ones1b, o, start=f, stop=s),
                                            after=[ee, (zevac.get((h, qc - 1)) or zevac.get((h - 1, 3))) if kb == 0 else None])
                avmm[(h, qc, kb)] = core.run("tensor",
                                             lambda e, o=eb, p=psum[3 + qc % 2],
                                             l=vh[:, sl][:, 128 * kb:128 * (kb + 1)],
                                             f=(kb == 0), s=(kb == 4 * qc + 3):
                                             e.matmul(p[:, 0:512], l, o, start=f, stop=s),
                                             after=[ee, lv, avn_ev.get((h, qc - 2)) if kb == 0 else None,
                                                    avn_ev.get((h - 1, qc + 2)) if kb == 0 and qc < 2 else None])
            z1 = core.run("vector", lambda e: e.tensor_copy(zrow[0:1, 0:512], psum[2][0:1, 0:512]),
                          after=[zmm[(h, qc, 4 * qc + 3)]])
            zevac[(h, qc)] = z1
            z2 = core.run("vector", lambda e: e.reciprocal(zrow[0:1, 0:512], zrow[0:1, 0:512]),
                          after=[z1])
            z3 = core.run("vector", lambda e: e.tensor_copy(zb16[0:1, 0:512], zrow[0:1, 0:512]),
                          after=[z2, zmm4.get((h, qc - 1)) or zmm4.get((h - 1, 3))])
            z4 = core.run("tensor", lambda e: e.matmul(psum[5][:, 0:512], ones1r, zb16[0:1, 0:512],
                                                       start=True, stop=True),
                          after=[z3, zbc_ev.get((h, qc - 1)) or zbc_ev.get((h - 1, 3))])
            z5 = core.run("vector", lambda e: e.tensor_copy(zbc[:, 0:512], psum[5][:, 0:512]),
                          after=[z4, avn_ev.get((h, qc - 1)) or avn_ev.get((h - 1, 3))])
            zbc_ev[(h, qc)] = z5
            zmm4[(h, qc)] = z4
            avn = core.run("vector",
                           lambda e, o=av_arena[:, T * h + 512 * qc:T * h + 512 * (qc + 1)],
                           p=psum[3 + qc % 2]:
                           e.tensor_mul(o, p[:, 0:512], zbc[:, 0:512]),
                           after=[z5, avmm[(h, qc, 4 * qc + 3)]])
            avn_ev[(h, qc)] = avn
            last_mm = avmm[(h, qc, 4 * qc + 3)]
        mm_h[h] = last_mm
    phC = [avn_ev[(15, qc)] for qc in range(NTB)] + [avn_ev[(14, qc)] for qc in range(NTB)]

    cv.reset(sub)
    osb = cv.take(4 * 512, F32)
    cb_sb = cv.take(16, F32)
    bvb = cv.take(16, BF16)
    bde = cv.take(16, F32)
    l1 = core.dma("sync", cb_sb, dram["bv"], stream="cst", after=phC[:1])
    l2 = core.dma("sync", bde, dram["bdense"], stream="cst", after=phC[:1])
    bc = core.run("vector", lambda e: e.tensor_copy(bvb, cb_sb), after=[l1])
    cbev = {}

    def mk_extra(mt, tb, bank, ev):
        if tb != 0:
            return ev
        dep = [bc, cbev.get(mt - 1)]
        kw = NCT * 128
        b = mt % 3
        last = None
        for ct in range(NCT):
            last = core.run("tensor",
                            lambda e, l=_gw["buf"][:, b * kw + ct * 128:b * kw + (ct + 1) * 128],
                            r=bvb[:, ct:ct + 1], f=(ct == 0), s=(ct == NCT - 1):
                            e.matmul(psum[6][:, 0:1], l, r, start=f, stop=s),
                            after=dep if ct == 0 else [], sig=(ct == NCT - 1))
        e1 = core.run("vector",
                      lambda e: e.scalar_tensor_tensor(cb_sb[:, mt:mt + 1], bde[:, mt:mt + 1],
                                                       1.0, psum[6][:, 0:1], ALU.mult, ALU.add),
                      after=[last, l2])
        cbev[mt] = e1
        return ev

    _gw = {}
    res_ev = []

    def epi_dense(mt, tb, bank, ev):
        e1 = core.run("vector",
                      lambda e, o=osb[:, 512 * tb:512 * (tb + 1)], p=bank:
                      e.tensor_scalar(o, p[:, 0:512], cb_sb[:, mt:mt + 1], None, ALU.add),
                      after=[ev, cbev[mt], res_ev[-4] if len(res_ev) >= 4 else None])
        st = core.dma("gpsimd", dram["x_resid"][128 * mt:128 * (mt + 1), 512 * tb:512 * (tb + 1)],
                      osb[:, 512 * tb:512 * (tb + 1)], stream=f"res{tb}",
                      after=[e1] + list(ln_guard), accum_op=ALU.add)
        res_ev.append(st)
        return e1

    def rhs_av(ct, tb):
        return av_arena[:, T * ct + 512 * tb: T * ct + 512 * tb + 512]

    wbufD = cv.take(3 * NCT * 128, BF16)
    _gw["buf"] = wbufD
    emit_gemm(core, "dens", dram["wd_slab"], 16, NCT, rhs_av, epi_dense,
              psum, (0, 1, 2, 3), wbuf=wbufD, after=phC, mm_extra=mk_extra)
    assert cv.off - base <= 70 * 1024 + 16 * T * 2
    return res_ev



def emit_moe(core, tag, xn_arena, cst, psum, dram, cv, after=(), ln_guard=()):
    base = cv.off
    ones1b, ones1r, identb = cst["ones1b"], cst["ones1r"], cst["identb"]

    glog = cv.take(2048, F32)
    gat = cv.take(16 * 16, F32)
    gatb = cv.take(16, BF16)
    gtmp = cv.take(72, F32)
    cwT = cv.take(2048, BF16)
    gw_sb = cv.take(16, BF16)
    gmm = {}
    gwld = {}
    for ct in range(NCT):
        gwld[ct] = core.dma("gpsimd", gw_sb, dram["gate_slab"][ct], stream="gw",
                            after=list(after) if ct == 0 else [gmm.get(3)])
        for tb in range(NTB):
            f, l = ct == 0, ct == NCT - 1
            gmm[tb] = core.run("tensor",
                               lambda e, p=psum[tb], l2=gw_sb,
                               r=xn_arena[:, T * ct + 512 * tb:T * ct + 512 * (tb + 1)],
                               f=f, s=l:
                               e.matmul(p[0:16, 0:512], l2, r, start=f, stop=s),
                               after=[gwld[ct]])
    eev = {}
    for tb in range(NTB):
        eev[tb] = core.run("scalar",
                           lambda e, p=psum[tb], o=glog[0:16, 512 * tb:512 * (tb + 1)]:
                           e.activation(o, p[0:16, 0:512], AF.Exp, bias=cst["neg8"][0:16, 0:1]),
                           after=[gmm[tb]])
    tev = {}
    ctev = {}
    cwe = {}
    for ch in range(16):
        pb = psum[4]
        pe = core.run("tensor",
                      lambda e, i=glog[0:16, 128 * ch:128 * (ch + 1)]:
                      e.transpose(pb[0:128, 0:16], i, cst["identf"][0:16, 0:16]),
                      after=[eev[ch // 4], tev.get(ch - 1)])
        tev[ch] = core.run("vector",
                           lambda e, o=gat[:, 16 * ch:16 * (ch + 1)], p=pb[0:128, 0:16]:
                           e.tensor_copy(o, p), after=[pe])
        g = gat[:, 16 * ch:16 * (ch + 1)]
        g3 = g.rearrange("p (a b) -> p a b", b=4)
        m01, n01 = gtmp[:, 0:4], gtmp[:, 4:8]
        m23, n23 = gtmp[:, 8:12], gtmp[:, 12:16]
        gm = gtmp[:, 16:20]
        kp = gtmp[:, 20:24]
        sec, t0 = gtmp[:, 24:25], gtmp[:, 25:26]
        mx = gtmp[:, 26:27]
        w16 = gtmp[:, 32:48]
        eqb = gtmp[:, 48:64]
        dep0 = [tev[ch], ctev.get(ch - 1)]
        e1 = core.run("vector", lambda e, a=g3: e.tensor_tensor(m01, a[:, :, 0], a[:, :, 1], ALU.max), after=dep0)
        e2 = core.run("vector", lambda e, a=g3: e.tensor_tensor(n01, a[:, :, 0], a[:, :, 1], ALU.min), after=dep0)
        e3 = core.run("vector", lambda e, a=g3: e.tensor_tensor(m23, a[:, :, 2], a[:, :, 3], ALU.max), after=dep0)
        e4 = core.run("vector", lambda e, a=g3: e.tensor_tensor(n23, a[:, :, 2], a[:, :, 3], ALU.min), after=dep0)
        e5 = core.run("vector", lambda e: e.tensor_tensor(gm, m01, m23, ALU.max), after=[e1, e3])
        f1 = core.run("vector", lambda e: e.tensor_tensor(gtmp[:, 27:28], gm[:, 0:1], gm[:, 1:2], ALU.max), after=[e5])
        f2 = core.run("vector", lambda e: e.tensor_tensor(gtmp[:, 28:29], gm[:, 0:1], gm[:, 1:2], ALU.min), after=[e5])
        f3 = core.run("vector", lambda e: e.tensor_tensor(gtmp[:, 29:30], gm[:, 2:3], gm[:, 3:4], ALU.max), after=[e5])
        f4 = core.run("vector", lambda e: e.tensor_tensor(gtmp[:, 30:31], gm[:, 2:3], gm[:, 3:4], ALU.min), after=[e5])
        f5 = core.run("vector", lambda e: e.tensor_tensor(t0, gtmp[:, 27:28], gtmp[:, 29:30], ALU.min), after=[f1, f3])
        f6 = core.run("vector", lambda e: e.tensor_tensor(sec, gtmp[:, 28:29], gtmp[:, 30:31], ALU.max), after=[f2, f4])
        f7 = core.run("vector", lambda e: e.tensor_tensor(sec, sec, t0, ALU.max), after=[f5, f6])
        e6 = core.run("vector", lambda e: e.tensor_scalar(kp, gm, sec, None, ALU.is_ge), after=[f7])
        e7 = core.run("vector",
                      lambda e, a=g3: e.tensor_tensor(
                          a, a, kp.rearrange("p (a o) -> p a o", o=1).broadcast_to([128, 4, 4]),
                          ALU.mult),
                      after=[e6])
        cp = core.run("vector", lambda e, g=g: e.tensor_copy(w16, g), after=[e7])
        prev = cp
        for it in range(3):
            r1 = core.run("vector", lambda e: e.reduce_max(mx, w16, axis=AX.X), after=[prev])
            r2 = core.run("vector", lambda e: e.tensor_scalar(eqb, w16, mx, None, ALU.is_lt), after=[r1])
            prev = core.run("vector", lambda e: e.tensor_mul(w16, w16, eqb), after=[r2])
        r4 = core.run("vector", lambda e: e.reduce_max(mx, w16, axis=AX.X), after=[prev])
        k1 = core.run("vector", lambda e, g=g: e.tensor_scalar(eqb, g, mx, None, ALU.is_ge), after=[r4])
        k2 = core.run("vector", lambda e, g=g: e.tensor_mul(g, g, eqb), after=[k1])
        k3 = core.run("vector", lambda e, g=g: e.reduce_sum(mx, g, axis=AX.X), after=[k2])
        k4 = core.run("vector", lambda e: e.tensor_scalar(mx, mx, 1e-20, None, ALU.add), after=[k3])
        k5 = core.run("vector", lambda e: e.reciprocal(mx, mx), after=[k4])
        k6 = core.run("vector", lambda e, g=g: e.tensor_scalar(gatb, g, mx, None, ALU.mult), after=[k5, ctev.get(ch - 1)])
        pe2 = core.run("tensor",
                       lambda e: e.transpose(psum[5].bitcast(BF16)[0:16, 0:128], gatb, identb),
                       after=[k6, ctev.get(ch - 1)])
        ctev[ch] = core.run("vector",
                            lambda e, o=cwT[0:16, 128 * ch:128 * (ch + 1)],
                            p=psum[5].bitcast(BF16)[0:16, 0:128]:
                            e.tensor_copy(o, p), after=[pe2])
        cwe[ch] = k6
    cwf = core.dram_tmp(f"cwf{tag}", (16, 2048), BF16)
    fl1 = core.dma("sync", cwf[:, :], cwT[0:16, :], stream="cwf",
                   after=[ctev[15]])
    ph_gate = [fl1, ctev[15]]

    cv.reset(base)
    hgrp = cv.take(33 * 1024, BF16)
    hsil = cv.take(1024, BF16)
    wup = cv.take(3 * 2048, BF16)
    wdn = cv.take(3 * 4224, BF16)
    otmp = cv.take(2 * 512, F32)
    cwrow = cv.take(1024, BF16)[0:1, :]
    cwb = cv.take(1024, BF16)
    groups = [[0, 1, 2], [3, 4, 5], [6, 7, 8], [9, 10, 11], [12, 13, 14], [15, "sh"]]
    res_ev = []
    evev = {}
    upmm = {}
    dnmm = {}
    silev = {}
    cw_c1 = {"prev": None, "mm0": None, "mm1": None, "ev0": None, "ev1": None}
    bank_free = {}
    hsil_free = {}
    otmp_free = {}
    hrd = {}
    last_sil = {}
    for tbh in range(2):
        for gi, grp in enumerate(groups):
            hoff = 0
            guards = [hrd.get((tbh, gi - 1)), hrd.get((tbh - 1, len(groups) - 1)) if gi == 0 else None]
            for e_ in grp:
                sh = e_ == "sh"
                n_it = 22 if sh else 11
                slab = dram["sup_slab"] if sh else dram["up_slab"][e_]
                if not sh:
                    lcw = core.dma("sync", cwrow,
                                   cwf[e_:e_ + 1, 1024 * tbh:1024 * (tbh + 1)],
                                   stream="cw", after=ph_gate + [cw_c1["prev"]])
                    c2 = core.run("tensor",
                                  lambda e: e.matmul(psum[6][:, 0:512], ones1r, cwrow[0:1, 0:512],
                                                     start=True, stop=True),
                                  after=[lcw, cw_c1["ev0"]])
                    c3 = core.run("tensor",
                                  lambda e: e.matmul(psum[7][:, 0:512], ones1r, cwrow[0:1, 512:1024],
                                                     start=True, stop=True),
                                  after=[lcw, cw_c1["ev1"]])
                    c4 = core.run("vector", lambda e: e.tensor_copy(cwb[:, 0:512], psum[6][:, 0:512]),
                                  after=[c2, last_sil.get("cw")])
                    c5 = core.run("vector", lambda e: e.tensor_copy(cwb[:, 512:1024], psum[7][:, 0:512]),
                                  after=[c3, last_sil.get("cw")])
                    cw_c1["prev"] = c3
                    cw_c1["ev0"], cw_c1["ev1"] = c4, c5
                    cwev = [c4, c5]
                else:
                    cwev = []
                for i in range(n_it):
                    for half, mt in ((0, i), (1, n_it + i)):
                        idx = 2 * i + half
                        b = idx % 3
                        sl = slice(b * 2048, (b + 1) * 2048)
                        wev = core.dma("gpsimd", wup[:, sl], slab[mt], stream=f"up{b}",
                                       after=[upmm.get((tbh, gi, e_, idx - 3)),
                                              upmm.get("last") if idx < 3 else None] +
                                       (list(after) if (tbh, gi, idx) < (0, 0, 3) else []))
                        for sub in range(2):
                            bank = psum[2 * half + sub]
                            dep = ([wev, bank_free.get(2 * half + sub)] +
                                   (guards if idx < 2 else []))
                            for ct in range(NCT):
                                ev = core.run("tensor",
                                              lambda e, p=bank,
                                              l=wup[:, b * 2048 + 128 * ct:b * 2048 + 128 * (ct + 1)],
                                              r=xn_arena[:, T * ct + 1024 * tbh + 512 * sub:
                                                         T * ct + 1024 * tbh + 512 * (sub + 1)],
                                              f=(ct == 0), s=(ct == NCT - 1):
                                              e.matmul(p[:, 0:512], l, r, start=f, stop=s),
                                              after=dep, sig=(ct == NCT - 1))
                                dep = []
                            upmm[(tbh, gi, e_, idx)] = ev
                            ssl = slice(512 * sub, 512 * (sub + 1))
                            if half == 0:
                                se = core.run(
                                    "scalar", lambda e, p=bank, o=hsil[:, ssl]:
                                    e.activation(o, p[:, 0:512], AF.Silu),
                                    after=[ev, hsil_free.get(sub)])
                                silev[(tbh, gi, e_, idx, sub)] = se
                                bank_free[2 * half + sub] = se
                            else:
                                hv = hgrp[:, 1024 * (hoff + i) + 512 * sub:
                                          1024 * (hoff + i) + 512 * (sub + 1)]
                                m1 = core.run("vector", lambda e, p=bank, o=hv, s2=hsil[:, ssl]:
                                              e.tensor_mul(o, s2, p[:, 0:512]),
                                              after=[ev, silev[(tbh, gi, e_, idx - 1, sub)]] +
                                              ([hrd.get((tbh, gi - 1)), hrd.get((tbh - 1, len(groups) - 1))] if idx < 2 else []))
                                bank_free[2 * half + sub] = m1
                                if not sh:
                                    m1 = core.run("vector", lambda e, o=hv, c=cwb[:, ssl]:
                                                  e.tensor_mul(o, o, c),
                                                  after=[m1] + cwev)
                                silev[(tbh, gi, e_, idx, sub)] = m1
                                bank_free[2 * half + sub] = m1
                                hsil_free[sub] = m1
                upmm["last"] = upmm[(tbh, gi, e_, 2 * n_it - 1)]
                last_sil["cw"] = silev[(tbh, gi, e_, 2 * n_it - 1, 1)]
                hoff += n_it
            grp_h = [silev[(tbh, gi, e_2, (43 if e_2 == "sh" else 21), s2)]
                     for e_2 in grp for s2 in range(2)]
            dnld = {}
            last_dn = None
            for ot in range(16):
                b2 = ot % 3
                woffs = {}
                wo = 0
                for e_ in grp:
                    sh = e_ == "sh"
                    wc = 2816 if sh else 1408
                    dslab = dram["sdn_slab"] if sh else dram["dn_slab"][e_]
                    woffs[e_] = wo
                    dnld[(e_, ot)] = core.dma("gpsimd",
                                              wdn[:, b2 * 4224 + wo:b2 * 4224 + wo + wc],
                                              dslab[ot], stream=f"dn{b2}",
                                              after=[dnmm.get((tbh, gi, ot - 3, 1))] +
                                              (grp_h if ot < 3 else []))
                    wo += wc
                for sub in range(2):
                    bank = psum[4 + sub]
                    first = True
                    dep = [evev.get((tbh, gi, ot - 1, sub)),
                           evev.get((tbh, gi - 1, 15, sub)) if ot == 0 else None,
                           evev.get((tbh - 1, len(groups) - 1, 15, sub)) if ot == 0 and gi == 0 else None] + grp_h
                    hoff2 = 0
                    ev = None
                    for e_ in grp:
                        sh = e_ == "sh"
                        n_it = 22 if sh else 11
                        wo = woffs[e_]
                        for it in range(n_it):
                            lv = wdn[:, b2 * 4224 + wo + 128 * it:b2 * 4224 + wo + 128 * (it + 1)]
                            ev = core.run("tensor",
                                          lambda e, p=bank, l=lv,
                                          r=hgrp[:, 1024 * (hoff2 + it) + 512 * sub:
                                                 1024 * (hoff2 + it) + 512 * (sub + 1)],
                                          f=first, s=(e_ is grp[-1] and it == n_it - 1):
                                          e.matmul(p[:, 0:512], l, r, start=f, stop=s),
                                          after=[dnld[(e_, ot)]] + (dep if first else []),
                                          sig=(e_ is grp[-1] and it == n_it - 1))
                            first = False
                        hoff2 += n_it
                    dnmm[(tbh, gi, ot, sub)] = ev
                    last_dn = ev
                    ov = core.run("vector",
                                  lambda e, p=bank, o=otmp[:, 512 * sub:512 * (sub + 1)]:
                                  e.tensor_copy(o, p[:, 0:512]),
                                  after=[ev, otmp_free.get(sub)])
                    evev[(tbh, gi, ot, sub)] = ov
                    res = core.dma("gpsimd",
                                   dram["x_resid"][128 * ot:128 * (ot + 1),
                                                   1024 * tbh + 512 * sub:
                                                   1024 * tbh + 512 * (sub + 1)],
                                   otmp[:, 512 * sub:512 * (sub + 1)],
                                   stream=f"mres{(2 * ot + sub) % 4}",
                                   after=[ov] + list(ln_guard), accum_op=ALU.add)
                    otmp_free[sub] = res
                    res_ev.append(res)
            hrd[(tbh, gi)] = last_dn
    return res_ev



def emit_l0_mixes(core, tag, xn_arena, cst, psum, dram, cv, after=()):
    base = cv.off
    tm6 = cv.take(96, F32)
    td = cv.take(16, F32)
    w1s = cv.take(16 * 160, BF16)
    w2s = cv.take(5 * 16 * 128, BF16)
    dw1s = cv.take(16 * 64, BF16)
    dw2s = cv.take(16 * 128, BF16)
    lds = []
    lds.append(core.dma("sync", tm6, dram["tm6"], stream="cst", after=list(after)))
    lds.append(core.dma("sync", td, dram["td"], stream="cst"))
    lds.append(core.dma("gpsimd", w1s, dram["w1m"], stream="wsm", after=list(after)))
    lds.append(core.dma("gpsimd", w2s[0:32, :], dram["w2m"], stream="wsm"))
    lds.append(core.dma("gpsimd", dw1s, dram["dw1"], stream="wsm"))
    lds.append(core.dma("gpsimd", dw2s[0:64, :], dram["dw2"], stream="wsm"))

    xxs = cv.take(16 * 512, BF16)
    xxx = cv.take(16 * 512, BF16)
    ajs = cv.take(5 * 512, BF16)
    a2 = cv.take(512, BF16)
    t1 = cv.take(2 * 512, F32)
    mslab = cv.take(16 * 512, BF16)
    wout = cv.take(16 * 512, BF16)
    phA_end = cv.off

    mixst = {}
    wst = {}
    amm = {}
    atan = {}
    xxev = {}
    xxxev = {}
    evjc = {}
    dtanev = {}
    t1_free = {}
    for tb in range(NTB):
        t0 = 512 * tb
        for ct in range(NCT):
            av = xn_arena[:, T * ct + t0:T * ct + t0 + 512]
            xv = xxs[:, 512 * ct:512 * (ct + 1)]
            dep = [mixst.get((tb - 1, 4)), lds[0]]
            if tb == 0:
                e0 = core.run("vector", lambda e, o=xv[:, 0:1], a=xn_arena[:, T * ct:T * ct + 1]:
                              e.tensor_scalar(o, a, -1.0, None, ALU.mult),
                              after=dep + [evjc.get((3, ct, "g"), None)])
                e1 = core.run("vector",
                              lambda e, o=xv[:, 1:512], a=xn_arena[:, T * ct:T * ct + 511],
                              b=xn_arena[:, T * ct + 1:T * ct + 512]:
                              e.tensor_sub(o, a, b), after=[e0])
            else:
                e1 = core.run("vector",
                              lambda e, o=xv, a=xn_arena[:, T * ct + t0 - 1:T * ct + t0 + 511],
                              b=av: e.tensor_sub(o, a, b),
                              after=dep + [evjc.get((tb - 1, ct, "g"), None)])
            xxev[ct] = e1
            xxxev[ct] = core.run("vector",
                                 lambda e, o=xxx[:, 512 * ct:512 * (ct + 1)], x=xv, a=av,
                                 s=tm6[:, 6 * ct:6 * ct + 1]:
                                 e.scalar_tensor_tensor(o, x, s, a, ALU.mult, ALU.add),
                                 after=[e1, atan.get((tb - 1, 4))])
        for j in range(5):
            for ct in range(NCT):
                f, l = ct == 0, ct == NCT - 1
                amm[j] = core.run("tensor",
                                  lambda e, p=psum[j], w=w1s[:, ct * 160 + 32 * j:ct * 160 + 32 * (j + 1)],
                                  r=xxx[:, 512 * ct:512 * (ct + 1)], f=f, s=l:
                                  e.matmul(p[0:32, 0:512], w, r, start=f, stop=s),
                                  after=[xxxev[ct], lds[2], atan.get((tb - 1, j)),
                                         dtanev.get(tb - 1) if j == 0 else None,
                                         evjc.get((tb - 1, 14, "we")) if j == 1 else None,
                                         evjc.get((tb - 1, 15, "we")) if j == 2 else None])
            atan[(tb, j)] = core.run("scalar",
                                     lambda e, p=psum[j], o=ajs[0:32, 512 * j:512 * (j + 1)]:
                                     e.activation(o, p[0:32, 0:512], AF.Tanh),
                                     after=[amm[j], evjc.get((tb - 1, 15, "m1s")) if j == 0 else None])
        for j in range(5):
            for ct in range(NCT):
                bank = psum[5 + ct % 3]
                mm = core.run("tensor",
                              lambda e, p=bank, w=w2s[0:32, (j * 16 + ct) * 128:(j * 16 + ct + 1) * 128],
                              r=ajs[0:32, 512 * j:512 * (j + 1)]:
                              e.matmul(p[:, 0:512], w, r, start=True, stop=True),
                              after=[atan[(tb, j)], lds[3],
                                     evjc.get((j, ct - 3, "e1s")) if ct >= 3 else
                                     (evjc.get((j - 1, [15, 13, 14][ct], "e1s")) if j > 0 else
                                      evjc.get((4, [15, 13, 14][ct], "e1s")))])
                e1s = core.run("scalar",
                               lambda e, p=bank, o=t1[:, 512 * (ct % 2):512 * (ct % 2 + 1)],
                               b=tm6[:, 6 * ct + j + 1:6 * ct + j + 2]:
                               e.activation(o, p[:, 0:512], AF.Identity, bias=b),
                               after=[mm, lds[0], t1_free.get(ct % 2)])
                e2g = core.run("vector",
                               lambda e, o=mslab[:, 512 * ct:512 * (ct + 1)],
                               a=xxs[:, 512 * ct:512 * (ct + 1)],
                               b=t1[:, 512 * (ct % 2):512 * (ct % 2 + 1)]:
                               e.tensor_mul(o, a, b),
                               after=[e1s, mixst.get((tb, j - 1)) if j > 0 else mixst.get((tb - 1, 4)),
                                      evjc.get(("dmm", tb)) if j == 1 else None])
                e3v = core.run("vector",
                               lambda e, o=mslab[:, 512 * ct:512 * (ct + 1)],
                               a=xn_arena[:, T * ct + t0:T * ct + t0 + 512]:
                               e.tensor_add(o, o, a), after=[e2g])
                evjc[(j, ct, "e1s")] = e1s
                evjc[(j, ct, "e2g")] = e2g
                t1_free[ct % 2] = e2g
                evjc[(j, ct, "e3v")] = e3v
            mixst[(tb, j)] = core.dma("sync",
                                      dram["mix"][j].rearrange("p (c t) -> p c t", t=T)[:, :, t0:t0 + 512],
                                      mslab[:].rearrange("p (c t) -> p c t", t=512),
                                      stream=f"mx{j % 2}",
                                      after=[evjc[(j, c2, "e3v")] for c2 in range(NCT)])
            if j == 0:
                dmm = None
                for ct in range(NCT):
                    f, l = ct == 0, ct == NCT - 1
                    dmm = core.run("tensor",
                                   lambda e, p=psum[0], w=dw1s[:, ct * 64:(ct + 1) * 64],
                                   r=mslab[:, 512 * ct:512 * (ct + 1)], f=f, s=l:
                                   e.matmul(p[0:64, 0:512], w, r, start=f, stop=s),
                                   after=[evjc[(0, ct, "e3v")], lds[4]])
                dtan = core.run("scalar", lambda e: e.activation(a2[0:64, :], psum[0][0:64, 0:512], AF.Tanh),
                                after=[dmm])
                dtanev[tb] = dtan
                evjc[("dmm", tb)] = dmm
                for mt in range(NCT):
                    bank = psum[1 + mt % 2]
                    wm = core.run("tensor",
                                  lambda e, p=bank, w=dw2s[0:64, 128 * mt:128 * (mt + 1)]:
                                  e.matmul(p[:, 0:512], w, a2[0:64, :], start=True, stop=True),
                                  after=[dtan, lds[5], evjc.get((tb, mt - 2, "we"))])
                    we = core.run("scalar",
                                  lambda e, p=bank, o=wout[:, 512 * mt:512 * (mt + 1)],
                                  b=td[:, mt:mt + 1]:
                                  e.activation(o, p[:, 0:512], AF.Identity, bias=b),
                                  after=[wm, lds[1], wst.get(tb - 1)])
                    evjc[(tb, mt, "we")] = we
                wst[tb] = core.dma("sync",
                                   dram["wT"].rearrange("p (c t) -> p c t", t=T)[:, :, t0:t0 + 512],
                                   wout[:].rearrange("p (c t) -> p c t", t=512),
                                   stream="wtt",
                                   after=[evjc[(tb, m2, "we")] for m2 in range(NCT)])
    phA = [mixst[(NTB - 1, j)] for j in range(5)] + [wst[NTB - 1]]

    cv.reset(base)
    msl = cv.take(16 * 512, BF16)
    oslab = cv.take(16 * 512, BF16)
    wbufp = cv.take(3 * 2048, BF16)
    projs = [("rT", "wr_slab", 3, None), ("kT", "wk_slab", 1, None),
             ("vT", "wv_slab", 2, None), ("gsT", "wg_slab", 4, AF.Silu)]
    out_ev = list(phA)
    mld = {}
    pmm = {}
    wbuf_free = {}
    pev = {}
    ost = {}
    for (oname, wname, j, act) in projs:
        for tb in range(NTB):
            t0 = 512 * tb
            mld[(oname, tb)] = core.dma("sync",
                                        msl[:].rearrange("p (c t) -> p c t", t=512),
                                        dram["mix"][j].rearrange("p (c t) -> p c t", t=T)[:, :, t0:t0 + 512],
                                        stream="msl",
                                        after=phA + [pmm.get((oname, tb - 1, mt2)) for mt2 in (14, 15)])
            for mt in range(NCT):
                b = mt % 3
                sl = slice(b * 2048, (b + 1) * 2048)
                wev = core.dma("gpsimd", wbufp[:, sl], dram[wname][mt], stream=f"pw{b}",
                               after=[wbuf_free.get(b)] + (phA if (tb, mt) < (0, 3) else []))
                bank = psum[mt % 4]
                dep = [wev, mld[(oname, tb)], pev.get((oname, tb, mt - 4)),
                       pev.get((oname, tb - 1, 12 + mt)) if mt < 4 else None]
                ev = None
                for ct in range(NCT):
                    ev = core.run("tensor",
                                  lambda e, p=bank, w=wbufp[:, b * 2048 + 128 * ct:b * 2048 + 128 * (ct + 1)],
                                  r=msl[:, 512 * ct:512 * (ct + 1)], f=(ct == 0), s=(ct == NCT - 1):
                                  e.matmul(p[:, 0:512], w, r, start=f, stop=s),
                                  after=dep, sig=(ct == NCT - 1))
                    dep = []
                pmm[(oname, tb, mt)] = ev
                wbuf_free[b] = ev
                if act is None:
                    pe = core.run("vector",
                                  lambda e, p=bank, o=oslab[:, 512 * mt:512 * (mt + 1)]:
                                  e.tensor_copy(o, p[:, 0:512]),
                                  after=[ev, ost.get("prev")])
                else:
                    pe = core.run("scalar",
                                  lambda e, p=bank, o=oslab[:, 512 * mt:512 * (mt + 1)], a2=act:
                                  e.activation(o, p[:, 0:512], a2),
                                  after=[ev, ost.get("prev")])
                pev[(oname, tb, mt)] = pe
            ost["prev"] = ost[(oname, tb)] = core.dma("sync",
                                        dram[oname].rearrange("p (c t) -> p c t", t=T)[:, :, t0:t0 + 512],
                                        oslab[:].rearrange("p (c t) -> p c t", t=512),
                                        stream="ost",
                                        after=[pev[(oname, tb, m2)] for m2 in range(NCT)])
            out_ev.append(ost[(oname, tb)])
    return out_ev



def emit_wkv(core, tag, cst, psum, dram, cv, after=()):
    identb = cst["identb"]
    ufa = cv.take(16, F32)
    uld = core.dma("sync", ufa, dram["ufa"], stream="cst", after=list(after))
    rb = cv.take(2 * T, BF16)
    kb = cv.take(2 * T, BF16)
    vb = cv.take(2 * T, BF16)
    wb = cv.take(2 * T, BF16)
    ew = cv.take(T, F32)
    gc = cv.take(T, F32)
    lc = cv.take(T, F32)
    d1 = cv.take(T, F32)
    qf = cv.take(T, BF16)
    kf = cv.take(T, BF16)
    qt = cv.take(2 * T, BF16)
    kt = cv.take(2 * T, BF16)
    kh = cv.take(2 * T, BF16)
    rku = cv.take(2 * T, BF16)
    etot = cv.take(64, F32)
    _alf = cv.take(64, F32)
    alf = [_alf[:, 0:32], _alf[:, 32:64]]
    _s32 = cv.take(128, F32)
    s32 = [_s32[:, 0:64], _s32[:, 64:128]]
    _sbf = cv.take(128, BF16)
    sbf = [_sbf[:, 0:64], _sbf[:, 64:128]]
    _sm = cv.take(12 * 128, BF16)
    vA = [_sm[:, 0:128], _sm[:, 128:256]]
    vB = [_sm[:, 256:384], _sm[:, 384:512]]
    hA = [_sm[:, 512:640], _sm[:, 640:768]]
    hB = [_sm[:, 768:896], _sm[:, 896:1024]]
    abar = [_sm[:, 1024:1152], _sm[:, 1152:1280]]
    ytm = [_sm[:, 1280:1408], _sm[:, 1408:1536]]
    ybuf = cv.take(2 * T, BF16)

    ld = {}
    w1 = {}
    tile_mm_last = {}
    tile_sbf_last = {}
    tile_s32_last = {}
    st_ev = {}
    tile_done = {}
    prevq = [{}, {}]

    def emit_loads(ct):
        p = ct % 2
        tsl = slice(p * T, (p + 1) * T)
        csl = slice(2048 * ct, 2048 * (ct + 1))
        dep = [tile_done.get(ct - 2)] + (list(after) if ct < 2 else [])
        l_r = core.dma("sync", rb[:, tsl], dram["rT"][:, csl], stream=f"wr{p}", after=dep)
        l_k = core.dma("sync", kb[:, tsl], dram["kT"][:, csl], stream=f"wk{p}", after=dep)
        l_v = core.dma("gpsimd", vb[:, tsl], dram["vT"][:, csl], stream=f"wv{p}", after=dep)
        l_w = core.dma("gpsimd", wb[:, tsl], dram["wT"][:, csl], stream=f"ww{p}", after=dep)
        ld[ct] = (l_r, l_k, l_v, l_w)

    def emit_w1(ct):
        p = ct % 2
        tsl = slice(p * T, (p + 1) * T)
        pv = w1.get(ct - 1, {})
        pv2 = w1.get(ct - 2, {})
        g2 = tile_mm_last.get(ct - 2)
        l_r, l_k, l_v, l_w = ld[ct]
        d = {}
        d["ew"] = core.run("scalar", lambda e, i=wb[:, tsl]: e.activation(ew[:], i, AF.Exp),
                           after=[l_w, pv.get("gc"), pv.get("d1")])
        d["gc"] = core.run("vector", lambda e: e.tensor_tensor_scan(gc[:], ew[:], ew[:], 0.0,
                                                                   ALU.add, ALU.bypass),
                           after=[d["ew"], pv.get("lc1"), pv.get("lc2"), pv.get("etot")])
        d["lc1"] = core.run("vector", lambda e: e.tensor_copy(lc[:, 0:64], gc[:, 0:64]),
                            after=[d["gc"], pv.get("d1"), pv.get("kf"), pv.get("etot")])
        d["lc2"] = core.run("vector", lambda e: e.tensor_tensor(
            lc[:, 64:T].rearrange("p (c n) -> p c n", n=64),
            gc[:, 64:T].rearrange("p (c n) -> p c n", n=64),
            gc[:, 63:T - 64:64].rearrange("p (c n) -> p c n", n=1).broadcast_to([128, 31, 64]),
            ALU.subtract),
            after=[d["gc"], pv.get("d1")])
        d["d1"] = core.run("vector", lambda e: e.tensor_sub(d1[:], lc[:], ew[:]),
                           after=[d["lc1"], d["lc2"], pv.get("qf")])
        d["qf"] = core.run("scalar", lambda e: e.activation(qf[:], d1[:], AF.Exp, scale=-1.0),
                           after=[d["d1"], pv.get("qt")])
        d["kf"] = core.run("scalar", lambda e: e.activation(kf[:], lc[:], AF.Exp),
                           after=[d["lc1"], d["lc2"], pv.get("kt")])
        d["etot"] = core.run("scalar",
                             lambda e, o=etot[0:128, 32 * p:32 * p + 32]:
                             e.activation(o, lc[:, 63:T:64], AF.Exp, scale=-1.0),
                             after=[d["lc1"], d["lc2"], pv2.get("kh"),
                                    tile_s32_last.get(ct - 2), g2])
        d["qt"] = core.run("gpsimd", lambda e, o=qt[:, tsl], r=rb[:, tsl]:
                           e.tensor_mul(o, r, qf[:]),
                           after=[l_r, d["qf"], g2])
        d["kt"] = core.run("gpsimd", lambda e, o=kt[:, tsl], k=kb[:, tsl]:
                           e.tensor_mul(o, k, kf[:]),
                           after=[l_k, d["kf"], g2])
        d["kh"] = core.run("vector", lambda e, o=kh[:, tsl], k=kt[:, tsl],
                           et=etot[0:128, 32 * p:32 * p + 32]:
                           e.tensor_tensor(
                               o.rearrange("p (c n) -> p c n", n=64),
                               k.rearrange("p (c n) -> p c n", n=64),
                               et.rearrange("p (c n) -> p c n", n=1).broadcast_to([128, 32, 64]),
                               ALU.mult),
                           after=[d["kt"], d["etot"], g2])
        d["rk"] = core.run("gpsimd", lambda e, o=rku[:, tsl], r=rb[:, tsl], k=kb[:, tsl]:
                           e.tensor_mul(o, r, k),
                           after=[l_r, l_k, g2])
        d["rku"] = core.run("vector", lambda e, o=rku[:, tsl], uc=ufa[:, ct:ct + 1]:
                            e.tensor_scalar(o, o, uc, None, ALU.mult),
                            after=[d["rk"], uld])
        w1[ct] = d

    def emit_w2_gen(ct):
        p = ct % 2
        tsl = slice(p * T, (p + 1) * T)
        d = w1[ct]
        l_r, l_k, l_v, l_w = ld[ct]
        qtv, ktv, khv, rkuv = qt[:, tsl], kt[:, tsl], kh[:, tsl], rku[:, tsl]
        eto = etot[0:128, 32 * p:32 * p + 32]
        prev = prevq[p]
        vAq, vBq, hAq, hBq = vA[p], vB[p], hA[p], hB[p]
        abq, ytq, alq, s3q, sbq = abar[p], ytm[p], alf[p], s32[p], sbf[p]
        bankW = psum[4 * p + 0]
        bankX = psum[4 * p + 1]
        bankY = psum[4 * p + 2]
        bankZ = psum[4 * p + 3]
        pVA = bankW.bitcast(BF16)[:, 0:128]
        pHA = bankW.bitcast(BF16)[:, 128:256]
        pVB = bankX.bitcast(BF16)[:, 0:128]
        pHB = bankX.bitcast(BF16)[:, 128:256]
        pYs = [bankW[:, 128:192], bankX[:, 128:192]]
        pA = bankY[:, 0:128]
        pAL = bankY[:, 128:160]
        pYT = bankY.bitcast(BF16)[:, 384:512]
        pS = bankZ[:, 0:64]
        s32_ev = core.run("vector", lambda e: e.memset(s3q[:], 0.0),
                          after=[tile_sbf_last.get(ct - 2), tile_mm_last.get(ct - 2)] +
                          (list(after) if ct == 0 else []))
        sbf_ev = core.run("scalar", lambda e: e.copy(sbq[:], s3q[:]), after=[s32_ev])
        aldep = [d["rku"], prev.get("bY")]
        alm = None
        for cc in range(32):
            for h in range(2):
                alm = core.run("tensor",
                               lambda e, l=rkuv[64 * h:64 * h + 64, 64 * cc:64 * cc + 64],
                               o=pAL[64 * h:64 * h + 64, cc:cc + 1], tp=(64 * h, 64 * h),
                               oc=cst["ones1b"][64 * h:64 * h + 64, 0:1]:
                               e.matmul(o, l, oc, start=True, stop=True, tile_position=tp),
                               after=aldep, sig=(cc == 31 and h == 1))
                aldep = []
        alev = core.run("scalar", lambda e: e.copy(alq[:], pAL),
                        after=[alm, prev.get("alrd")])
        ymm_last = None
        for cp in range(16):
            c0, c1 = 2 * cp, 2 * cp + 1
            w128 = slice(128 * cp, 128 * (cp + 1))
            bW = [prev.get("bW")] if cp else [alev]
            bX = [prev.get("bX")] if cp else [alev]
            bY = [prev.get("bY2"), alev] if cp else [alev]
            tvA = core.run("tensor", lambda e, i=vb[:, tsl][:, w128]:
                           e.transpose(pVA, i, identb), after=[l_v] + bW)
            tvB0 = core.run("tensor", lambda e, i=vb[:, tsl][:, 128 * cp + 64:128 * cp + 128]:
                            e.transpose(pVB[0:64, :], i, identb), after=[l_v] + bX)
            tvB1 = core.run("tensor", lambda e, i=vb[:, tsl][:, 128 * cp:128 * cp + 64]:
                            e.transpose(pVB[64:128, :], i, identb, tile_position=(0, 64)),
                            after=[l_v])
            thA = core.run("tensor", lambda e, i=khv[:, w128]:
                           e.transpose(pHA, i, identb), after=[d["kh"]])
            thB0 = core.run("tensor", lambda e, i=khv[:, 128 * cp + 64:128 * cp + 128]:
                            e.transpose(pHB[0:64, :], i, identb), after=[d["kh"]])
            thB1 = core.run("tensor", lambda e, i=khv[:, 128 * cp:128 * cp + 64]:
                            e.transpose(pHB[64:128, :], i, identb, tile_position=(0, 64)),
                            after=[d["kh"]])
            vAe = core.run("scalar", lambda e: e.copy(vAq[:], pVA),
                           after=[tvA, thA, prev.get("yselv"), prev.get("ysels")])
            vBe = core.run("scalar", lambda e: e.copy(vBq[:], pVB),
                           after=[tvB0, tvB1, thB0, thB1, prev.get("yselv"), prev.get("ysels")])
            hAe = core.run("scalar", lambda e: e.copy(hAq[:], pHA),
                           after=[thA, tvA, prev.get("su1")])
            hBe = core.run("scalar", lambda e: e.copy(hBq[:], pHB),
                           after=[thB0, thB1, tvB0, tvB1, prev.get("su2")])
            amms = []
            for h in range(2):
                for c, cc in ((0, c0), (1, c1)):
                    amms.append(core.run("tensor",
                                         lambda e, l=ktv[64 * h:64 * h + 64, 64 * cc:64 * cc + 64],
                                         r=qtv[64 * h:64 * h + 64, 64 * cc:64 * cc + 64],
                                         o=pA[64 * h:64 * h + 64, 64 * c:64 * c + 64],
                                         tp=(64 * h, 64 * h):
                                         e.matmul(o, l, r, start=True, stop=True, tile_position=tp),
                                         after=[d["qt"], d["kt"], prev.get("mask")] + bY,
                                         sig=(h == 1 and c == 1)))
                    bY = []
            msk = core.run("vector", lambda e: e.tensor_mul(abq[:], pA, cst["maskU"]),
                           after=amms + [prev.get("abrd")])
            prev["mask"] = msk
            selmap = {(0, 0): vAq, (0, 1): vBq, (1, 0): vBq, (1, 1): vAq}
            helmap = {0: hAq, 1: hBq}
            ev_y = {}
            i1last = None
            for c, cc in ((0, c0), (1, c1)):
                pY = pYs[c]
                i2 = None
                for h in range(2):
                    i1 = core.run("tensor",
                                  lambda e, l=abq[64 * h:64 * h + 64, 64 * c:64 * c + 64],
                                  r=selmap[(h, c)][64 * h:64 * h + 64, 64 * h:64 * h + 64],
                                  o=pY[64 * h:64 * h + 64, :], tp=(64 * h, 64 * h):
                                  e.matmul(o, l, r, start=True, stop=False, tile_position=tp),
                                  after=[msk, vAe, vBe, hAe, hBe, prev.get(("yev", c))])
                    i1last = i1
                    i2 = core.run("tensor",
                                  lambda e, l=qtv[64 * h:64 * h + 64, 64 * cc:64 * cc + 64],
                                  r=sbq[64 * h:64 * h + 64, 0:64],
                                  o=pY[64 * h:64 * h + 64, :], tp=(64 * h, 64 * h):
                                  e.matmul(o, l, r, start=False, stop=True, tile_position=tp),
                                  after=[sbf_ev], sig=(h == 1))
                ymm_last = i2
                yevs = []
                for h in range(2):
                    yevs.append(core.run("vector",
                                         lambda e, v2=selmap[(h, c)][64 * h:64 * h + 64, 64 * h:64 * h + 64],
                                         a=alq[64 * h:64 * h + 64, cc:cc + 1],
                                         p2=pY[64 * h:64 * h + 64, :],
                                         o=ytq[64 * h:64 * h + 64, 64 * c:64 * c + 64]:
                                         e.scalar_tensor_tensor(o, v2, a, p2, ALU.mult, ALU.add),
                                         after=[i2, alev, prev.get(("ytev", c, h))]))
                prev[("yev", c)] = yevs[-1]
                ev_y[c] = yevs
                su = []
                for h in range(2):
                    su.append(core.run("tensor",
                                       lambda e, l=helmap[c][0:64, 64 * h:64 * h + 64],
                                       r=selmap[(0, c)][0:64, 64 * h:64 * h + 64],
                                       o=pS[64 * h:64 * h + 64, :], tp=(0, 64 * h):
                                       e.matmul(o, l, r, start=True, stop=True, tile_position=tp),
                                       after=[hAe, hBe, vAe, vBe, prev.get("sue")]))
                s32_ev = core.run("vector",
                                  lambda e, sc=eto[0:128, c0 + c:c0 + c + 1]:
                                  e.scalar_tensor_tensor(s3q[:], s3q[:], sc, pS,
                                                         ALU.mult, ALU.add),
                                  after=su + [s32_ev, sbf_ev])
                sbf_ev = core.run("scalar", lambda e: e.copy(sbq[:], s3q[:]),
                                  after=[s32_ev])
                prev["sue"] = sbf_ev
                if c == 0:
                    prev["su1"] = su[-1]
                else:
                    prev["su2"] = su[-1]
            prev["abrd"] = i1last
            prev["yselv"] = ev_y[1][-1]
            prev["ysels"] = prev["su2"]
            prev["alrd"] = ev_y[1][-1]
            ytts = []
            for c in range(2):
                for h in range(2):
                    ytts.append(core.run("tensor",
                                         lambda e, i=ytq[64 * h:64 * h + 64, 64 * c:64 * c + 64],
                                         o=pYT[64 * h:64 * h + 64, 64 * c:64 * c + 64],
                                         idn=identb[64 * h:64 * h + 64, 64 * h:64 * h + 64],
                                         tp=(64 * h, 64 * h):
                                         e.transpose(o, i, idn, tile_position=tp),
                                         after=[ev_y[c][h], prev.get("ytrd")]))
                    prev[("ytev", c, h)] = ytts[-1]
            yte = core.run("vector", lambda e, o=ybuf[:, tsl][:, w128]:
                           e.tensor_copy(o, pYT),
                           after=ytts + [st_ev.get(ct - 2)])
            prev["ytrd"] = yte
            prev["bW"] = ev_y[0][-1]
            prev["bX"] = ev_y[1][-1]
            prev["bY"] = yte
            prev["bY2"] = yte
            yield
        tile_mm_last[ct] = ymm_last
        tile_sbf_last[ct] = sbf_ev
        tile_s32_last[ct] = s32_ev
        st_ev[ct] = core.dma("sync", dram["yT"][:, 2048 * ct:2048 * (ct + 1)], ybuf[:, tsl],
                             stream=f"yst{p}", after=[yte])
        tile_done[ct] = st_ev[ct]
        yield

    pend = None
    for ct in range(NCT):
        emit_loads(ct)
        emit_w1(ct)
        g = emit_w2_gen(ct)
        if pend is None:
            for _ in range(8):
                next(g)
        else:
            for _ in range(8):
                next(pend)
                next(g)
            next(pend)
            for _ in pend:
                pass
        pend = g
    for _ in pend:
        pass
    return [st_ev[c9] for c9 in range(NCT)]



def emit_l0_post(core, tag, cst, psum, dram, cv, after=(), ln_guard=(), resid_q="gpsimd"):
    base = cv.off
    EPS = 1e-5 * 64.0
    gnw = cv.take(32, F32)
    gl = core.dma("sync", gnw, dram["gnw"], stream="cst", after=list(after))
    yblk = cv.take(16 * 512, BF16)
    gblk = cv.take(16 * 512, BF16)
    sq = cv.take(512, BF16)
    zsl = cv.take(16 * 512, BF16)
    rows = cv.take(8 * 512, F32)
    rowb = cv.take(2 * 512, BF16)
    wbuf = cv.take(3 * 2048, BF16)
    osb = cv.take(2 * 512, F32)
    t1b = cv.take(512, F32)

    res_ev = []
    ldprev = {}
    zev = {}
    mmo = {}
    sqev = {}
    stev = {}
    bcev = {}
    t1ev = {}
    oev = {}
    for tb in range(NTB):
        ysrc = dram["yT"].rearrange("p (c t) -> p c t", t=T)[:, :, 512 * tb:512 * (tb + 1)]
        gsrc = dram["gsT"].rearrange("p (c t) -> p c t", t=T)[:, :, 512 * tb:512 * (tb + 1)]
        ly = core.dma("sync", yblk.rearrange("p (c t) -> p c t", t=512), ysrc, stream="ypl",
                      after=[zev.get((tb - 1, 15))] + (list(after) if tb == 0 else []))
        lg = core.dma("gpsimd", gblk.rearrange("p (c t) -> p c t", t=512), gsrc, stream="gpl",
                      after=[zev.get((tb - 1, 15))] + (list(after) if tb == 0 else []))
        for ct in range(NCT):
            yv = yblk[:, 512 * ct:512 * (ct + 1)]
            e_sq = core.run("scalar", lambda e, i=yv: e.activation(sq[:], i, AF.Square),
                            after=[ly, sqev.get((tb, ct - 1))])
            m_ev = []
            for q, (rhs_, oc) in enumerate(((yv, "onesh0"), (yv, "onesh1"),
                                            (sq[:], "onesh0"), (sq[:], "onesh1"))):
                dep = [ly if q < 2 else e_sq,
                       stev.get((tb, ct - 1, q)) if ct else stev.get((tb - 1, 15, q))]
                m_ev.append(core.run("tensor",
                                     lambda e, p=psum[q], r=rhs_, o=cst[oc]:
                                     e.matmul(p[0:1, 0:512], o, r, start=True, stop=True),
                                     after=dep))
            sqev[(tb, ct)] = e_sq
            ev_r = []
            for hh in range(2):
                mu = rows[0:1, 512 * hh:512 * (hh + 1)]
                qq = rows[0:1, 1024 + 512 * hh:1536 + 512 * hh]
                A = rows[0:1, 2048 + 512 * hh:2560 + 512 * hh]
                B = rows[0:1, 3072 + 512 * hh:3584 + 512 * hh]
                r1 = core.run("vector", lambda e, o=mu, p=psum[hh]:
                              e.tensor_scalar(o, p[0:1, 0:512], 1.0 / 64, None, ALU.mult),
                              after=[m_ev[hh], bcev.get((tb, ct - 1))])
                stev[(tb, ct, hh)] = r1
                r2 = core.run("vector", lambda e, o=qq, p=psum[2 + hh]:
                              e.tensor_scalar(o, p[0:1, 0:512], 1.0 / 64, None, ALU.mult),
                              after=[m_ev[2 + hh], bcev.get((tb, ct - 1))])
                stev[(tb, ct, 2 + hh)] = r2
                r3 = core.run("scalar", lambda e, o=A, i=mu: e.activation(o, i, AF.Square),
                              after=[r1, bcev.get((tb, ct - 1))])
                r4 = core.run("vector", lambda e, o=A, i=qq: e.tensor_sub(o, i, o), after=[r3, r2])
                r5 = core.run("vector", lambda e, o=A: e.tensor_scalar(o, o, EPS, None, ALU.add),
                              after=[r4])
                r6 = core.run("scalar", lambda e, o=A: e.activation(o, o, AF.Sqrt), after=[r5])
                r7 = core.run("vector", lambda e, o=A: e.reciprocal(o, o), after=[r6])
                r8 = core.run("vector", lambda e, o=B, i=mu, a=A: e.tensor_mul(o, i, a), after=[r7])
                ev_r.append(r8)
            c1 = core.run("vector", lambda e: e.tensor_copy(rowb[0:1, 0:512], rows[0:1, 2048:2560]),
                          after=ev_r + [bcev.get((tb, ct - 1))])
            c2 = core.run("vector", lambda e: e.tensor_copy(rowb[0:1, 512:1024], rows[0:1, 2560:3072]),
                          after=ev_r)
            c3 = core.run("vector", lambda e: e.tensor_copy(rowb[0:1, 0:512][0:1, 0:512], rowb[0:1, 0:512]),
                          after=[]) if False else None
            c4 = core.run("vector", lambda e: e.tensor_copy(t1b[0:1, 0:512], rows[0:1, 3072:3584]),
                          after=ev_r) if False else None
            bA, bB = [], []
            for hh in range(2):
                bA.append(core.run("tensor",
                                   lambda e, r=rowb[0:1, 512 * hh:512 * (hh + 1)], tp=(0, 64 * hh),
                                   o=psum[4][64 * hh:64 * hh + 64, 0:512],
                                   l=cst["ones1r"][0:1, 0:64]:
                                   e.matmul(o, l, r, start=True, stop=True, tile_position=tp),
                                   after=[c1, c2, t1ev.get((tb, ct - 1)) if ct else t1ev.get((tb - 1, 15))]))
            cB1 = core.run("vector", lambda e: e.tensor_copy(rowb[0:1, 0:512], rows[0:1, 3072:3584]),
                           after=[bA[0], bA[1]])
            cB2 = core.run("vector", lambda e: e.tensor_copy(rowb[0:1, 512:1024], rows[0:1, 3584:4096]),
                           after=[bA[0], bA[1]])
            for hh in range(2):
                bB.append(core.run("tensor",
                                   lambda e, r=rowb[0:1, 512 * hh:512 * (hh + 1)], tp=(0, 64 * hh),
                                   o=psum[5][64 * hh:64 * hh + 64, 0:512],
                                   l=cst["ones1r"][0:1, 0:64]:
                                   e.matmul(o, l, r, start=True, stop=True, tile_position=tp),
                                   after=[cB1, cB2, (t1ev.get((tb, ct - 1, "b")) if ct else t1ev.get((tb - 1, 15, "b")))]))
            bcev[(tb, ct)] = bB[1]
            e1 = core.run("vector", lambda e, i=yv: e.tensor_mul(t1b[:], i, psum[4][:, 0:512]),
                          after=bA + [t1ev.get((tb, ct - 1))])
            t1ev[(tb, ct)] = e1
            e2 = core.run("vector", lambda e: e.tensor_sub(t1b[:], t1b[:], psum[5][:, 0:512]),
                          after=[e1] + bB)
            t1ev[(tb, ct, "b")] = e2
            e3 = core.run("vector",
                          lambda e, g2=gnw[:, ct:ct + 1], b2=gnw[:, 16 + ct:17 + ct]:
                          e.tensor_scalar(t1b[:], t1b[:], g2, b2, ALU.mult, ALU.add),
                          after=[e2, gl])
            e4 = core.run("vector",
                          lambda e, o=zsl[:, 512 * ct:512 * (ct + 1)], g3=gblk[:, 512 * ct:512 * (ct + 1)]:
                          e.tensor_mul(o, t1b[:], g3),
                          after=[e3, lg, mmo.get((tb - 1, 15)) if ct == 0 else None])
            zev[(tb, ct)] = e4
        for mt in range(NCT):
            b = mt % 3
            sl = slice(b * 2048, (b + 1) * 2048)
            wev = core.dma("gpsimd", wbuf[:, sl], dram["wo_slab"][mt], stream=f"wo{b}",
                           after=[mmo.get((tb, mt - 3)), mmo.get((tb - 1, 13 + mt)) if mt < 3 else None] +
                           (list(after) if (tb, mt) < (0, 3) else []))
            bank = psum[6 + mt % 2]
            dep = [wev, oev.get((tb, mt - 2)), oev.get((tb - 1, 14 + mt)) if mt < 2 else None] + \
                  [zev[(tb, c2b)] for c2b in range(NCT)]
            ev = None
            for ctk in range(NCT):
                ev = core.run("tensor",
                              lambda e, p=bank, l=wbuf[:, b * 2048 + 128 * ctk:b * 2048 + 128 * (ctk + 1)],
                              r=zsl[:, 512 * ctk:512 * (ctk + 1)], f=(ctk == 0), s=(ctk == NCT - 1):
                              e.matmul(p[:, 0:512], l, r, start=f, stop=s),
                              after=dep, sig=(ctk == NCT - 1))
                dep = []
            mmo[(tb, mt)] = ev
            oe = core.run("vector", lambda e, p=bank, o=osb[:, 512 * (mt % 2):512 * (mt % 2 + 1)]:
                          e.tensor_copy(o, p[:, 0:512]),
                          after=[ev, res_ev[-2] if len(res_ev) >= 2 else None])
            oev[(tb, mt)] = oe
            res_ev.append(core.dma(resid_q,
                                   dram["x_resid"][128 * mt:128 * (mt + 1), 512 * tb:512 * (tb + 1)],
                                   osb[:, 512 * (mt % 2):512 * (mt % 2 + 1)],
                                   stream=f"ores{mt % 2}",
                                   after=[oe] + list(ln_guard), accum_op=ALU.add))
    return res_ev



def host_prep(inputs):
    f = {}
    tms = [np.asarray(inputs[f"L0_attn_time_maa_{n}"]).reshape(-1)
           for n in ["x", "w", "k", "v", "r", "g"]]
    f["tm6"] = np.ascontiguousarray(
        np.stack(tms, 1).reshape(16, 128, 6).transpose(1, 0, 2).reshape(128, 96)).astype(np.float32)
    f["td"] = np.ascontiguousarray(
        np.asarray(inputs["L0_attn_time_decay"]).reshape(-1).reshape(16, 128).T).astype(np.float32)
    f["w1m"] = _bf(np.asarray(inputs["L0_attn_time_maa_w1"])
                   .reshape(16, 128, 160).transpose(1, 0, 2).reshape(128, 2560))
    f["w2m"] = _bf(np.asarray(inputs["L0_attn_time_maa_w2"])
                   .reshape(5, 32, 16, 128).transpose(1, 0, 2, 3).reshape(32, 10240))
    f["dw1"] = _bf(np.asarray(inputs["L0_attn_time_decay_w1"])
                   .reshape(16, 128, 64).transpose(1, 0, 2).reshape(128, 1024))
    f["dw2"] = _bf(np.asarray(inputs["L0_attn_time_decay_w2"])
                   .reshape(64, 16, 128).transpose(1, 0, 2).reshape(64, 2048))
    for nm in ["r", "k", "v", "g", "o"]:
        f[f"w{nm}_slab"] = pack_lhsT(np.asarray(inputs[f"L0_attn_W_{nm}"]), BF)
    f["ufa"] = np.ascontiguousarray(
        np.asarray(inputs["L0_attn_time_faaaa"]).reshape(-1).reshape(16, 128).T).astype(np.float32)
    f["gnw"] = np.ascontiguousarray(np.concatenate(
        [np.asarray(inputs["L0_attn_ln_gamma"]).reshape(16, 128).T,
         np.asarray(inputs["L0_attn_ln_beta"]).reshape(16, 128).T], 1)).astype(np.float32)
    f["gu_bytes"], f["gu_sc"] = pack_bytes_fp4(np.asarray(inputs["L0_ffn_gate_up_packed"]),
                                               np.asarray(inputs["L0_ffn_gate_up_scales"]))
    f["dn_bytes"], f["dn_sc"] = pack_bytes_fp4(np.asarray(inputs["L0_ffn_down_packed"]),
                                               np.asarray(inputs["L0_ffn_down_scales"]))
    wqkv = np.asarray(inputs["L1_attn_w_qkv"])
    bqkv = np.asarray(inputs["L1_attn_b_qkv"])
    f["wqkv_slab"] = pack_lhsT(wqkv[0:4096], BF)
    f["wv_rhs"] = _bf(np.ascontiguousarray(
        wqkv[4096:].T.reshape(16, 128, 4, 512).transpose(2, 1, 0, 3)).reshape(4, 128, 8192))
    f["wd_slab"] = pack_lhsT(np.asarray(inputs["L1_attn_w_dense"]), BF)
    f["bqk"] = np.ascontiguousarray(bqkv[0:4096].reshape(32, 128).T)
    f["bv"] = np.ascontiguousarray(bqkv[4096:].reshape(16, 128).T)
    f["bdense"] = np.ascontiguousarray(np.asarray(inputs["L1_attn_b_dense"]).reshape(16, 128).T)
    f["gate_slab"] = _bf(np.asarray(inputs["L1_ffn_gate_w"]).T.reshape(16, 128, 16))
    up_w = np.asarray(inputs["L1_ffn_expert_up_w"])
    dn_w = np.asarray(inputs["L1_ffn_expert_down_w"])
    f["up_slab"] = np.stack([pack_lhsT(up_w[e], BF) for e in range(16)])
    f["dn_slab"] = np.stack([pack_lhsT(dn_w[e], BF) for e in range(16)])
    f["sup_slab"] = pack_lhsT(np.asarray(inputs["L1_ffn_shared_up_w"]), BF)
    f["sdn_slab"] = pack_lhsT(np.asarray(inputs["L1_ffn_shared_down_w"]), BF)

    c = host_constants()
    S = T
    inv = 1.0 / (10000.0 ** (np.arange(0, 128, 2) / 128.0))
    fr = np.arange(S)[:, None] * inv[None, :]
    emb = np.concatenate([fr, fr], 1)
    c["cosb"] = _bf(np.cos(emb).T)
    c["sinb"] = _bf(np.sin(emb).T)
    R = np.zeros((128, 128), np.float32)
    for dr in range(64):
        R[dr, dr + 64] = -1.0
        R[dr + 64, dr] = 1.0
    c["rt"] = _bf(R.T)
    c["neg20"] = np.full((128, 1), -20.0, np.float32)
    c["neg8"] = np.full((128, 1), -8.0, np.float32)
    for j in range(4):
        kl = np.arange(128)[:, None]
        ql = np.arange(512)[None, :]
        c[f"mask{j}"] = _bf((128 * j + kl <= ql).astype(np.float32))
    rr = np.arange(128)[:, None] % 64
    cc2 = np.arange(128)[None, :] % 64
    c["maskU"] = _bf((rr < cc2).astype(np.float32))
    o0 = np.zeros((128, 1), np.float32); o0[0:64] = 1
    o1 = np.zeros((128, 1), np.float32); o1[64:128] = 1
    c["onesh0"] = _bf(o0)
    c["onesh1"] = _bf(o1)
    return f, c


_DTYPES = {"gu_bytes": U8, "dn_bytes": U8, "gu_sc": F16, "dn_sc": F16}
for _n in ["w1m", "w2m", "dw1", "dw2", "wr_slab", "wk_slab", "wv_slab", "wg_slab",
           "wo_slab", "wqkv_slab", "wv_rhs", "wd_slab", "gate_slab", "up_slab",
           "dn_slab", "sup_slab", "sdn_slab"]:
    _DTYPES[_n] = BF16


def _declare(core, f, names):
    d = {}
    for n in names:
        arr = f[n]
        dt = _DTYPES.get(n, F32)
        d[n] = core.dram_in(n, arr.shape, dt)
    return d


CSTDT = {"shift01": U16, "six16": U16, "neg20": F32, "neg8": F32, "identf": F32,
         "ones1b": BF16}


def _run(nc, feeds_list, n):
    return run_bass_kernel_spmd(nc, feeds_list, core_ids=list(range(n)))


def _mk_core():
    nc = bass.Bass("TRN2", target_bir_lowering=False)
    core = Core(nc)
    psum = [core.psb(f"p{i}") for i in range(8)]
    return nc, core, psum


ALL_CST = ["ones1b", "ones1r", "identb", "identf", "shift01", "neg20", "neg8",
           "cosb", "sinb", "rt", "mask0", "mask1", "mask2", "mask3", "maskU",
           "onesh0", "onesh1"]
ALL_W = ["tm6", "td", "w1m", "w2m", "dw1", "dw2", "wr_slab", "wk_slab", "wv_slab",
         "wg_slab", "wo_slab", "ufa", "gnw", "gu_bytes", "gu_sc", "dn_bytes",
         "dn_sc", "wqkv_slab", "wv_rhs", "wd_slab", "bqk", "bv", "bdense",
         "gate_slab", "up_slab", "dn_slab", "sup_slab", "sdn_slab"]


def build_fused():
    nc = bass.Bass("TRN2", target_bir_lowering=False)
    core = Core(nc)
    psum = [core.psb(f"p{i}") for i in range(8)]
    xd = core.dram_in("x", (C, T), F32)
    out = core.dram_out("out", (C, T), F32)
    dram = {}
    shapes = {"tm6": (128, 96), "td": (128, 16), "w1m": (128, 2560), "w2m": (32, 10240),
              "dw1": (128, 1024), "dw2": (64, 2048), "ufa": (128, 16), "gnw": (128, 32),
              "gu_bytes": (88, 128, 2048), "gu_sc": (88, 128, 2048),
              "dn_bytes": (16, 128, 5632), "dn_sc": (16, 128, 5632),
              "wqkv_slab": (32, 128, 2048), "wv_rhs": (4, 128, 8192),
              "wd_slab": (16, 128, 2048), "bqk": (128, 32), "bv": (128, 16),
              "bdense": (128, 16), "gate_slab": (16, 128, 16),
              "up_slab": (16, 22, 128, 2048), "dn_slab": (16, 16, 128, 1408),
              "sup_slab": (44, 128, 2048), "sdn_slab": (16, 128, 2816)}
    for nm in ["wr_slab", "wk_slab", "wv_slab", "wg_slab", "wo_slab"]:
        shapes[nm] = (16, 128, 2048)
    for nm in ALL_W:
        dram[nm] = core.dram_in(nm, shapes[nm], _DTYPES.get(nm, F32))
    dram["mix"] = core.dram_tmp("mix", (5, 128, NCT * T), BF16)
    for nm in ["rT", "kT", "vT", "gsT", "wT", "yT"]:
        dram[nm] = core.dram_tmp(nm, (128, NCT * T), BF16)
    dram["h_scr"] = core.dram_tmp("h_scr", (128, 44 * T), BF16)
    dram["qr"] = core.dram_tmp("qr", (16, 128, T), BF16)
    dram["kr"] = core.dram_tmp("kr", (16, 128, T), BF16)
    dram["vtm"] = core.dram_tmp("vtm", (128, 16 * T), BF16)
    dram["x_resid"] = out

    cst, cev = load_constants(core, {**host_constants(), **_fused_cst_stub()}, ALL_CST)
    arena = core.sb("arena", (128, NCT * T), BF16)
    cv = Carver(core, nbytes=120 * 1024)

    iev = core.dma("gpsimd", out[:, :], xd[:, :], stream="init")
    cv.reset(0)
    ln0 = emit_ln(core, "0", xd, arena[:], cst, psum, cv, after=cev)
    cv.reset(0)
    mx = emit_l0_mixes(core, "0", arena[:], cst, psum, dram, cv,
                       after=ln0 + core.stream_tails())
    cv.reset(0)
    wk = emit_wkv(core, "0", cst, psum, dram, cv, after=mx + core.stream_tails())
    cv.reset(0)
    po = emit_l0_post(core, "0", cst, psum, dram, cv,
                      after=wk + core.stream_tails(), ln_guard=[iev])
    cv.reset(0)
    ff = emit_ffn_fp4(core, "0", arena[:], cst, psum, dram, cv,
                      after=po + core.stream_tails(), resid_q="gpsimd")
    cv.reset(0)
    ln1 = emit_ln(core, "1", out, arena[:], cst, psum, cv,
                  after=ff + po + [iev] + core.stream_tails())
    cv.reset(0)
    at = emit_l1_attn(core, "1", arena[:], cst, psum, dram, cv,
                      after=ln1 + core.stream_tails(), ln_guard=ln1)
    cv.reset(0)
    mo = emit_moe(core, "1", arena[:], cst, psum, dram, cv,
                  after=at + core.stream_tails(), ln_guard=ln1)
    core.wait("sync", mo + at + ff + po + [iev])
    core.build_block()
    core.finish()
    return nc


_CST_CACHE = {}


def _fused_cst_stub():
    c = {}
    S = T
    inv = 1.0 / (10000.0 ** (np.arange(0, 128, 2) / 128.0))
    fr = np.arange(S)[:, None] * inv[None, :]
    emb = np.concatenate([fr, fr], 1)
    c["cosb"] = _bf(np.cos(emb).T)
    c["sinb"] = _bf(np.sin(emb).T)
    R = np.zeros((128, 128), np.float32)
    for dr in range(64):
        R[dr, dr + 64] = -1.0
        R[dr + 64, dr] = 1.0
    c["rt"] = _bf(R.T)
    c["neg20"] = np.full((128, 1), -20.0, np.float32)
    c["neg8"] = np.full((128, 1), -8.0, np.float32)
    for j in range(4):
        kl = np.arange(128)[:, None]
        ql = np.arange(512)[None, :]
        c[f"mask{j}"] = _bf((128 * j + kl <= ql).astype(np.float32))
    rr = np.arange(128)[:, None] % 64
    cc2 = np.arange(128)[None, :] % 64
    c["maskU"] = _bf((rr < cc2).astype(np.float32))
    o0 = np.zeros((128, 1), np.float32); o0[0:64] = 1
    o1 = np.zeros((128, 1), np.float32); o1[64:128] = 1
    c["onesh0"] = _bf(o0)
    c["onesh1"] = _bf(o1)
    return c


def kernel(**inputs):
    f, cst_host = host_prep(inputs)
    x = np.asarray(inputs["L0_attn_x"])
    ncores = x.shape[0]
    nc = build_fused()
    feeds = []
    for b in range(ncores):
        m = {n: f[n] for n in ALL_W}
        for n in ALL_CST:
            m[f"cst_{n}"] = cst_host[n]
        m["x"] = np.ascontiguousarray(x[b].T).astype(np.float32)
        feeds.append(m)
    res = run_bass_kernel_spmd(nc, feeds, core_ids=list(range(ncores)))
    return np.stack([np.ascontiguousarray(res.results[b]["out"].T)
                     for b in range(ncores)]).astype(np.float32)
